# version 07
# baseline (speedup 1.0000x reference)
import contextlib
import numpy as np
import ml_dtypes
import concourse.bass as bass
import concourse.mybir as mybir
from concourse.bass_utils import run_bass_kernel_spmd

F32 = mybir.dt.float32
BF16 = mybir.dt.bfloat16
AF = mybir.ActivationFunctionType
ALU = mybir.AluOpType
AX = mybir.AxisListType

T = 4096
D = 1024
DIH = 1024
NCH = 8
NS = 16
TW = 512
NTW = T // TW
NTT = T // 128
NB = 4
EPS = 1e-5
GROUPS = [[0, 1], [2, 3], [4, 5], [6, 7]]

DEBUG_TAPS = ()
STOP_AFTER = None


class Sem:

    def __init__(self, es, nc, name):
        self.h = es.enter_context(nc.semaphore(name))
        self.v = 0

    def inc(self, inst, k=1):
        inst.then_inc(self.h, k)
        self.v += k
        return self.v


class RotSem:

    def __init__(self, es, nc, name, n=6):
        self.sems = [es.enter_context(nc.semaphore(f"{name}{i}")) for i in range(n)]
        self.vals = [0] * n
        self.i = 0
        self.n = n

    def rotate(self):
        self.i = (self.i + 1) % self.n

    def issue(self, inst, k=16):
        j = self.i
        inst.then_inc(self.sems[j], k)
        self.vals[j] += k
        return (self.sems[j], self.vals[j])

    def barrier(self):
        return [(self.sems[j], self.vals[j]) for j in range(self.n)]


def W(eng, *toks):
    for t in toks:
        if t is None:
            continue
        if isinstance(t, list):
            for tt in t:
                eng.wait_ge(tt[0], tt[1])
        else:
            eng.wait_ge(t[0], t[1])


def build_program():
    all_phases = ["p0", "mpre", "mscan", "ln2", "ffna", "ffnb", "final"]
    phases = all_phases if STOP_AFTER is None else all_phases[: all_phases.index(STOP_AFTER) + 1]

    nc = bass.Bass("TRN2", target_bir_lowering=False)

    def inp(name, shape, dtype=F32):
        return nc.dram_tensor(name, list(shape), dtype, kind="ExternalInput").ap()

    x_tok = inp("x_tok", [T, D])
    x_fm = inp("x_fm", [D, T])
    ipT = inp("ipT", [D, 2 * DIH])
    cw = inp("cw", [128, NCH * 4])
    cb = inp("cb", [128, NCH])
    xpT = inp("xpT", [128, NCH * 33])
    dtw = inp("dtw", [1, DIH])
    dtb = inp("dtb", [128, NCH])
    logA = inp("logA", [128, NCH * NS])
    DpI = inp("Dp", [128, NCH])
    nwI = inp("nw", [128, NCH])
    nbI = inp("nb", [128, NCH])
    opT = inp("opT", [DIH, D])
    gwT = inp("gwT", [128, NCH * 8])
    w1I = inp("w1", [9 * D, 512])
    w3I = inp("w3", [9 * D, 512])
    w2I = inp("w2", [9 * 512, D])
    b1tI = inp("b1t", [128, 9 * 4])
    b3tI = inp("b3t", [128, 9 * 4])
    b2sI = inp("b2s", [9, D])
    identI = inp("ident_bf", [128, 128], BF16)
    sel32I = inp("sel32", [32, 32 * 128], BF16)
    sel8I = inp("sel8", [8, 8 * 128], BF16)
    onesbI = inp("ones_bf", [128, 1], BF16)
    ones1I = inp("ones_f1", [1, 128])
    ones1bI = inp("ones_r1b", [1, 128], BF16)
    epsI = inp("epsc", [128, 1])

    out_fm = nc.dram_tensor("out_fm", [D, T], F32, kind="ExternalOutput").ap()

    u_d = nc.dram_tensor("u_d", [DIH, T], F32).ap()
    sz_d = nc.dram_tensor("sz_d", [DIH, T], F32).ap()
    y_d = nc.dram_tensor("y_d", [DIH, T], F32).ap()
    xnt_d = nc.dram_tensor("xnt_d", [D, T], BF16).ap()
    hh_d = nc.dram_tensor("hh_d", [36 * 128, T], BF16).ap()
    cc_ssm_i = nc.dram_tensor("cc_ssm_i", [33, T], F32)
    cc_ssm_o = nc.dram_tensor("cc_ssm_o", [33, T], F32)
    cc_st_i = nc.dram_tensor("cc_st_i", [2, T], F32)
    cc_st_o = nc.dram_tensor("cc_st_o", [2, T], F32)
    cc_sg_i = nc.dram_tensor("cc_sg_i", [2 * D, T], F32)
    cc_sg_o = nc.dram_tensor("cc_sg_o", [2 * D, T], F32)

    taps = {}
    tapshape = {"xnt": ([D, T], BF16), "u": ([DIH, T], F32), "sz": ([DIH, T], F32),
                "ssm": ([33, T], F32), "y": ([DIH, T], F32), "stats": ([2, T], F32),
                "sg": ([2 * D, T], F32), "hh": ([36 * 128, T], BF16),
                "combt": ([9, T], F32), "carr": ([128, 128], F32), "delta": ([DIH, T], F32), "du": ([DIH, T], BF16), "gated": ([DIH, T], BF16), "rbt": ([128, 32], F32), "combb": ([128, 256], BF16)}
    for tn in DEBUG_TAPS:
        shp, dt = tapshape[tn]
        taps[tn] = nc.dram_tensor("tap_" + tn, shp, dt, kind="ExternalOutput").ap()

    es = contextlib.ExitStack()
    with es:
        E = es.enter_context

        Acol = E(nc.sbuf_tensor("Acol", [128, NCH * NS], F32))
        s_cw = E(nc.sbuf_tensor("s_cw", [128, NCH * 4], F32))
        s_cb = E(nc.sbuf_tensor("s_cb", [128, NCH], F32))
        s_dtb = E(nc.sbuf_tensor("s_dtb", [128, NCH], F32))
        s_Dp = E(nc.sbuf_tensor("s_Dp", [128, NCH], F32))
        s_nw = E(nc.sbuf_tensor("s_nw", [128, NCH], F32))
        s_nb = E(nc.sbuf_tensor("s_nb", [128, NCH], F32))
        s_logA = E(nc.sbuf_tensor("s_logA", [128, NCH * NS], F32))
        s_id = E(nc.sbuf_tensor("s_id", [128, 128], BF16))
        s_sel32 = E(nc.sbuf_tensor("s_sel32", [32, 32 * 128], BF16))
        s_sel8 = E(nc.sbuf_tensor("s_sel8", [8, 8 * 128], BF16))
        s_onesb = E(nc.sbuf_tensor("s_onesb", [128, 1], BF16))
        s_ones1 = E(nc.sbuf_tensor("s_ones1", [1, 128], F32))
        s_ones1b = E(nc.sbuf_tensor("s_ones1b", [1, 128], BF16))
        s_eps = E(nc.sbuf_tensor("s_eps", [128, 1], F32))
        s_dtwb = E(nc.sbuf_tensor("s_dtwb", [1, DIH], BF16))
        s_dtwf = E(nc.sbuf_tensor("s_dtwf", [1, DIH], F32))
        combT = E(nc.sbuf_tensor("combT", [8, T], BF16))
        combT05 = E(nc.sbuf_tensor("combT05", [9, T], BF16))

        pb = [E(nc.psum_tensor(f"pb{i}", [128, TW], F32)) for i in range(5)]
        pst1 = E(nc.psum_tensor("pst1", [1, TW], F32))
        pst2 = E(nc.psum_tensor("pst2", [1, TW], F32))
        ptr2 = E(nc.psum_tensor("ptr2", [128, 1024], BF16))

        dS = RotSem(es, nc, "dS")
        dG = RotSem(es, nc, "dG")
        sV = Sem(es, nc, "sV")
        sS = Sem(es, nc, "sS")
        sT = Sem(es, nc, "sT")
        sP = Sem(es, nc, "sP")
        sC = Sem(es, nc, "sC")

        blk = E(nc.Block())

        @blk.sync
        def _(q):
            for dst, src in ((s_cw, cw), (s_cb, cb), (s_dtb, dtb), (s_Dp, DpI),
                             (s_nw, nwI), (s_nb, nbI), (s_logA, logA),
                             (s_id, identI), (s_sel32, sel32I), (s_sel8, sel8I),
                             (s_onesb, onesbI), (s_ones1, ones1I),
                             (s_ones1b, ones1bI), (s_eps, epsI),
                             (s_dtwf, dtw)):
                dS.issue(q.dma_start(out=dst[:], in_=src))
        c0 = dS.barrier()
        dS.rotate()

        @blk.scalar
        def _(s):
            W(s, c0)
            sS.inc(s.activation(Acol[:], s_logA[:], AF.Exp))
        aexp = sS.v

        @blk.vector
        def _(v):
            v.wait_ge(sS.h, aexp)
            sV.inc(v.tensor_scalar(Acol[:], Acol[:], -1.0, None, ALU.mult))
            sV.inc(v.tensor_copy(s_dtwb[:], s_dtwf[:]))
        consts_ready = sV.v

        with contextlib.ExitStack() as p0s:
            xnT = p0s.enter_context(nc.sbuf_tensor("xnT", [128, NCH * T], BF16))
            p0x = p0s.enter_context(nc.sbuf_tensor("p0x", [128, 2 * D], F32))
            p0sq = p0s.enter_context(nc.sbuf_tensor("p0sq", [128, D], F32))
            p0xn = p0s.enter_context(nc.sbuf_tensor("p0xn", [128, 2 * D], BF16))
            p0r = p0s.enter_context(nc.sbuf_tensor("p0r", [128, 16], F32))

            xload = [None] * NTT
            xncnt = [0] * NTT
            rstdcnt = [0] * NTT
            trcnt = [[0] * NCH for _ in range(NTT)]
            evaccnt = [[0] * NCH for _ in range(NTT)]

            for tt in range(NTT):
                par = tt % 2
                co = par * 8

                @blk.sync
                def _(q, tt=tt, par=par):
                    if tt >= 2:
                        q.wait_ge(sV.h, xncnt[tt - 2])
                    dS.rotate()
                    xload[tt] = dS.issue(q.dma_start(
                        out=p0x[:, par * D:(par + 1) * D],
                        in_=x_tok[tt * 128:(tt + 1) * 128, :]))

                @blk.vector
                def _(v, tt=tt, par=par, co=co):
                    xs = p0x[:, par * D:(par + 1) * D]
                    W(v, xload[tt])
                    if tt >= 2:
                        v.wait_ge(sS.h, rstdcnt[tt - 2])
                    v.reduce_sum(p0r[:, co:co + 1], xs, axis=AX.X)
                    v.tensor_tensor(p0sq[:], xs, xs, ALU.mult)
                    sV.inc(v.reduce_sum(p0r[:, co + 1:co + 2], p0sq[:], axis=AX.X))
                    v.wait_ge(sV.h, sV.v)
                    v.tensor_scalar(p0r[:, co + 2:co + 3], p0r[:, co:co + 1],
                                    1.0 / D, None, ALU.mult)
                    v.tensor_scalar(p0r[:, co + 3:co + 4], p0r[:, co + 1:co + 2],
                                    1.0 / D, None, ALU.mult)
                    sV.inc(v.scalar_tensor_tensor(
                        p0r[:, co + 4:co + 5], p0r[:, co + 2:co + 3],
                        p0r[:, co + 2:co + 3], p0r[:, co + 3:co + 4],
                        ALU.mult, ALU.subtract))
                nv = sV.v

                @blk.scalar
                def _(s, tt=tt, co=co, nv=nv):
                    s.wait_ge(sV.h, nv)
                    s.activation(p0r[:, co + 5:co + 6], p0r[:, co + 4:co + 5],
                                 AF.Ln, scale=-1.0, bias=s_eps[:])
                    rstdcnt[tt] = sS.inc(s.activation(
                        p0r[:, co + 6:co + 7], p0r[:, co + 5:co + 6],
                        AF.Exp, scale=-0.5))

                @blk.vector
                def _(v, tt=tt, par=par, co=co):
                    v.wait_ge(sS.h, rstdcnt[tt])
                    if tt >= 2:
                        v.wait_ge(sT.h, trcnt[tt - 2][NCH - 1])
                    xncnt[tt] = sV.inc(v.tensor_scalar(
                        p0xn[:, par * D:(par + 1) * D],
                        p0x[:, par * D:(par + 1) * D],
                        p0r[:, co + 2:co + 3], p0r[:, co + 6:co + 7],
                        ALU.subtract, ALU.mult))

                @blk.tensor
                def _(t, tt=tt, par=par):
                    t.wait_ge(sV.h, xncnt[tt])
                    t.wait_ge(sV.h, consts_ready)
                    if tt > 0:
                        t.wait_ge(sS.h, evaccnt[tt - 1][NCH - 1])
                    for j in range(NCH):
                        trcnt[tt][j] = sT.inc(t.matmul(
                            ptr2[:, j * 128:(j + 1) * 128],
                            p0xn[:, par * D + j * 128: par * D + (j + 1) * 128],
                            s_id[:], is_transpose=True,
                            start=(j == 0), stop=(j == NCH - 1)))

                @blk.scalar
                def _(s, tt=tt):
                    for j in range(NCH):
                        s.wait_ge(sT.h, trcnt[tt][j])
                        evaccnt[tt][j] = sS.inc(s.copy(
                            xnT[:, j * T + tt * 128: j * T + (tt + 1) * 128],
                            ptr2[:, j * 128:(j + 1) * 128]))

                @blk.gpsimd
                def _(g, tt=tt):
                    g.wait_ge(sS.h, evaccnt[tt][NCH - 1])
                    if tt % 4 == 0:
                        dG.rotate()
                    for j in range(NCH):
                        dG.issue(g.dma_start(
                            out=xnt_d[j * 128:(j + 1) * 128, tt * 128:(tt + 1) * 128],
                            in_=xnT[:, j * T + tt * 128: j * T + (tt + 1) * 128]))
            p0_done_s = evaccnt[NTT - 1][NCH - 1]
            p0_bar = dG.barrier()

            if "xnt" in taps:
                @blk.gpsimd
                def _(g):
                    W(g, p0_bar)
                    dG.rotate()
                    dG.issue(g.dma_start(out=taps["xnt"], in_=xnt_d))

            if "mpre" in phases:
                with contextlib.ExitStack() as mp:
                    ipTb = mp.enter_context(nc.sbuf_tensor("ipTb", [128, NCH * 2 * DIH], BF16))
                    ipstg = mp.enter_context(nc.sbuf_tensor("ipstg", [128, 2 * 2 * DIH], F32))
                    xpTb = mp.enter_context(nc.sbuf_tensor("xpTb", [128, NCH * 33], BF16))
                    xpstg = mp.enter_context(nc.sbuf_tensor("xpstg", [128, NCH * 33], F32))
                    xp_sb = mp.enter_context(nc.sbuf_tensor("xp_sb", [128, TW + 3], F32))
                    cvac = mp.enter_context(nc.sbuf_tensor("cvac", [128, 2 * TW], F32))
                    tails = mp.enter_context(nc.sbuf_tensor("tails", [128, NCH * 3], F32))
                    u_sb = mp.enter_context(nc.sbuf_tensor("u_sb", [128, 2 * TW], F32))
                    sz_sb = mp.enter_context(nc.sbuf_tensor("sz_sb", [128, 2 * TW], F32))
                    ubf = mp.enter_context(nc.sbuf_tensor("ubf", [128, NCH * TW], BF16))
                    ssm_sb = mp.enter_context(nc.sbuf_tensor("ssm_sb", [33, TW], F32))

                    ipld = [None] * NCH
                    ipcast = [0] * NCH

                    @blk.sync
                    def _(q):
                        dS.rotate()
                        dS.issue(q.dma_start(out=xpstg[:], in_=xpT))
                    xp_ld = dS.barrier()

                    for k in range(NCH):
                        @blk.sync
                        def _(q, k=k):
                            if k >= 2:
                                q.wait_ge(sV.h, ipcast[k - 2])
                            dS.rotate()
                            ipld[k] = dS.issue(q.dma_start(
                                out=ipstg[:, (k % 2) * 2 * DIH:(k % 2 + 1) * 2 * DIH],
                                in_=ipT[k * 128:(k + 1) * 128, :]))

                        @blk.vector
                        def _(v, k=k):
                            W(v, ipld[k])
                            ipcast[k] = sV.inc(v.tensor_copy(
                                ipTb[:, k * 2 * DIH:(k + 1) * 2 * DIH],
                                ipstg[:, (k % 2) * 2 * DIH:(k % 2 + 1) * 2 * DIH]))

                    @blk.vector
                    def _(v):
                        W(v, xp_ld)
                        sV.inc(v.tensor_copy(xpTb[:], xpstg[:]))
                        sV.inc(v.memset(tails[:], 0.0))
                    mpre_ready = sV.v

                    mm_ev = [[0] * 16 for _ in range(NTW)]
                    u_done = [[0] * NCH for _ in range(NTW)]
                    ubf_done = [[0] * NCH for _ in range(NTW)]
                    ssm_ev = [0] * NTW
                    mmcnt = [[0] * 16 for _ in range(NTW)]
                    xprojcnt = [0] * NTW

                    for tw in range(NTW):
                        t0 = tw * TW
                        for m in range(16):
                            par = m % 2

                            @blk.tensor
                            def _(t, tw=tw, m=m, par=par, t0=t0):
                                if tw == 0 and m == 0:
                                    t.wait_ge(sV.h, mpre_ready)
                                    t.wait_ge(sS.h, p0_done_s)
                                pm = (tw, m - 2) if m >= 2 else (tw - 1, 14 + m)
                                if pm[0] >= 0:
                                    cnt = mm_ev[pm[0]][pm[1]]
                                    if pm[1] < 8:
                                        t.wait_ge(sV.h, cnt)
                                    else:
                                        t.wait_ge(sS.h, cnt)
                                if m == 0:
                                    t.wait_ge(sV.h, ipcast[NCH - 1])
                                for k in range(NCH):
                                    mmi = t.matmul(
                                        pb[par][:],
                                        ipTb[:, k * 2 * DIH + m * 128: k * 2 * DIH + (m + 1) * 128],
                                        xnT[:, k * T + t0: k * T + t0 + TW],
                                        start=(k == 0), stop=(k == NCH - 1))
                                mmcnt[tw][m] = sT.inc(mmi)

                            if m < 8:
                                @blk.vector
                                def _(v, tw=tw, m=m, par=par):
                                    v.wait_ge(sT.h, mmcnt[tw][m])
                                    v.tensor_copy(xp_sb[:, 0:3], tails[:, m * 3:(m + 1) * 3])
                                    mm_ev[tw][m] = sV.inc(v.tensor_copy(
                                        xp_sb[:, 3:TW + 3], pb[par][:]))
                                    v.wait_ge(sV.h, sV.v)
                                    v.tensor_copy(tails[:, m * 3:(m + 1) * 3],
                                                  xp_sb[:, TW:TW + 3])
                                    ac = cvac[:, par * TW:(par + 1) * TW]
                                    v.tensor_scalar(ac, xp_sb[:, 0:TW],
                                                    s_cw[:, m * 4:m * 4 + 1], None, ALU.mult)
                                    for kk in range(1, 4):
                                        v.scalar_tensor_tensor(
                                            ac, xp_sb[:, kk:kk + TW],
                                            s_cw[:, m * 4 + kk:m * 4 + kk + 1],
                                            ac, ALU.mult, ALU.add)
                                    sV.inc(v.tensor_copy(xp_sb[:, 0:1], xp_sb[:, 0:1]))
                                cvdone = sV.v

                                @blk.scalar
                                def _(s, tw=tw, m=m, par=par, cvdone=cvdone):
                                    s.wait_ge(sV.h, cvdone)
                                    u_done[tw][m] = sS.inc(s.activation(
                                        u_sb[:, par * TW:(par + 1) * TW],
                                        cvac[:, par * TW:(par + 1) * TW],
                                        AF.Silu, bias=s_cb[:, m:m + 1]))

                                @blk.vector
                                def _(v, tw=tw, m=m, par=par):
                                    v.wait_ge(sS.h, u_done[tw][m])
                                    ubf_done[tw][m] = sV.inc(v.tensor_copy(
                                        ubf[:, m * TW:(m + 1) * TW],
                                        u_sb[:, par * TW:(par + 1) * TW]))

                                @blk.gpsimd
                                def _(g, tw=tw, m=m, par=par, t0=t0):
                                    g.wait_ge(sS.h, u_done[tw][m])
                                    if m == 0:
                                        dG.rotate()
                                    dG.issue(g.dma_start(
                                        out=u_d[m * 128:(m + 1) * 128, t0:t0 + TW],
                                        in_=u_sb[:, par * TW:(par + 1) * TW]))
                            else:
                                @blk.scalar
                                def _(s, tw=tw, m=m, par=par):
                                    s.wait_ge(sT.h, mmcnt[tw][m])
                                    mm_ev[tw][m] = sS.inc(s.activation(
                                        sz_sb[:, par * TW:(par + 1) * TW],
                                        pb[par][:], AF.Silu))

                                @blk.gpsimd
                                def _(g, tw=tw, m=m, par=par, t0=t0):
                                    g.wait_ge(sS.h, mm_ev[tw][m])
                                    if m == 8:
                                        dG.rotate()
                                    dG.issue(g.dma_start(
                                        out=sz_d[(m - 8) * 128:(m - 7) * 128, t0:t0 + TW],
                                        in_=sz_sb[:, par * TW:(par + 1) * TW]))

                        @blk.tensor
                        def _(t, tw=tw, t0=t0):
                            t.wait_ge(sV.h, ubf_done[tw][NCH - 1])
                            if tw >= 1:
                                t.wait_ge(sV.h, ssm_ev[tw - 1])
                            for k in range(NCH):
                                mmi = t.matmul(
                                    pb[2][0:33, :],
                                    xpTb[:, k * 33:(k + 1) * 33],
                                    ubf[:, k * TW:(k + 1) * TW],
                                    start=(k == 0), stop=(k == NCH - 1))
                            xprojcnt[tw] = sT.inc(mmi)

                        @blk.vector
                        def _(v, tw=tw):
                            v.wait_ge(sT.h, xprojcnt[tw])
                            ssm_ev[tw] = sV.inc(v.tensor_copy(ssm_sb[:], pb[2][0:33, :]))

                        @blk.gpsimd
                        def _(g, tw=tw, t0=t0):
                            g.wait_ge(sV.h, ssm_ev[tw])
                            dG.rotate()
                            dG.issue(g.dma_start(out=cc_ssm_i.ap()[:, t0:t0 + TW],
                                                 in_=ssm_sb[:]))

                    mpre_bar = dG.barrier()

                    @blk.gpsimd
                    def _(g):
                        W(g, mpre_bar)
                        sC.inc(g.collective_compute(
                            "AllReduce", ALU.add, replica_groups=GROUPS,
                            ins=[cc_ssm_i.ap().opt()], outs=[cc_ssm_o.ap().opt()]))
                    cc_ssm_done = sC.v

                    if "u" in taps or "sz" in taps or "ssm" in taps:
                        @blk.gpsimd
                        def _(g):
                            g.wait_ge(sC.h, cc_ssm_done)
                            dG.rotate()
                            if "u" in taps:
                                dG.issue(g.dma_start(out=taps["u"], in_=u_d))
                            if "sz" in taps:
                                dG.issue(g.dma_start(out=taps["sz"], in_=sz_d))
                            if "ssm" in taps:
                                dG.issue(g.dma_start(out=taps["ssm"], in_=cc_ssm_o.ap()))

        if "mscan" in phases:
            with contextlib.ExitStack() as ms:
                BCb = ms.enter_context(nc.sbuf_tensor("BCb", [32, T], BF16))
                rawb = ms.enter_context(nc.sbuf_tensor("rawb", [1, T], BF16))
                u_tw = ms.enter_context(nc.sbuf_tensor("u_tw", [128, NCH * TW], F32))
                del_tw = ms.enter_context(nc.sbuf_tensor("del_tw", [128, NCH * TW], F32))
                du_tw = ms.enter_context(nc.sbuf_tensor("du_tw", [128, NCH * TW], BF16))
                vb = ms.enter_context(nc.sbuf_tensor("vb", [128, 2 * TW], F32))
                eb = ms.enter_context(nc.sbuf_tensor("eb", [128, 2 * TW], F32))
                dAb = ms.enter_context(nc.sbuf_tensor("dAb", [128, 8 * TW], F32))
                dbub = ms.enter_context(nc.sbuf_tensor("dbub", [128, 8 * TW], BF16))
                s_bufs = ms.enter_context(nc.sbuf_tensor("s_bufs", [128, 16 * TW], F32))
                csb = ms.enter_context(nc.sbuf_tensor("csb", [128, 8 * TW], BF16))
                crep = ms.enter_context(nc.sbuf_tensor("crep", [128, 2 * TW], BF16))
                brep = ms.enter_context(nc.sbuf_tensor("brep", [128, 2 * TW], BF16))
                carr = ms.enter_context(nc.sbuf_tensor("carr", [128, 128], F32))
                y_sb = ms.enter_context(nc.sbuf_tensor("y_sb", [128, 2 * TW], F32))
                ybfb = ms.enter_context(nc.sbuf_tensor("ybfb", [128, TW], BF16))
                ysqb = ms.enter_context(nc.sbuf_tensor("ysqb", [128, TW], BF16))
                stev = ms.enter_context(nc.sbuf_tensor("stev", [1, 2 * TW], F32))

                mscan_entry_bar = dG.barrier() + dS.barrier()

                with contextlib.ExitStack() as bcs:
                    BCf = bcs.enter_context(nc.sbuf_tensor("BCf", [33, 2 * 1024], F32))
                    bch_ld = [None] * 4
                    bch_cast = [0] * 4
                    for ch in range(4):
                        @blk.sync
                        def _(q, ch=ch):
                            if ch == 0:
                                q.wait_ge(sC.h, cc_ssm_done)
                                W(q, mscan_entry_bar)
                            if ch >= 2:
                                q.wait_ge(sV.h, bch_cast[ch - 2])
                            dS.rotate()
                            co = (ch % 2) * 1024
                            dS.issue(q.dma_start(
                                out=BCf[0:32, co:co + 1024],
                                in_=cc_ssm_o.ap()[0:32, ch * 1024:(ch + 1) * 1024]))
                            bch_ld[ch] = dS.issue(q.dma_start(
                                out=BCf[32:33, co:co + 1024],
                                in_=cc_ssm_o.ap()[32:33, ch * 1024:(ch + 1) * 1024]))

                        @blk.vector
                        def _(v, ch=ch):
                            W(v, bch_ld[ch])
                            co = (ch % 2) * 1024
                            v.tensor_copy(BCb[:, ch * 1024:(ch + 1) * 1024],
                                          BCf[0:32, co:co + 1024])
                            bch_cast[ch] = sV.inc(v.tensor_copy(
                                rawb[:, ch * 1024:(ch + 1) * 1024],
                                BCf[32:33, co:co + 1024]))

                    @blk.vector
                    def _(v):
                        v.wait_ge(sV.h, bch_cast[3])
                        sV.inc(v.memset(carr[:], 0.0))
                    scan_ready = sV.v

                last_tts = [0] * NCH
                last_cs = [0] * NCH
                last_nsum = [0] * NCH
                last_dA = [0] * NCH
                tts_slot = [0] * 4
                cs_slot = [0] * 4
                cs_hist = [0] * 16
                nsum_slot = [0] * 4
                yev_slot = [0] * 4
                ydma_slot = [None, None]
                stat_guard = 0
                last_Cevac = 0
                bc_evac = 0
                b_evac = 0
                dbu_hist = [0, 0]
                u_ld = [None] * NTW
                del_done = [[0] * NCH for _ in range(NTW)]
                du_done = [[0] * NCH for _ in range(NTW)]
                yev = [0] * NCH
                stat_mm = 0
                stat_ev = [0] * NTW
                prev_del_consumed = 0
                prev_u_consumed = 0

                vcnt = [0] * NCH
                ecnt = [0] * NCH
                for tw in range(NTW):
                    t0 = tw * TW

                    @blk.sync
                    def _(q, tw=tw, t0=t0):
                        if tw == 0:
                            q.wait_ge(sC.h, cc_ssm_done)
                            q.wait_ge(sV.h, scan_ready)
                        else:
                            q.wait_ge(sV.h, prev_u_consumed)
                        dS.rotate()
                        for m in range(NCH):
                            u_ld[tw] = dS.issue(q.dma_start(
                                out=u_tw[:, m * TW:(m + 1) * TW],
                                in_=u_d[m * 128:(m + 1) * 128, t0:t0 + TW]))

                    for mc in range(NCH):
                        par = mc % 2

                        @blk.tensor
                        def _(t, tw=tw, mc=mc, t0=t0):
                            if tw == 0 and mc == 0:
                                t.wait_ge(sV.h, scan_ready)
                            t.wait_ge(sV.h, vcnt[(mc - 1) % NCH])
                            t.wait_ge(sS.h, bc_evac)
                            mmi = t.matmul(pb[4][:],
                                           s_dtwb[0:1, mc * 128:(mc + 1) * 128],
                                           rawb[0:1, t0:t0 + TW],
                                           start=True, stop=True)
                            del_done[tw][mc] = sT.inc(mmi)

                        @blk.vector
                        def _(v, tw=tw, mc=mc, par=par):
                            v.wait_ge(sT.h, del_done[tw][mc])
                            v.wait_ge(sS.h, ecnt[(mc - 2) % NCH])
                            vcnt[mc] = sV.inc(v.tensor_scalar(
                                vb[:, par * TW:(par + 1) * TW], pb[4][:],
                                s_dtb[:, mc:mc + 1], 20.0, ALU.add, ALU.min))

                        @blk.scalar
                        def _(s, tw=tw, mc=mc, par=par):
                            s.wait_ge(sV.h, vcnt[mc])
                            s.activation(eb[:, par * TW:(par + 1) * TW],
                                         vb[:, par * TW:(par + 1) * TW], AF.Exp)
                            ecnt[mc] = sS.inc(s.activation(
                                eb[:, par * TW:(par + 1) * TW],
                                eb[:, par * TW:(par + 1) * TW], AF.Ln, bias=1.0))

                        @blk.vector
                        def _(v, tw=tw, mc=mc, par=par):
                            v.wait_ge(sS.h, ecnt[mc])
                            if tw >= 1:
                                v.wait_ge(sS.h, prev_del_consumed)
                            v.tensor_scalar(
                                del_tw[:, mc * TW:(mc + 1) * TW],
                                eb[:, par * TW:(par + 1) * TW],
                                1e-3, 0.1, ALU.max, ALU.min)
                            W(v, u_ld[tw])
                            du_done[tw][mc] = sV.inc(v.tensor_tensor(
                                du_tw[:, mc * TW:(mc + 1) * TW],
                                del_tw[:, mc * TW:(mc + 1) * TW],
                                u_tw[:, mc * TW:(mc + 1) * TW], ALU.mult))

                    if "delta" in taps or "du" in taps:
                        @blk.gpsimd
                        def _(g, tw=tw, t0=t0):
                            g.wait_ge(sV.h, du_done[tw][NCH - 1])
                            dG.rotate()
                            for m in range(NCH):
                                if "delta" in taps:
                                    dG.issue(g.dma_start(
                                        out=taps["delta"][m * 128:(m + 1) * 128, t0:t0 + TW],
                                        in_=del_tw[:, m * TW:(m + 1) * TW]))
                                if "du" in taps:
                                    dG.issue(g.dma_start(
                                        out=taps["du"][m * 128:(m + 1) * 128, t0:t0 + TW],
                                        in_=du_tw[:, m * TW:(m + 1) * TW]))

                    for dh in range(2):
                        for n in range(NS):
                            npar = n % 2

                            @blk.tensor
                            def _(t, tw=tw, dh=dh, n=n):
                                t.wait_ge(sS.h, bc_evac)
                                t.wait_ge(sV.h, du_done[tw][NCH - 1])
                                t.matmul(pb[4][:],
                                         s_sel32[:, n * 128:(n + 1) * 128],
                                         BCb[0:32, tw * TW:tw * TW + TW],
                                         start=True, stop=True).then_inc(sT.h, 1)
                            sT.v += 1
                            bbuild = sT.v

                            @blk.scalar
                            def _(s, n=n, npar=npar, bbuild=bbuild):
                                s.wait_ge(sT.h, bbuild)
                                s.wait_ge(sV.h, dbu_hist[npar])
                                nonlocal b_evac
                                b_evac = sS.inc(s.copy(
                                    brep[:, npar * TW:(npar + 1) * TW], pb[4][:]))
                            b_evac_c = 0

                            @blk.vector
                            def _(v, tw=tw, dh=dh, n=n, npar=npar):
                                v.wait_ge(sS.h, b_evac)
                                for dci in range(4):
                                    dc = dh * 4 + dci
                                    v.tensor_tensor(
                                        dbub[:, dc * TW:(dc + 1) * TW],
                                        du_tw[:, dc * TW:(dc + 1) * TW],
                                        brep[:, npar * TW:(npar + 1) * TW], ALU.mult)
                                dbu_hist[npar] = sV.inc(v.tensor_copy(dbub[:, 0:1], dbub[:, 0:1]))
                            dbu_last = sV.v

                            @blk.tensor
                            def _(t, tw=tw, n=n):
                                t.wait_ge(sS.h, b_evac)
                                t.matmul(pb[4][:],
                                         s_sel32[:, (16 + n) * 128:(17 + n) * 128],
                                         BCb[0:32, tw * TW:tw * TW + TW],
                                         start=True, stop=True).then_inc(sT.h, 1)
                            sT.v += 1
                            cbuild = sT.v

                            @blk.scalar
                            def _(s, n=n, npar=npar, cbuild=cbuild):
                                s.wait_ge(sT.h, cbuild)
                                s.wait_ge(sP.h, sP.v)
                                nonlocal last_Cevac, bc_evac
                                last_Cevac = sS.inc(s.copy(
                                    crep[:, npar * TW:(npar + 1) * TW], pb[4][:]))
                                bc_evac = last_Cevac

                            @blk.scalar
                            def _(s, tw=tw, dh=dh, n=n):
                                for dci in range(4):
                                    dc = dh * 4 + dci
                                    s.wait_ge(sV.h, last_tts[dc])
                                    last_dA[dc] = sS.inc(s.activation(
                                        dAb[:, dc * TW:(dc + 1) * TW],
                                        del_tw[:, dc * TW:(dc + 1) * TW],
                                        AF.Exp, scale=Acol[:, dc * NS + n:dc * NS + n + 1]))

                            @blk.vector
                            def _(v, tw=tw, dh=dh, n=n, npar=npar):
                                for dci in range(4):
                                    dc = dh * 4 + dci
                                    sl = npar * 8 + dc
                                    v.wait_ge(sS.h, last_dA[dc])
                                    v.wait_ge(sP.h, cs_hist[sl])
                                    ci = n * 8 + dc
                                    init = carr[:, ci:ci + 1] if tw > 0 else 0.0
                                    sV.inc(v.tensor_tensor_scan(
                                        s_bufs[:, sl * TW:(sl + 1) * TW],
                                        dAb[:, dc * TW:(dc + 1) * TW],
                                        dbub[:, dc * TW:(dc + 1) * TW],
                                        init, ALU.mult, ALU.add))
                                v.wait_ge(sV.h, sV.v)
                                c0_ = n * 8 + dh * 4
                                svw = s_bufs.rearrange("p (c w) -> p c w", w=TW)
                                tcnt = sV.inc(v.tensor_copy(
                                    carr[:, c0_:c0_ + 4],
                                    svw[:, npar * 8 + dh * 4:npar * 8 + dh * 4 + 4, TW - 1:TW]))
                                for dci in range(4):
                                    last_tts[dh * 4 + dci] = tcnt

                            @blk.gpsimd
                            def _(g, tw=tw, dh=dh, n=n, npar=npar):
                                for dci in range(4):
                                    dc = dh * 4 + dci
                                    sl = npar * 8 + dc
                                    g.wait_ge(sV.h, last_tts[dc])
                                    g.wait_ge(sS.h, last_Cevac)
                                    g.wait_ge(sT.h, last_nsum[dc])
                                    last_cs[dc] = sP.inc(g.tensor_tensor(
                                        csb[:, dc * TW:(dc + 1) * TW],
                                        s_bufs[:, sl * TW:(sl + 1) * TW],
                                        crep[:, npar * TW:(npar + 1) * TW], ALU.mult))
                                    cs_hist[sl] = last_cs[dc]

                            @blk.tensor
                            def _(t, tw=tw, dh=dh, n=n):
                                for dci in range(4):
                                    dc = dh * 4 + dci
                                    t.wait_ge(sP.h, last_cs[dc])
                                    if n == 0:
                                        t.wait_ge(sV.h, yev_slot[dci])
                                    last_nsum[dc] = sT.inc(t.matmul(
                                        pb[dci][:], s_id[:],
                                        csb[:, dc * TW:(dc + 1) * TW],
                                        start=(n == 0), stop=(n == NS - 1)))
                                    nsum_slot[dci] = last_nsum[dc]

                        for dci in range(4):
                            dc = dh * 4 + dci
                            ypar = dci % 2

                            @blk.vector
                            def _(v, tw=tw, dh=dh, dc=dc, dci=dci, ypar=ypar):
                                v.wait_ge(sT.h, last_nsum[dc])
                                W(v, ydma_slot[ypar])
                                v.wait_ge(sT.h, stat_guard)
                                v.scalar_tensor_tensor(
                                    y_sb[:, ypar * TW:(ypar + 1) * TW],
                                    u_tw[:, dc * TW:(dc + 1) * TW],
                                    s_Dp[:, dc:dc + 1], pb[dci][:],
                                    ALU.mult, ALU.add)
                                v.tensor_copy(ybfb[:], y_sb[:, ypar * TW:(ypar + 1) * TW])
                                yev[dc] = sV.inc(v.tensor_tensor(
                                    ysqb[:], ybfb[:], ybfb[:], ALU.mult))
                                yev_slot[dci] = yev[dc]

                            @blk.gpsimd
                            def _(g, tw=tw, dc=dc, ypar=ypar, t0=t0):
                                g.wait_ge(sV.h, yev[dc])
                                if dc % 4 == 0:
                                    dG.rotate()
                                ydma_slot[ypar] = dG.issue(g.dma_start(
                                    out=y_d[dc * 128:(dc + 1) * 128, t0:t0 + TW],
                                    in_=y_sb[:, ypar * TW:(ypar + 1) * TW]))

                            @blk.tensor
                            def _(t, tw=tw, dh=dh, dc=dc, dci=dci):
                                t.wait_ge(sV.h, yev[dc])
                                if tw > 0 and dh == 0 and dci == 0:
                                    t.wait_ge(sV.h, stat_ev[tw - 1])
                                first = (dh == 0 and dci == 0)
                                last = (dh == 1 and dci == 3)
                                t.matmul(pst1[:], s_onesb[:], ybfb[:],
                                         start=first, stop=last)
                                mmi = t.matmul(pst2[:], s_onesb[:], ysqb[:],
                                               start=first, stop=last)
                                nonlocal stat_mm, stat_guard
                                stat_mm = sT.inc(mmi)
                                stat_guard = stat_mm
                        prev_u_consumed = max(yev)

                    prev_del_consumed = max(last_dA)

                    if "carr" in taps and tw == 0:
                        @blk.gpsimd
                        def _(g):
                            g.wait_ge(sV.h, max(last_tts))
                            dG.rotate()
                            dG.issue(g.dma_start(out=taps["carr"], in_=carr[:]))

                    @blk.vector
                    def _(v, tw=tw):
                        v.wait_ge(sT.h, stat_mm)
                        v.tensor_copy(stev[:, 0:TW], pst1[:])
                        stat_ev[tw] = sV.inc(v.tensor_copy(stev[:, TW:2 * TW], pst2[:]))

                    @blk.gpsimd
                    def _(g, tw=tw, t0=t0):
                        g.wait_ge(sV.h, stat_ev[tw])
                        dG.rotate()
                        dG.issue(g.dma_start(out=cc_st_i.ap()[0:1, t0:t0 + TW],
                                             in_=stev[:, 0:TW]))
                        dG.issue(g.dma_start(out=cc_st_i.ap()[1:2, t0:t0 + TW],
                                             in_=stev[:, TW:2 * TW]))

                mscan_bar = dG.barrier()

                @blk.gpsimd
                def _(g):
                    W(g, mscan_bar)
                    sC.inc(g.collective_compute(
                        "AllReduce", ALU.add, replica_groups=GROUPS,
                        ins=[cc_st_i.ap().opt()], outs=[cc_st_o.ap().opt()]))
                cc_st_done = sC.v

                if "y" in taps or "stats" in taps:
                    @blk.gpsimd
                    def _(g):
                        g.wait_ge(sC.h, cc_st_done)
                        dG.rotate()
                        if "y" in taps:
                            dG.issue(g.dma_start(out=taps["y"], in_=y_d))
                        if "stats" in taps:
                            dG.issue(g.dma_start(out=taps["stats"], in_=cc_st_o.ap()))

        if "ln2" in phases:
            with contextlib.ExitStack() as l2:
                opTb = l2.enter_context(nc.sbuf_tensor("opTb", [128, NCH * D], BF16))
                opstg = l2.enter_context(nc.sbuf_tensor("opstg", [128, 2 * D], F32))
                st0 = l2.enter_context(nc.sbuf_tensor("st0", [1, T], F32))
                st1 = l2.enter_context(nc.sbuf_tensor("st1", [1, T], F32))
                murow = l2.enter_context(nc.sbuf_tensor("murow", [1, T], F32))
                rsrow = l2.enter_context(nc.sbuf_tensor("rsrow", [1, T], F32))
                muhi = l2.enter_context(nc.sbuf_tensor("muhi", [1, T], BF16))
                mulo = l2.enter_context(nc.sbuf_tensor("mulo", [1, T], BF16))
                rshi = l2.enter_context(nc.sbuf_tensor("rshi", [1, T], BF16))
                rslo = l2.enter_context(nc.sbuf_tensor("rslo", [1, T], BF16))
                tmpr = l2.enter_context(nc.sbuf_tensor("tmpr", [1, T], F32))
                yb2 = l2.enter_context(nc.sbuf_tensor("yb2", [128, 2 * TW], F32))
                szb2 = l2.enter_context(nc.sbuf_tensor("szb2", [128, 2 * TW], F32))
                t1b = l2.enter_context(nc.sbuf_tensor("t1b", [128, TW], F32))
                t2b = l2.enter_context(nc.sbuf_tensor("t2b", [128, TW], F32))
                gatedb = l2.enter_context(nc.sbuf_tensor("gatedb", [128, NCH * TW], BF16))
                sgsb = l2.enter_context(nc.sbuf_tensor("sgsb", [128, 2 * TW], F32))

                opld = [None] * NCH
                opcast = [0] * NCH
                ln2_entry_bar = dG.barrier() + dS.barrier()
                for k in range(NCH):
                    @blk.sync
                    def _(q, k=k):
                        if k == 0:
                            W(q, ln2_entry_bar)
                        if k >= 2:
                            q.wait_ge(sV.h, opcast[k - 2])
                        dS.rotate()
                        opld[k] = dS.issue(q.dma_start(
                            out=opstg[:, (k % 2) * D:(k % 2 + 1) * D],
                            in_=opT[k * 128:(k + 1) * 128, :]))

                    @blk.vector
                    def _(v, k=k):
                        W(v, opld[k])
                        opcast[k] = sV.inc(v.tensor_copy(
                            opTb[:, k * D:(k + 1) * D],
                            opstg[:, (k % 2) * D:(k % 2 + 1) * D]))

                @blk.sync
                def _(q):
                    q.wait_ge(sC.h, cc_st_done)
                    dS.rotate()
                    dS.issue(q.dma_start(out=st0[:], in_=cc_st_o.ap()[0:1, :]))
                    dS.issue(q.dma_start(out=st1[:], in_=cc_st_o.ap()[1:2, :]))
                st_ld = dS.barrier()

                @blk.vector
                def _(v):
                    W(v, st_ld)
                    v.tensor_scalar(murow[:], st0[:], 1.0 / 2048.0, None, ALU.mult)
                    v.tensor_scalar(st1[:], st1[:], 1.0 / 2048.0, None, ALU.mult)
                    v.tensor_tensor(rsrow[:], murow[:], murow[:], ALU.mult)
                    sV.inc(v.tensor_tensor(rsrow[:], rsrow[:], st1[:], ALU.subtract))
                negvar_done = sV.v

                @blk.scalar
                def _(s):
                    s.wait_ge(sV.h, negvar_done)
                    s.activation(rsrow[:], rsrow[:], AF.Ln, scale=-1.0,
                                 bias=s_eps[0:1, :])
                    sS.inc(s.activation(rsrow[:], rsrow[:], AF.Exp, scale=-0.5))
                rstd_done = sS.v

                @blk.vector
                def _(v):
                    v.wait_ge(sS.h, rstd_done)
                    v.tensor_copy(muhi[:], murow[:])
                    v.tensor_copy(rshi[:], rsrow[:])
                    sV.inc(v.tensor_tensor(tmpr[:], murow[:], muhi[:], ALU.subtract))
                    v.wait_ge(sV.h, sV.v)
                    v.tensor_copy(mulo[:], tmpr[:])
                    sV.inc(v.tensor_tensor(tmpr[:], rsrow[:], rshi[:], ALU.subtract))
                    v.wait_ge(sV.h, sV.v)
                    sV.inc(v.tensor_copy(rslo[:], tmpr[:]))
                hilo_done = sV.v

                g_done = [0] * NCH
                sg_ev = [0, 0]
                ln2_y_ld = [None] * NCH
                opmm = [0] * 8
                ln2_prev_g = 0

                for tw in range(NTW):
                    t0 = tw * TW

                    @blk.tensor
                    def _(t, tw=tw, t0=t0):
                        t.wait_ge(sV.h, hilo_done)
                        t.wait_ge(sV.h, opcast[NCH - 1])
                        if tw >= 1:
                            t.wait_ge(sV.h, g_done[NCH - 1])
                        t.matmul(pb[2][:], s_ones1b[:], muhi[0:1, t0:t0 + TW],
                                 start=True, stop=False)
                        t.matmul(pb[2][:], s_ones1b[:], mulo[0:1, t0:t0 + TW],
                                 start=False, stop=True)
                        t.matmul(pb[3][:], s_ones1b[:], rshi[0:1, t0:t0 + TW],
                                 start=True, stop=False)
                        t.matmul(pb[3][:], s_ones1b[:], rslo[0:1, t0:t0 + TW],
                                 start=False, stop=True).then_inc(sT.h, 1)
                    sT.v += 1
                    bcast_done = sT.v

                    for dc in range(NCH):
                        par = dc % 2

                        @blk.sync
                        def _(q, tw=tw, dc=dc, par=par, t0=t0):
                            if tw == 0 and dc == 0:
                                q.wait_ge(sC.h, cc_st_done)
                            prev = g_done[dc - 2] if dc >= 2 else ln2_prev_g
                            if tw > 0 or dc >= 2:
                                q.wait_ge(sV.h, prev)
                            if dc % 4 == 0:
                                dS.rotate()
                            dS.issue(q.dma_start(
                                out=yb2[:, par * TW:(par + 1) * TW],
                                in_=y_d[dc * 128:(dc + 1) * 128, t0:t0 + TW]))
                            ln2_y_ld[dc] = dS.issue(q.dma_start(
                                out=szb2[:, par * TW:(par + 1) * TW],
                                in_=sz_d[dc * 128:(dc + 1) * 128, t0:t0 + TW]))

                        @blk.vector
                        def _(v, tw=tw, dc=dc, par=par, bcast_done=bcast_done):
                            W(v, ln2_y_ld[dc])
                            v.wait_ge(sT.h, bcast_done)
                            v.tensor_tensor(t1b[:], yb2[:, par * TW:(par + 1) * TW],
                                            pb[2][:], ALU.subtract)
                            v.tensor_tensor(t2b[:], t1b[:], pb[3][:], ALU.mult)
                            v.tensor_scalar(t1b[:], t2b[:], s_nw[:, dc:dc + 1],
                                            s_nb[:, dc:dc + 1], ALU.mult, ALU.add)
                            g_done[dc] = sV.inc(v.tensor_tensor(
                                gatedb[:, dc * TW:(dc + 1) * TW], t1b[:],
                                szb2[:, par * TW:(par + 1) * TW], ALU.mult))
                    ln2_prev_g = g_done[NCH - 1]
                    if "gated" in taps:
                        @blk.gpsimd
                        def _(g, tw=tw, t0=t0):
                            g.wait_ge(sV.h, g_done[NCH - 1])
                            dG.rotate()
                            for m in range(NCH):
                                dG.issue(g.dma_start(
                                    out=taps["gated"][m * 128:(m + 1) * 128, t0:t0 + TW],
                                    in_=gatedb[:, m * TW:(m + 1) * TW]))

                    for m in range(8):
                        par = m % 2

                        @blk.tensor
                        def _(t, tw=tw, m=m, par=par):
                            t.wait_ge(sV.h, g_done[NCH - 1])
                            if tw * 8 + m >= 2:
                                t.wait_ge(sV.h, sg_ev[par])
                            for k in range(NCH):
                                mmi = t.matmul(
                                    pb[par][:],
                                    opTb[:, k * D + m * 128: k * D + (m + 1) * 128],
                                    gatedb[:, k * TW:(k + 1) * TW],
                                    start=(k == 0), stop=(k == NCH - 1))
                            opmm[m] = sT.inc(mmi)

                        @blk.vector
                        def _(v, tw=tw, m=m, par=par):
                            v.wait_ge(sT.h, opmm[m])
                            sg_ev[par] = sV.inc(v.tensor_copy(
                                sgsb[:, par * TW:(par + 1) * TW], pb[par][:]))

                        @blk.gpsimd
                        def _(g, tw=tw, m=m, par=par, t0=t0):
                            g.wait_ge(sV.h, sg_ev[par])
                            if m == 0:
                                dG.rotate()
                            dG.issue(g.dma_start(
                                out=cc_sg_i.ap()[m * 128:(m + 1) * 128, t0:t0 + TW],
                                in_=sgsb[:, par * TW:(par + 1) * TW]))

        ln2_bar = dG.barrier() + dS.barrier()

        if "ffna" in phases:
            with contextlib.ExitStack() as fa:
                xnT2 = fa.enter_context(nc.sbuf_tensor("xnT2", [128, NCH * T], BF16))
                gwTb = fa.enter_context(nc.sbuf_tensor("gwTb", [128, NCH * 8], BF16))
                gwstg = fa.enter_context(nc.sbuf_tensor("gwstg", [128, NCH * 8], F32))
                rb = fa.enter_context(nc.sbuf_tensor("rb", [128, 32], F32))
                combb = fa.enter_context(nc.sbuf_tensor("combb", [128, 256], BF16))
                w13stg = fa.enter_context(nc.sbuf_tensor("w13stg", [128, 2 * 8 * 512], F32))
                w1b = fa.enter_context(nc.sbuf_tensor("w1b", [128, NCH * 512], BF16))
                w3b = fa.enter_context(nc.sbuf_tensor("w3b", [128, NCH * 512], BF16))
                b1c = fa.enter_context(nc.sbuf_tensor("b1c", [128, 9 * 4], F32))
                b3c = fa.enter_context(nc.sbuf_tensor("b3c", [128, 9 * 4], F32))
                cbf = fa.enter_context(nc.sbuf_tensor("cbf", [128, TW], BF16))
                hsb = fa.enter_context(nc.sbuf_tensor("hsb", [128, 2 * TW], BF16))
                hscb = fa.enter_context(nc.sbuf_tensor("hscb", [128, 2 * TW], BF16))
                hhb = fa.enter_context(nc.sbuf_tensor("hhb", [128, 2 * TW], BF16))

                @blk.sync
                def _(q):
                    W(q, p0_bar)
                    W(q, ln2_bar)
                    dS.rotate()
                    dS.issue(q.dma_start(out=gwstg[:], in_=gwT))
                    dS.issue(q.dma_start(out=b1c[:], in_=b1tI))
                    dS.issue(q.dma_start(out=b3c[:], in_=b3tI))
                    for j in range(NCH):
                        dS.issue(q.dma_start(
                            out=xnT2[:, j * T:(j + 1) * T],
                            in_=xnt_d[j * 128:(j + 1) * 128, :]))
                ffna_ld = dS.barrier()

                @blk.vector
                def _(v):
                    W(v, ffna_ld)
                    sV.inc(v.tensor_copy(gwTb[:], gwstg[:]))
                    sV.inc(v.memset(combT05[:], 0.5))
                router_ready = sV.v

                r_mm = [0] * NTT
                r_soft = [0] * NTT
                tr_cnt = [0] * NTT
                tr_ev = [0] * NTT
                for tt in range(NTT):
                    @blk.tensor
                    def _(t, tt=tt):
                        t.wait_ge(sV.h, router_ready)
                        if tt >= 2:
                            t.wait_ge(sV.h, r_soft[tt - 2])
                        for k in range(NCH):
                            mmi = t.matmul(
                                pb[4][:, 0:8],
                                xnT2[:, k * T + tt * 128: k * T + (tt + 1) * 128],
                                gwTb[:, k * 8:(k + 1) * 8],
                                start=(k == 0), stop=(k == NCH - 1))
                        r_mm[tt] = sT.inc(mmi)

                    @blk.vector
                    def _(v, tt=tt):
                        v.wait_ge(sT.h, r_mm[tt])
                        v.tensor_copy(rb[:, 0:8], pb[4][:, 0:8])
                        sV.inc(v.reduce_max(rb[:, 8:9], rb[:, 0:8], axis=AX.X))
                        v.wait_ge(sV.h, sV.v)
                        sV.inc(v.tensor_scalar(rb[:, 9:10], rb[:, 8:9], -1.0,
                                               None, ALU.mult))
                    sm1 = sV.v

                    @blk.scalar
                    def _(s, tt=tt, sm1=sm1):
                        s.wait_ge(sV.h, sm1)
                        sS.inc(s.activation(rb[:, 16:24], rb[:, 0:8], AF.Exp,
                                            bias=rb[:, 9:10]))
                    sm2 = sS.v

                    @blk.vector
                    def _(v, tt=tt, sm2=sm2):
                        v.wait_ge(sS.h, sm2)
                        sV.inc(v.reduce_sum(rb[:, 10:11], rb[:, 16:24], axis=AX.X))
                        v.wait_ge(sV.h, sV.v)
                        sV.inc(v.reciprocal(rb[:, 11:12], rb[:, 10:11]))
                        v.wait_ge(sV.h, sV.v)
                        sV.inc(v.tensor_scalar(rb[:, 16:24], rb[:, 16:24],
                                               rb[:, 11:12], None, ALU.mult))
                        v.wait_ge(sV.h, sV.v)
                        sV.inc(v.reduce_max(rb[:, 8:9], rb[:, 16:24], axis=AX.X))
                        v.wait_ge(sV.h, sV.v)
                        sV.inc(v.tensor_scalar(rb[:, 24:32], rb[:, 16:24],
                                               rb[:, 8:9], None, ALU.is_equal))
                        v.wait_ge(sV.h, sV.v)
                        sV.inc(v.scalar_tensor_tensor(rb[:, 0:8], rb[:, 24:32], -1e30,
                                                      rb[:, 16:24], ALU.mult, ALU.add))
                        v.wait_ge(sV.h, sV.v)
                        sV.inc(v.reduce_max(rb[:, 9:10], rb[:, 0:8], axis=AX.X))
                        v.wait_ge(sV.h, sV.v)
                        sV.inc(v.tensor_scalar(rb[:, 0:8], rb[:, 0:8], rb[:, 9:10],
                                               None, ALU.is_equal))
                        v.wait_ge(sV.h, sV.v)
                        sV.inc(v.tensor_tensor(rb[:, 24:32], rb[:, 24:32], rb[:, 0:8],
                                               ALU.add))
                        v.wait_ge(sV.h, sV.v)
                        par = tt % 2
                        r_soft[tt] = sV.inc(v.tensor_tensor(
                            combb[:, par * 128:par * 128 + 8],
                            rb[:, 16:24], rb[:, 24:32], ALU.mult))

                    @blk.tensor
                    def _(t, tt=tt):
                        t.wait_ge(sV.h, r_soft[tt])
                        if tt >= 1:
                            t.wait_ge(sS.h, tr_ev[tt - 1])
                        tr_cnt[tt] = sT.inc(t.matmul(
                            ptr2[:, 0:128],
                            combb[:, (tt % 2) * 128:(tt % 2) * 128 + 128],
                            s_id[:], is_transpose=True,
                            start=True, stop=True))

                    @blk.scalar
                    def _(s, tt=tt):
                        s.wait_ge(sT.h, tr_cnt[tt])
                        srcp = ptr2[0:8, 0:128]
                        s.copy(combT[0:8, tt * 128:(tt + 1) * 128], srcp)
                        tr_ev[tt] = sS.inc(s.mul(
                            combT05[0:8, tt * 128:(tt + 1) * 128], srcp, 0.5))
                router_done_s = tr_ev[NTT - 1]
                if "rbt" in taps:
                    @blk.gpsimd
                    def _(g):
                        g.wait_ge(sV.h, r_soft[NTT - 1])
                        dG.rotate()
                        dG.issue(g.dma_start(out=taps["rbt"], in_=rb[:]))
                if "combb" in taps:
                    @blk.gpsimd
                    def _(g):
                        g.wait_ge(sV.h, r_soft[NTT - 1])
                        dG.rotate()
                        dG.issue(g.dma_start(out=taps["combb"], in_=combb[:]))

                if "combt" in taps:
                    with nc.sbuf_tensor("ctf", [9, T], F32) as ctf:
                        @blk.vector
                        def _(v):
                            v.wait_ge(sS.h, router_done_s)
                            sV.inc(v.tensor_copy(ctf[0:9, :], combT05[:]))
                        tapc = sV.v

                        @blk.gpsimd
                        def _(g):
                            g.wait_ge(sV.h, tapc)
                            dG.rotate()
                            dG.issue(g.dma_start(out=taps["combt"], in_=ctf[:]))

                w13ld = [None, None]
                w13cast = [0, 0]
                crep_ev = 0
                hs_done = 0
                hs_done2 = [0, 0]
                h1mm2 = [0, 0]
                h3mm2 = [0, 0]
                hh_done = [0, 0]
                hh_dma = [None, None]
                h1mm = 0
                h3mm = 0

                for br in range(9):
                    for half in range(2):
                        @blk.sync
                        def _(q, br=br, half=half):
                            if br > 0:
                                q.wait_ge(sV.h, w13cast[half])
                            dS.rotate()
                            wsrc = w1I if half == 0 else w3I
                            ho = half * 8 * 512
                            for k in range(NCH):
                                w13ld[half] = dS.issue(q.dma_start(
                                    out=w13stg[:, ho + k * 512: ho + (k + 1) * 512],
                                    in_=wsrc[br * D + k * 128: br * D + (k + 1) * 128, :]))

                        @blk.vector
                        def _(v, br=br, half=half):
                            W(v, w13ld[half])
                            if br > 0:
                                v.wait_ge(sT.h, h1mm if half == 0 else h3mm)
                            dstb = w1b if half == 0 else w3b
                            ho = half * 8 * 512
                            w13cast[half] = sV.inc(v.tensor_copy(
                                dstb[:], w13stg[:, ho:ho + 8 * 512]))

                    for tw in range(NTW):
                        t0 = tw * TW

                        if br < 8:
                            @blk.tensor
                            def _(t, br=br, tw=tw, t0=t0):
                                t.wait_ge(sS.h, router_done_s)
                                t.wait_ge(sV.h, crep_ev)
                                t.matmul(pb[4][:],
                                         s_sel8[:, br * 128:(br + 1) * 128],
                                         combT[0:8, t0:t0 + TW],
                                         start=True, stop=True).then_inc(sT.h, 1)
                            sT.v += 1
                            crep_mm = sT.v

                            @blk.vector
                            def _(v, crep_mm=crep_mm):
                                v.wait_ge(sT.h, crep_mm)
                                nonlocal crep_ev
                                crep_ev = sV.inc(v.tensor_copy(cbf[:], pb[4][:]))

                        for mi in range(4):
                            mpar = mi % 2
                            pA, pB = (0, 1) if mpar == 0 else (2, 3)

                            @blk.tensor
                            def _(t, br=br, tw=tw, mi=mi, t0=t0, pA=pA, pB=pB, mpar=mpar):
                                t.wait_ge(sV.h, max(w13cast))
                                t.wait_ge(sS.h, hs_done2[mpar])
                                t.wait_ge(sV.h, hh_done[mpar])
                                for k in range(NCH):
                                    mmi = t.matmul(
                                        pb[pA][:],
                                        w1b[:, k * 512 + mi * 128: k * 512 + (mi + 1) * 128],
                                        xnT2[:, k * T + t0: k * T + t0 + TW],
                                        start=(k == 0), stop=(k == NCH - 1))
                                nonlocal h1mm
                                h1mm = sT.inc(mmi)
                                h1mm2[mpar] = h1mm
                                for k in range(NCH):
                                    mmi = t.matmul(
                                        pb[pB][:],
                                        w3b[:, k * 512 + mi * 128: k * 512 + (mi + 1) * 128],
                                        xnT2[:, k * T + t0: k * T + t0 + TW],
                                        start=(k == 0), stop=(k == NCH - 1))
                                nonlocal h3mm
                                h3mm = sT.inc(mmi)
                                h3mm2[mpar] = h3mm

                            @blk.scalar
                            def _(s, br=br, tw=tw, mi=mi, pA=pA, mpar=mpar):
                                s.wait_ge(sT.h, h1mm2[mpar])
                                hs_done2[mpar] = sS.inc(s.activation(
                                    hsb[:, mpar * TW:(mpar + 1) * TW], pb[pA][:],
                                    AF.Silu, bias=b1c[:, br * 4 + mi:br * 4 + mi + 1]))

                            @blk.vector
                            def _(v, br=br, tw=tw, mi=mi, pB=pB, mpar=mpar):
                                v.wait_ge(sS.h, hs_done2[mpar])
                                v.wait_ge(sT.h, h3mm2[mpar])
                                if br < 8:
                                    v.wait_ge(sV.h, crep_ev)
                                    v.tensor_tensor(hscb[:, mpar * TW:(mpar + 1) * TW],
                                                    hsb[:, mpar * TW:(mpar + 1) * TW],
                                                    cbf[:], ALU.mult)
                                    hsrc = hscb[:, mpar * TW:(mpar + 1) * TW]
                                else:
                                    hsrc = hsb[:, mpar * TW:(mpar + 1) * TW]
                                W(v, hh_dma[mpar])
                                hh_done[mpar] = sV.inc(v.scalar_tensor_tensor(
                                    hhb[:, mpar * TW:(mpar + 1) * TW], pb[pB][:],
                                    b3c[:, br * 4 + mi:br * 4 + mi + 1], hsrc,
                                    ALU.add, ALU.mult))

                            @blk.gpsimd
                            def _(g, br=br, tw=tw, mi=mi, t0=t0, mpar=mpar):
                                g.wait_ge(sV.h, hh_done[mpar])
                                if mi == 0:
                                    dG.rotate()
                                row = (br * 4 + mi) * 128
                                hh_dma[mpar] = dG.issue(g.dma_start(
                                    out=hh_d[row:row + 128, t0:t0 + TW],
                                    in_=hhb[:, mpar * TW:(mpar + 1) * TW]))

                ffna_bar = dG.barrier()
                if "hh" in taps:
                    @blk.gpsimd
                    def _(g):
                        W(g, ffna_bar)
                        dG.rotate()
                        dG.issue(g.dma_start(out=taps["hh"], in_=hh_d))

        if "ffnb" in phases:
            with contextlib.ExitStack() as fb:
                w2b = fb.enter_context(nc.sbuf_tensor("w2b", [128, 36 * D], BF16))
                w2stg = fb.enter_context(nc.sbuf_tensor("w2stg", [128, 2 * D], F32))
                b2stg = fb.enter_context(nc.sbuf_tensor("b2stg", [9, D], F32))
                b2b = fb.enter_context(nc.sbuf_tensor("b2b", [9, D], BF16))
                hh_sb = fb.enter_context(nc.sbuf_tensor("hh_sb", [128, 2 * 36 * TW], BF16))
                fsb = fb.enter_context(nc.sbuf_tensor("fsb", [128, 2 * TW], F32))

                w2ld = [None] * 36
                w2cast = [0] * 36

                ffnb_entry_bar = dG.barrier() + dS.barrier()

                @blk.sync
                def _(q):
                    W(q, ffnb_entry_bar)
                    dS.rotate()
                    dS.issue(q.dma_start(out=b2stg[:], in_=b2sI))
                b2ld = dS.barrier()

                @blk.vector
                def _(v):
                    W(v, b2ld)
                    sV.inc(v.tensor_copy(b2b[:], b2stg[:]))
                b2c = sV.v

                for j in range(36):
                    @blk.sync
                    def _(q, j=j):
                        if j >= 2:
                            q.wait_ge(sV.h, w2cast[j - 2])
                        dS.rotate()
                        w2ld[j] = dS.issue(q.dma_start(
                            out=w2stg[:, (j % 2) * D:(j % 2 + 1) * D],
                            in_=w2I[j * 128:(j + 1) * 128, :]))

                    @blk.vector
                    def _(v, j=j):
                        W(v, w2ld[j])
                        w2cast[j] = sV.inc(v.tensor_copy(
                            w2b[:, j * D:(j + 1) * D],
                            w2stg[:, (j % 2) * D:(j % 2 + 1) * D]))
                w2_ready = w2cast[35]

                hh_ld = [None, None]
                f_ev = [0, 0]
                f_dma = [None, None]
                fmm = [0, 0]

                for tw in range(NTW):
                    t0 = tw * TW
                    twp = tw % 2

                    @blk.sync
                    def _(q, tw=tw, t0=t0, twp=twp):
                        if tw == 0:
                            W(q, ffna_bar)
                        if tw >= 2:
                            q.wait_ge(sT.h, fmm[twp])
                        dS.rotate()
                        for j in range(36):
                            hh_ld[twp] = dS.issue(q.dma_start(
                                out=hh_sb[:, (twp * 36 + j) * TW:(twp * 36 + j + 1) * TW],
                                in_=hh_d[j * 128:(j + 1) * 128, t0:t0 + TW]))

                    for m in range(8):
                        par = m % 2

                        @blk.tensor
                        def _(t, tw=tw, m=m, par=par, twp=twp, t0=t0):
                            t.wait_ge(sV.h, max(w2_ready, b2c))
                            t.wait_ge(sS.h, router_done_s)
                            W(t, hh_ld[twp])
                            if tw * 8 + m >= 2:
                                t.wait_ge(sV.h, f_ev[par])
                            t.matmul(pb[par][:],
                                     b2b[0:9, m * 128:(m + 1) * 128],
                                     combT05[0:9, t0:t0 + TW],
                                     start=True, stop=False)
                            for j in range(36):
                                mmi = t.matmul(
                                    pb[par][:],
                                    w2b[:, j * D + m * 128: j * D + (m + 1) * 128],
                                    hh_sb[:, (twp * 36 + j) * TW:(twp * 36 + j + 1) * TW],
                                    start=False, stop=(j == 35))
                            fmm[twp] = sT.inc(mmi)

                        @blk.vector
                        def _(v, tw=tw, m=m, par=par, twp=twp):
                            v.wait_ge(sT.h, fmm[twp])
                            W(v, f_dma[par])
                            f_ev[par] = sV.inc(v.tensor_copy(
                                fsb[:, par * TW:(par + 1) * TW], pb[par][:]))

                        @blk.gpsimd
                        def _(g, tw=tw, m=m, par=par, t0=t0):
                            g.wait_ge(sV.h, f_ev[par])
                            if m == 0:
                                dG.rotate()
                            f_dma[par] = dG.issue(g.dma_start(
                                out=cc_sg_i.ap()[D + m * 128:D + (m + 1) * 128, t0:t0 + TW],
                                in_=fsb[:, par * TW:(par + 1) * TW]))

                ffnb_bar = dG.barrier()

                @blk.gpsimd
                def _(g):
                    W(g, ffnb_bar)
                    sC.inc(g.collective_compute(
                        "AllReduce", ALU.add, replica_groups=GROUPS,
                        ins=[cc_sg_i.ap().opt()], outs=[cc_sg_o.ap().opt()]))
                cc_sg_done = sC.v

                if "sg" in taps:
                    @blk.gpsimd
                    def _(g):
                        g.wait_ge(sC.h, cc_sg_done)
                        dG.rotate()
                        dG.issue(g.dma_start(out=taps["sg"], in_=cc_sg_o.ap()))

        if "final" in phases:
            fin_entry_bar = dG.barrier() + dS.barrier()
            with contextlib.ExitStack() as fi:
                fs = fi.enter_context(nc.sbuf_tensor("fs", [128, 2 * TW], F32))
                ff = fi.enter_context(nc.sbuf_tensor("ff", [128, 2 * TW], F32))
                fx = fi.enter_context(nc.sbuf_tensor("fx", [128, 2 * TW], F32))
                fg = fi.enter_context(nc.sbuf_tensor("fg", [128, 2 * TW], BF16))
                fo = fi.enter_context(nc.sbuf_tensor("fo", [128, 2 * TW], F32))

                fin_ld = None
                fin_g = 0
                fin_o = 0
                fin_dma = [None, None]
                for dc in range(NCH):
                    for tw in range(NTW):
                        t0 = tw * TW
                        par = (dc * NTW + tw) % 2

                        @blk.sync
                        def _(q, dc=dc, tw=tw, t0=t0, par=par):
                            q.wait_ge(sC.h, cc_sg_done)
                            if dc == 0 and tw == 0:
                                W(q, fin_entry_bar)
                            if dc * NTW + tw >= 2:
                                q.wait_ge(sV.h, fin_o)
                            dS.rotate()
                            r0 = dc * 128
                            dS.issue(q.dma_start(out=fs[:, par * TW:(par + 1) * TW],
                                                 in_=cc_sg_o.ap()[r0:r0 + 128, t0:t0 + TW]))
                            dS.issue(q.dma_start(out=ff[:, par * TW:(par + 1) * TW],
                                                 in_=cc_sg_o.ap()[D + r0:D + r0 + 128, t0:t0 + TW]))
                            nonlocal fin_ld
                            fin_ld = dS.issue(q.dma_start(
                                out=fx[:, par * TW:(par + 1) * TW],
                                in_=x_fm[r0:r0 + 128, t0:t0 + TW]))

                        @blk.scalar
                        def _(s, dc=dc, tw=tw, par=par):
                            W(s, fin_ld)
                            nonlocal fin_g
                            fin_g = sS.inc(s.activation(
                                fg[:, par * TW:(par + 1) * TW],
                                ff[:, par * TW:(par + 1) * TW], AF.Silu))

                        @blk.vector
                        def _(v, dc=dc, tw=tw, par=par):
                            v.wait_ge(sS.h, fin_g)
                            W(v, fin_dma[par])
                            v.tensor_tensor(fo[:, par * TW:(par + 1) * TW],
                                            fs[:, par * TW:(par + 1) * TW],
                                            fg[:, par * TW:(par + 1) * TW], ALU.mult)
                            nonlocal fin_o
                            fin_o = sV.inc(v.tensor_tensor(
                                fo[:, par * TW:(par + 1) * TW],
                                fo[:, par * TW:(par + 1) * TW],
                                fx[:, par * TW:(par + 1) * TW], ALU.add))

                        @blk.gpsimd
                        def _(g, dc=dc, tw=tw, t0=t0, par=par):
                            g.wait_ge(sV.h, fin_o)
                            if tw == 0:
                                dG.rotate()
                            r0 = dc * 128
                            fin_dma[par] = dG.issue(g.dma_start(
                                out=out_fm[r0:r0 + 128, t0:t0 + TW],
                                in_=fo[:, par * TW:(par + 1) * TW]))

        fin_bar_g = dG.barrier()
        fin_bar_s = dS.barrier()

        @blk.sync
        def _(q):
            W(q, fin_bar_g)
            W(q, fin_bar_s)

        @blk.gpsimd
        def _(g):
            W(g, fin_bar_g)

    return nc


def host_inputs(inp, core):
    b, h = core // 2, core % 2
    dsl = slice(h * 1024, (h + 1) * 1024)
    isl = slice(h * 512, (h + 1) * 512)
    x = np.asarray(inp["L0_attn_x"], np.float32)[b]
    in_proj = np.asarray(inp["L0_attn_in_proj_w"], np.float32)
    conv_w = np.asarray(inp["L0_attn_conv_w"], np.float32)[:, 0, :]
    conv_b = np.asarray(inp["L0_attn_conv_b"], np.float32)
    x_proj = np.asarray(inp["L0_attn_x_proj_w"], np.float32)
    dtwf = np.asarray(inp["L0_attn_dt_proj_w"], np.float32)[:, 0]
    dtbf = np.asarray(inp["L0_attn_dt_proj_b"], np.float32)
    logA = np.asarray(inp["L0_attn_log_A"], np.float32)
    Dpf = np.asarray(inp["L0_attn_D_param"], np.float32)
    nwf = np.asarray(inp["L0_attn_norm_w"], np.float32)
    nbf = np.asarray(inp["L0_attn_norm_b"], np.float32)
    out_proj = np.asarray(inp["L0_attn_out_proj_w"], np.float32)
    gate_w = np.asarray(inp["L0_ffn_gate_w"], np.float32)
    w1 = np.asarray(inp["L0_ffn_w1"], np.float32)
    b1 = np.asarray(inp["L0_ffn_b1"], np.float32)
    w2 = np.asarray(inp["L0_ffn_w2"], np.float32)
    b2 = np.asarray(inp["L0_ffn_b2"], np.float32)
    w3 = np.asarray(inp["L0_ffn_w3"], np.float32)
    b3 = np.asarray(inp["L0_ffn_b3"], np.float32)
    sw1 = np.asarray(inp["L0_ffn_sw1"], np.float32)
    sb1 = np.asarray(inp["L0_ffn_sb1"], np.float32)
    sw2 = np.asarray(inp["L0_ffn_sw2"], np.float32)
    sb2 = np.asarray(inp["L0_ffn_sb2"], np.float32)
    sw3 = np.asarray(inp["L0_ffn_sw3"], np.float32)
    sb3 = np.asarray(inp["L0_ffn_sb3"], np.float32)

    C = np.ascontiguousarray

    def tile8(vec):
        return C(vec.reshape(8, 128).T)

    ipr = np.concatenate([in_proj[dsl], in_proj[2048 + h * 1024: 2048 + (h + 1) * 1024]], 0)
    w1h = np.concatenate([w1[:, :, isl].reshape(8 * 1024, 512), sw1[:, isl]], 0)
    w3h = np.concatenate([w3[:, :, isl].reshape(8 * 1024, 512), sw3[:, isl]], 0)
    w2h = np.concatenate([w2[:, isl, :].reshape(8 * 512, 1024), sw2[isl, :]], 0)
    b1h = np.concatenate([b1[:, isl], sb1[isl][None, :]], 0)
    b3h = np.concatenate([b3[:, isl], sb3[isl][None, :]], 0)
    b2s = np.concatenate([b2, sb2[None, :]], 0)

    sel32 = np.zeros((32, 32 * 128), np.float32)
    for n in range(32):
        sel32[n, n * 128:(n + 1) * 128] = 1.0
    sel8 = np.zeros((8, 8 * 128), np.float32)
    for e in range(8):
        sel8[e, e * 128:(e + 1) * 128] = 1.0

    bf = lambda a: C(a).astype(ml_dtypes.bfloat16)

    return {
        "x_tok": C(x),
        "x_fm": C(x.T),
        "ipT": C(ipr.T),
        "cw": C(conv_w[dsl].reshape(8, 128, 4).transpose(1, 0, 2).reshape(128, 32)),
        "cb": tile8(conv_b[dsl]),
        "xpT": C(x_proj[:, dsl].T.reshape(8, 128, 33).transpose(1, 0, 2).reshape(128, 8 * 33)),
        "dtw": C(dtwf[dsl][None, :]),
        "dtb": tile8(dtbf[dsl]),
        "logA": C(logA[dsl].reshape(8, 128, 16).transpose(1, 0, 2).reshape(128, 128)),
        "Dp": tile8(Dpf[dsl]),
        "nw": tile8(nwf[dsl]),
        "nb": tile8(nbf[dsl]),
        "opT": C(out_proj[:, dsl].T),
        "gwT": C(gate_w.T.reshape(8, 128, 8).transpose(1, 0, 2).reshape(128, 64)),
        "w1": C(w1h), "w3": C(w3h), "w2": C(w2h),
        "b1t": C(b1h.reshape(9, 4, 128).transpose(2, 0, 1).reshape(128, 36)),
        "b3t": C(b3h.reshape(9, 4, 128).transpose(2, 0, 1).reshape(128, 36)),
        "b2s": C(b2s),
        "ident_bf": bf(np.eye(128, dtype=np.float32)),
        "sel32": bf(sel32),
        "sel8": bf(sel8),
        "ones_bf": bf(np.ones((128, 1), np.float32)),
        "ones_f1": np.ones((1, 128), np.float32),
        "ones_r1b": bf(np.ones((1, 128), np.float32)),
        "epsc": np.full((128, 1), 1e-5, np.float32),
    }


def kernel(**inputs):
    nc = build_program()
    in_maps = [host_inputs(inputs, c) for c in range(8)]
    res = run_bass_kernel_spmd(nc, in_maps, core_ids=list(range(8)))
    out = np.stack([np.asarray(res.results[2 * b]["out_fm"]).T for b in range(NB)])
    return np.ascontiguousarray(out.astype(np.float32))


# version 09
# speedup vs baseline: 1.0905x; 1.0905x over previous
import contextlib
import numpy as np
import ml_dtypes
import concourse.bass as bass
import concourse.mybir as mybir
from concourse.bass_utils import run_bass_kernel_spmd

F32 = mybir.dt.float32
BF16 = mybir.dt.bfloat16
AF = mybir.ActivationFunctionType
ALU = mybir.AluOpType
AX = mybir.AxisListType

T = 4096
D = 1024
DIH = 1024
NCH = 8
NS = 16
TW = 512
NTW = T // TW
NTT = T // 128
NB = 4
EPS = 1e-5
GROUPS = [[0, 1], [2, 3], [4, 5], [6, 7]]

DEBUG_TAPS = ()
STOP_AFTER = None


class Sem:

    def __init__(self, es, nc, name):
        self.h = es.enter_context(nc.semaphore(name))
        self.v = 0

    def inc(self, inst, k=1):
        inst.then_inc(self.h, k)
        self.v += k
        return self.v


class RotSem:

    def __init__(self, es, nc, name, n=6):
        self.sems = [es.enter_context(nc.semaphore(f"{name}{i}")) for i in range(n)]
        self.vals = [0] * n
        self.i = 0
        self.n = n

    def rotate(self):
        self.i = (self.i + 1) % self.n

    def issue(self, inst, k=16):
        j = self.i
        inst.then_inc(self.sems[j], k)
        self.vals[j] += k
        return (self.sems[j], self.vals[j])

    def barrier(self):
        return [(self.sems[j], self.vals[j]) for j in range(self.n)]


def W(eng, *toks):
    for t in toks:
        if t is None:
            continue
        if isinstance(t, list):
            for tt in t:
                eng.wait_ge(tt[0], tt[1])
        else:
            eng.wait_ge(t[0], t[1])


def build_program():
    all_phases = ["p0", "mpre", "mscan", "ln2", "ffna", "ffnb", "final"]
    phases = all_phases if STOP_AFTER is None else all_phases[: all_phases.index(STOP_AFTER) + 1]

    nc = bass.Bass("TRN2", target_bir_lowering=False)

    def inp(name, shape, dtype=F32):
        return nc.dram_tensor(name, list(shape), dtype, kind="ExternalInput").ap()

    x_tok = inp("x_tok", [T, D])
    x_fm = inp("x_fm", [D, T])
    ipT = inp("ipT", [D, 2 * DIH])
    cw = inp("cw", [128, NCH * 4])
    cb = inp("cb", [128, NCH])
    xpT = inp("xpT", [128, NCH * 33])
    dtw = inp("dtw", [1, DIH])
    dtb = inp("dtb", [128, NCH])
    logA = inp("logA", [128, NCH * NS])
    DpI = inp("Dp", [128, NCH])
    nwI = inp("nw", [128, NCH])
    nbI = inp("nb", [128, NCH])
    opT = inp("opT", [DIH, D])
    gwT = inp("gwT", [128, NCH * 8])
    w1I = inp("w1", [9 * D, 512])
    w3I = inp("w3", [9 * D, 512])
    w2I = inp("w2", [9 * 512, D])
    b1tI = inp("b1t", [128, 9 * 4])
    b3tI = inp("b3t", [128, 9 * 4])
    b2sI = inp("b2s", [9, D])
    identI = inp("ident_bf", [128, 128], BF16)
    sel32I = inp("sel32", [32, 32 * 128], BF16)
    sel8I = inp("sel8", [8, 8 * 128], BF16)
    onesbI = inp("ones_bf", [128, 1], BF16)
    ones1I = inp("ones_f1", [1, 128])
    ones1bI = inp("ones_r1b", [1, 128], BF16)
    epsI = inp("epsc", [128, 1])

    out_fm = nc.dram_tensor("out_fm", [D, T], F32, kind="ExternalOutput").ap()

    u_d = nc.dram_tensor("u_d", [DIH, T], F32).ap()
    sz_d = nc.dram_tensor("sz_d", [DIH, T], F32).ap()
    y_d = nc.dram_tensor("y_d", [DIH, T], F32).ap()
    xnt_d = nc.dram_tensor("xnt_d", [D, T], BF16).ap()
    hh_d = nc.dram_tensor("hh_d", [36 * 128, T], BF16).ap()
    cc_ssm_i = nc.dram_tensor("cc_ssm_i", [33, T], F32)
    cc_ssm_o = nc.dram_tensor("cc_ssm_o", [33, T], F32)
    cc_st_i = nc.dram_tensor("cc_st_i", [2, T], F32)
    cc_st_o = nc.dram_tensor("cc_st_o", [2, T], F32)
    cc_sg_i = nc.dram_tensor("cc_sg_i", [2 * D, T], F32)
    cc_sg_o = nc.dram_tensor("cc_sg_o", [2 * D, T], F32)

    taps = {}
    tapshape = {"xnt": ([D, T], BF16), "u": ([DIH, T], F32), "sz": ([DIH, T], F32),
                "ssm": ([33, T], F32), "y": ([DIH, T], F32), "stats": ([2, T], F32),
                "sg": ([2 * D, T], F32), "hh": ([36 * 128, T], BF16),
                "combt": ([9, T], F32), "carr": ([128, 128], F32), "delta": ([DIH, T], F32), "du": ([DIH, T], BF16), "gated": ([DIH, T], BF16), "rbt": ([128, 32], F32), "combb": ([128, 256], BF16)}
    for tn in DEBUG_TAPS:
        shp, dt = tapshape[tn]
        taps[tn] = nc.dram_tensor("tap_" + tn, shp, dt, kind="ExternalOutput").ap()

    es = contextlib.ExitStack()
    with es:
        E = es.enter_context

        Acol = E(nc.sbuf_tensor("Acol", [128, NCH * NS], F32))
        s_cw = E(nc.sbuf_tensor("s_cw", [128, NCH * 4], F32))
        s_cb = E(nc.sbuf_tensor("s_cb", [128, NCH], F32))
        s_dtb = E(nc.sbuf_tensor("s_dtb", [128, NCH], F32))
        s_Dp = E(nc.sbuf_tensor("s_Dp", [128, NCH], F32))
        s_nw = E(nc.sbuf_tensor("s_nw", [128, NCH], F32))
        s_nb = E(nc.sbuf_tensor("s_nb", [128, NCH], F32))
        s_logA = E(nc.sbuf_tensor("s_logA", [128, NCH * NS], F32))
        s_id = E(nc.sbuf_tensor("s_id", [128, 128], BF16))
        s_sel32 = E(nc.sbuf_tensor("s_sel32", [32, 32 * 128], BF16))
        s_sel8 = E(nc.sbuf_tensor("s_sel8", [8, 8 * 128], BF16))
        s_onesb = E(nc.sbuf_tensor("s_onesb", [128, 1], BF16))
        s_ones1 = E(nc.sbuf_tensor("s_ones1", [1, 128], F32))
        s_ones1b = E(nc.sbuf_tensor("s_ones1b", [1, 128], BF16))
        s_eps = E(nc.sbuf_tensor("s_eps", [128, 1], F32))
        s_dtwb = E(nc.sbuf_tensor("s_dtwb", [1, DIH], BF16))
        s_dtwf = E(nc.sbuf_tensor("s_dtwf", [1, DIH], F32))
        combT = E(nc.sbuf_tensor("combT", [8, T], BF16))
        combT05 = E(nc.sbuf_tensor("combT05", [9, T], BF16))

        pb = [E(nc.psum_tensor(f"pb{i}", [128, TW], F32)) for i in range(5)]
        pst1 = E(nc.psum_tensor("pst1", [1, TW], F32))
        pst2 = E(nc.psum_tensor("pst2", [1, TW], F32))
        ptr2 = E(nc.psum_tensor("ptr2", [128, 1024], BF16))

        dS = RotSem(es, nc, "dS")
        dG = RotSem(es, nc, "dG")
        sV = Sem(es, nc, "sV")
        sS = Sem(es, nc, "sS")
        sT = Sem(es, nc, "sT")
        sP = Sem(es, nc, "sP")
        sC = Sem(es, nc, "sC")

        blk = E(nc.Block())

        @blk.sync
        def _(q):
            for dst, src in ((s_cw, cw), (s_cb, cb), (s_dtb, dtb), (s_Dp, DpI),
                             (s_nw, nwI), (s_nb, nbI), (s_logA, logA),
                             (s_id, identI), (s_sel32, sel32I), (s_sel8, sel8I),
                             (s_onesb, onesbI), (s_ones1, ones1I),
                             (s_ones1b, ones1bI), (s_eps, epsI),
                             (s_dtwf, dtw)):
                dS.issue(q.dma_start(out=dst[:], in_=src))
        c0 = dS.barrier()
        dS.rotate()

        @blk.scalar
        def _(s):
            W(s, c0)
            sS.inc(s.activation(Acol[:], s_logA[:], AF.Exp))
        aexp = sS.v

        @blk.vector
        def _(v):
            v.wait_ge(sS.h, aexp)
            sV.inc(v.tensor_scalar(Acol[:], Acol[:], -1.0, None, ALU.mult))
            sV.inc(v.tensor_copy(s_dtwb[:], s_dtwf[:]))
        consts_ready = sV.v

        with contextlib.ExitStack() as p0s:
            xnT = p0s.enter_context(nc.sbuf_tensor("xnT", [128, NCH * T], BF16))
            p0x = p0s.enter_context(nc.sbuf_tensor("p0x", [128, 2 * D], F32))
            p0sq = p0s.enter_context(nc.sbuf_tensor("p0sq", [128, D], F32))
            p0xn = p0s.enter_context(nc.sbuf_tensor("p0xn", [128, 2 * D], BF16))
            p0r = p0s.enter_context(nc.sbuf_tensor("p0r", [128, 16], F32))

            xload = [None] * NTT
            xncnt = [0] * NTT
            rstdcnt = [0] * NTT
            trcnt = [[0] * NCH for _ in range(NTT)]
            evaccnt = [[0] * NCH for _ in range(NTT)]

            for tt in range(NTT):
                par = tt % 2
                co = par * 8

                @blk.sync
                def _(q, tt=tt, par=par):
                    if tt >= 2:
                        q.wait_ge(sV.h, xncnt[tt - 2])
                    dS.rotate()
                    xload[tt] = dS.issue(q.dma_start(
                        out=p0x[:, par * D:(par + 1) * D],
                        in_=x_tok[tt * 128:(tt + 1) * 128, :]))

                @blk.vector
                def _(v, tt=tt, par=par, co=co):
                    xs = p0x[:, par * D:(par + 1) * D]
                    W(v, xload[tt])
                    if tt >= 2:
                        v.wait_ge(sS.h, rstdcnt[tt - 2])
                    v.reduce_sum(p0r[:, co:co + 1], xs, axis=AX.X)
                    v.tensor_tensor(p0sq[:], xs, xs, ALU.mult)
                    sV.inc(v.reduce_sum(p0r[:, co + 1:co + 2], p0sq[:], axis=AX.X))
                    v.wait_ge(sV.h, sV.v)
                    v.tensor_scalar(p0r[:, co + 2:co + 3], p0r[:, co:co + 1],
                                    1.0 / D, None, ALU.mult)
                    v.tensor_scalar(p0r[:, co + 3:co + 4], p0r[:, co + 1:co + 2],
                                    1.0 / D, None, ALU.mult)
                    sV.inc(v.scalar_tensor_tensor(
                        p0r[:, co + 4:co + 5], p0r[:, co + 2:co + 3],
                        p0r[:, co + 2:co + 3], p0r[:, co + 3:co + 4],
                        ALU.mult, ALU.subtract))
                nv = sV.v

                @blk.scalar
                def _(s, tt=tt, co=co, nv=nv):
                    s.wait_ge(sV.h, nv)
                    s.activation(p0r[:, co + 5:co + 6], p0r[:, co + 4:co + 5],
                                 AF.Ln, scale=-1.0, bias=s_eps[:])
                    rstdcnt[tt] = sS.inc(s.activation(
                        p0r[:, co + 6:co + 7], p0r[:, co + 5:co + 6],
                        AF.Exp, scale=-0.5))

                @blk.vector
                def _(v, tt=tt, par=par, co=co):
                    v.wait_ge(sS.h, rstdcnt[tt])
                    if tt >= 2:
                        v.wait_ge(sT.h, trcnt[tt - 2][NCH - 1])
                    xncnt[tt] = sV.inc(v.tensor_scalar(
                        p0xn[:, par * D:(par + 1) * D],
                        p0x[:, par * D:(par + 1) * D],
                        p0r[:, co + 2:co + 3], p0r[:, co + 6:co + 7],
                        ALU.subtract, ALU.mult))

                @blk.tensor
                def _(t, tt=tt, par=par):
                    t.wait_ge(sV.h, xncnt[tt])
                    t.wait_ge(sV.h, consts_ready)
                    if tt > 0:
                        t.wait_ge(sS.h, evaccnt[tt - 1][NCH - 1])
                    for j in range(NCH):
                        trcnt[tt][j] = sT.inc(t.matmul(
                            ptr2[:, j * 128:(j + 1) * 128],
                            p0xn[:, par * D + j * 128: par * D + (j + 1) * 128],
                            s_id[:], is_transpose=True,
                            start=(j == 0), stop=(j == NCH - 1)))

                @blk.scalar
                def _(s, tt=tt):
                    for j in range(NCH):
                        s.wait_ge(sT.h, trcnt[tt][j])
                        evaccnt[tt][j] = sS.inc(s.copy(
                            xnT[:, j * T + tt * 128: j * T + (tt + 1) * 128],
                            ptr2[:, j * 128:(j + 1) * 128]))

                @blk.gpsimd
                def _(g, tt=tt):
                    g.wait_ge(sS.h, evaccnt[tt][NCH - 1])
                    if tt % 4 == 0:
                        dG.rotate()
                    for j in range(NCH):
                        dG.issue(g.dma_start(
                            out=xnt_d[j * 128:(j + 1) * 128, tt * 128:(tt + 1) * 128],
                            in_=xnT[:, j * T + tt * 128: j * T + (tt + 1) * 128]))
            p0_done_s = evaccnt[NTT - 1][NCH - 1]
            p0_bar = dG.barrier()

            if "xnt" in taps:
                @blk.gpsimd
                def _(g):
                    W(g, p0_bar)
                    dG.rotate()
                    dG.issue(g.dma_start(out=taps["xnt"], in_=xnt_d))

            if "mpre" in phases:
                with contextlib.ExitStack() as mp:
                    ipTb = mp.enter_context(nc.sbuf_tensor("ipTb", [128, NCH * 2 * DIH], BF16))
                    ipstg = mp.enter_context(nc.sbuf_tensor("ipstg", [128, 2 * 2 * DIH], F32))
                    xpTb = mp.enter_context(nc.sbuf_tensor("xpTb", [128, NCH * 33], BF16))
                    xpstg = mp.enter_context(nc.sbuf_tensor("xpstg", [128, NCH * 33], F32))
                    xp_sb = mp.enter_context(nc.sbuf_tensor("xp_sb", [128, TW + 3], F32))
                    cvac = mp.enter_context(nc.sbuf_tensor("cvac", [128, 2 * TW], F32))
                    tails = mp.enter_context(nc.sbuf_tensor("tails", [128, NCH * 3], F32))
                    u_sb = mp.enter_context(nc.sbuf_tensor("u_sb", [128, 2 * TW], F32))
                    sz_sb = mp.enter_context(nc.sbuf_tensor("sz_sb", [128, 2 * TW], F32))
                    ubf = mp.enter_context(nc.sbuf_tensor("ubf", [128, NCH * TW], BF16))
                    ssm_sb = mp.enter_context(nc.sbuf_tensor("ssm_sb", [33, TW], F32))

                    ipld = [None] * NCH
                    ipcast = [0] * NCH

                    @blk.sync
                    def _(q):
                        dS.rotate()
                        dS.issue(q.dma_start(out=xpstg[:], in_=xpT))
                    xp_ld = dS.barrier()

                    for k in range(NCH):
                        @blk.sync
                        def _(q, k=k):
                            if k >= 2:
                                q.wait_ge(sV.h, ipcast[k - 2])
                            dS.rotate()
                            ipld[k] = dS.issue(q.dma_start(
                                out=ipstg[:, (k % 2) * 2 * DIH:(k % 2 + 1) * 2 * DIH],
                                in_=ipT[k * 128:(k + 1) * 128, :]))

                        @blk.vector
                        def _(v, k=k):
                            W(v, ipld[k])
                            ipcast[k] = sV.inc(v.tensor_copy(
                                ipTb[:, k * 2 * DIH:(k + 1) * 2 * DIH],
                                ipstg[:, (k % 2) * 2 * DIH:(k % 2 + 1) * 2 * DIH]))

                    @blk.vector
                    def _(v):
                        W(v, xp_ld)
                        sV.inc(v.tensor_copy(xpTb[:], xpstg[:]))
                        sV.inc(v.memset(tails[:], 0.0))
                    mpre_ready = sV.v

                    mm_ev = [[0] * 16 for _ in range(NTW)]
                    u_done = [[0] * NCH for _ in range(NTW)]
                    ubf_done = [[0] * NCH for _ in range(NTW)]
                    ssm_ev = [0] * NTW
                    mmcnt = [[0] * 16 for _ in range(NTW)]
                    xprojcnt = [0] * NTW

                    for tw in range(NTW):
                        t0 = tw * TW
                        for m in range(16):
                            par = m % 2

                            @blk.tensor
                            def _(t, tw=tw, m=m, par=par, t0=t0):
                                if tw == 0 and m == 0:
                                    t.wait_ge(sV.h, mpre_ready)
                                    t.wait_ge(sS.h, p0_done_s)
                                pm = (tw, m - 2) if m >= 2 else (tw - 1, 14 + m)
                                if pm[0] >= 0:
                                    cnt = mm_ev[pm[0]][pm[1]]
                                    if pm[1] < 8:
                                        t.wait_ge(sV.h, cnt)
                                    else:
                                        t.wait_ge(sS.h, cnt)
                                if m == 0:
                                    t.wait_ge(sV.h, ipcast[NCH - 1])
                                for k in range(NCH):
                                    mmi = t.matmul(
                                        pb[par][:],
                                        ipTb[:, k * 2 * DIH + m * 128: k * 2 * DIH + (m + 1) * 128],
                                        xnT[:, k * T + t0: k * T + t0 + TW],
                                        start=(k == 0), stop=(k == NCH - 1))
                                mmcnt[tw][m] = sT.inc(mmi)

                            if m < 8:
                                @blk.vector
                                def _(v, tw=tw, m=m, par=par):
                                    v.wait_ge(sT.h, mmcnt[tw][m])
                                    v.tensor_copy(xp_sb[:, 0:3], tails[:, m * 3:(m + 1) * 3])
                                    mm_ev[tw][m] = sV.inc(v.tensor_copy(
                                        xp_sb[:, 3:TW + 3], pb[par][:]))
                                    v.wait_ge(sV.h, sV.v)
                                    v.tensor_copy(tails[:, m * 3:(m + 1) * 3],
                                                  xp_sb[:, TW:TW + 3])
                                    ac = cvac[:, par * TW:(par + 1) * TW]
                                    v.tensor_scalar(ac, xp_sb[:, 0:TW],
                                                    s_cw[:, m * 4:m * 4 + 1], None, ALU.mult)
                                    for kk in range(1, 4):
                                        v.scalar_tensor_tensor(
                                            ac, xp_sb[:, kk:kk + TW],
                                            s_cw[:, m * 4 + kk:m * 4 + kk + 1],
                                            ac, ALU.mult, ALU.add)
                                    sV.inc(v.tensor_copy(xp_sb[:, 0:1], xp_sb[:, 0:1]))
                                cvdone = sV.v

                                @blk.scalar
                                def _(s, tw=tw, m=m, par=par, cvdone=cvdone):
                                    s.wait_ge(sV.h, cvdone)
                                    u_done[tw][m] = sS.inc(s.activation(
                                        u_sb[:, par * TW:(par + 1) * TW],
                                        cvac[:, par * TW:(par + 1) * TW],
                                        AF.Silu, bias=s_cb[:, m:m + 1]))

                                @blk.vector
                                def _(v, tw=tw, m=m, par=par):
                                    v.wait_ge(sS.h, u_done[tw][m])
                                    ubf_done[tw][m] = sV.inc(v.tensor_copy(
                                        ubf[:, m * TW:(m + 1) * TW],
                                        u_sb[:, par * TW:(par + 1) * TW]))

                                @blk.gpsimd
                                def _(g, tw=tw, m=m, par=par, t0=t0):
                                    g.wait_ge(sS.h, u_done[tw][m])
                                    if m == 0:
                                        dG.rotate()
                                    dG.issue(g.dma_start(
                                        out=u_d[m * 128:(m + 1) * 128, t0:t0 + TW],
                                        in_=u_sb[:, par * TW:(par + 1) * TW]))
                            else:
                                @blk.scalar
                                def _(s, tw=tw, m=m, par=par):
                                    s.wait_ge(sT.h, mmcnt[tw][m])
                                    mm_ev[tw][m] = sS.inc(s.activation(
                                        sz_sb[:, par * TW:(par + 1) * TW],
                                        pb[par][:], AF.Silu))

                                @blk.gpsimd
                                def _(g, tw=tw, m=m, par=par, t0=t0):
                                    g.wait_ge(sS.h, mm_ev[tw][m])
                                    if m == 8:
                                        dG.rotate()
                                    dG.issue(g.dma_start(
                                        out=sz_d[(m - 8) * 128:(m - 7) * 128, t0:t0 + TW],
                                        in_=sz_sb[:, par * TW:(par + 1) * TW]))

                        @blk.tensor
                        def _(t, tw=tw, t0=t0):
                            t.wait_ge(sV.h, ubf_done[tw][NCH - 1])
                            if tw >= 1:
                                t.wait_ge(sV.h, ssm_ev[tw - 1])
                            for k in range(NCH):
                                mmi = t.matmul(
                                    pb[2][0:33, :],
                                    xpTb[:, k * 33:(k + 1) * 33],
                                    ubf[:, k * TW:(k + 1) * TW],
                                    start=(k == 0), stop=(k == NCH - 1))
                            xprojcnt[tw] = sT.inc(mmi)

                        @blk.vector
                        def _(v, tw=tw):
                            v.wait_ge(sT.h, xprojcnt[tw])
                            ssm_ev[tw] = sV.inc(v.tensor_copy(ssm_sb[:], pb[2][0:33, :]))

                        @blk.gpsimd
                        def _(g, tw=tw, t0=t0):
                            g.wait_ge(sV.h, ssm_ev[tw])
                            dG.rotate()
                            dG.issue(g.dma_start(out=cc_ssm_i.ap()[:, t0:t0 + TW],
                                                 in_=ssm_sb[:]))

                    mpre_bar = dG.barrier()

                    @blk.gpsimd
                    def _(g):
                        W(g, mpre_bar)
                        sC.inc(g.collective_compute(
                            "AllReduce", ALU.add, replica_groups=GROUPS,
                            ins=[cc_ssm_i.ap().opt()], outs=[cc_ssm_o.ap().opt()]))
                    cc_ssm_done = sC.v

                    if "u" in taps or "sz" in taps or "ssm" in taps:
                        @blk.gpsimd
                        def _(g):
                            g.wait_ge(sC.h, cc_ssm_done)
                            dG.rotate()
                            if "u" in taps:
                                dG.issue(g.dma_start(out=taps["u"], in_=u_d))
                            if "sz" in taps:
                                dG.issue(g.dma_start(out=taps["sz"], in_=sz_d))
                            if "ssm" in taps:
                                dG.issue(g.dma_start(out=taps["ssm"], in_=cc_ssm_o.ap()))

        if "mscan" in phases:
            with contextlib.ExitStack() as ms:
                BCb = ms.enter_context(nc.sbuf_tensor("BCb", [32, T], BF16))
                rawb = ms.enter_context(nc.sbuf_tensor("rawb", [1, T], BF16))
                u_tw = ms.enter_context(nc.sbuf_tensor("u_tw", [128, NCH * TW], F32))
                del_tw = ms.enter_context(nc.sbuf_tensor("del_tw", [128, NCH * TW], F32))
                du_tw = ms.enter_context(nc.sbuf_tensor("du_tw", [128, NCH * TW], BF16))
                vb = ms.enter_context(nc.sbuf_tensor("vb", [128, 2 * TW], F32))
                eb = ms.enter_context(nc.sbuf_tensor("eb", [128, 2 * TW], F32))
                dAb = ms.enter_context(nc.sbuf_tensor("dAb", [128, 8 * TW], F32))
                dbub = ms.enter_context(nc.sbuf_tensor("dbub", [128, 8 * TW], BF16))
                s_bufs = ms.enter_context(nc.sbuf_tensor("s_bufs", [128, 16 * TW], F32))
                csb = ms.enter_context(nc.sbuf_tensor("csb", [128, 8 * TW], BF16))
                crep = ms.enter_context(nc.sbuf_tensor("crep", [128, 2 * TW], BF16))
                brep = ms.enter_context(nc.sbuf_tensor("brep", [128, 2 * TW], BF16))
                carr = ms.enter_context(nc.sbuf_tensor("carr", [128, 128], F32))
                y_sb = ms.enter_context(nc.sbuf_tensor("y_sb", [128, 2 * TW], F32))
                ybfb = ms.enter_context(nc.sbuf_tensor("ybfb", [128, TW], BF16))
                ysqb = ms.enter_context(nc.sbuf_tensor("ysqb", [128, TW], BF16))
                stev = ms.enter_context(nc.sbuf_tensor("stev", [1, 2 * TW], F32))

                mscan_entry_bar = dG.barrier() + dS.barrier()

                with contextlib.ExitStack() as bcs:
                    BCf = bcs.enter_context(nc.sbuf_tensor("BCf", [33, 2 * 1024], F32))
                    bch_ld = [None] * 4
                    bch_cast = [0] * 4
                    for ch in range(4):
                        @blk.sync
                        def _(q, ch=ch):
                            if ch == 0:
                                q.wait_ge(sC.h, cc_ssm_done)
                                W(q, mscan_entry_bar)
                            if ch >= 2:
                                q.wait_ge(sV.h, bch_cast[ch - 2])
                            dS.rotate()
                            co = (ch % 2) * 1024
                            dS.issue(q.dma_start(
                                out=BCf[0:32, co:co + 1024],
                                in_=cc_ssm_o.ap()[0:32, ch * 1024:(ch + 1) * 1024]))
                            bch_ld[ch] = dS.issue(q.dma_start(
                                out=BCf[32:33, co:co + 1024],
                                in_=cc_ssm_o.ap()[32:33, ch * 1024:(ch + 1) * 1024]))

                        @blk.vector
                        def _(v, ch=ch):
                            W(v, bch_ld[ch])
                            co = (ch % 2) * 1024
                            v.tensor_copy(BCb[:, ch * 1024:(ch + 1) * 1024],
                                          BCf[0:32, co:co + 1024])
                            bch_cast[ch] = sV.inc(v.tensor_copy(
                                rawb[:, ch * 1024:(ch + 1) * 1024],
                                BCf[32:33, co:co + 1024]))

                    @blk.vector
                    def _(v):
                        v.wait_ge(sV.h, bch_cast[3])
                        sV.inc(v.memset(carr[:], 0.0))
                    scan_ready = sV.v

                last_tts = [0] * NCH
                last_cs = [0] * NCH
                last_nsum = [0] * NCH
                last_dA = [0] * NCH
                tts_slot = [0] * 4
                cs_slot = [0] * 4
                cs_hist = [0] * 16
                nsum_slot = [0] * 4
                yev_slot = [0] * 4
                ydma_slot = [None, None]
                stat_guard = 0
                last_Cevac = 0
                bc_evac = 0
                b_evac = 0
                dbu_hist = [0, 0]
                u_ld = [None] * NTW
                del_done = [[0] * NCH for _ in range(NTW)]
                du_done = [[0] * NCH for _ in range(NTW)]
                yev = [0] * NCH
                stat_mm = 0
                stat_ev = [0] * NTW
                prev_del_consumed = 0
                prev_u_consumed = 0

                vcnt = [0] * NCH
                ecnt = [0] * NCH
                for tw in range(NTW):
                    t0 = tw * TW

                    @blk.sync
                    def _(q, tw=tw, t0=t0):
                        if tw == 0:
                            q.wait_ge(sC.h, cc_ssm_done)
                            q.wait_ge(sV.h, scan_ready)
                        else:
                            q.wait_ge(sV.h, prev_u_consumed)
                        dS.rotate()
                        for m in range(NCH):
                            u_ld[tw] = dS.issue(q.dma_start(
                                out=u_tw[:, m * TW:(m + 1) * TW],
                                in_=u_d[m * 128:(m + 1) * 128, t0:t0 + TW]))

                    for mc in range(NCH):
                        par = mc % 2

                        @blk.tensor
                        def _(t, tw=tw, mc=mc, t0=t0):
                            if tw == 0 and mc == 0:
                                t.wait_ge(sV.h, scan_ready)
                            t.wait_ge(sV.h, max(vcnt[(mc - 1) % NCH], bc_evac))
                            mmi = t.matmul(pb[4][:],
                                           s_dtwb[0:1, mc * 128:(mc + 1) * 128],
                                           rawb[0:1, t0:t0 + TW],
                                           start=True, stop=True)
                            del_done[tw][mc] = sT.inc(mmi)

                        @blk.vector
                        def _(v, tw=tw, mc=mc, par=par):
                            v.wait_ge(sT.h, del_done[tw][mc])
                            v.wait_ge(sS.h, ecnt[(mc - 2) % NCH])
                            vcnt[mc] = sV.inc(v.tensor_scalar(
                                vb[:, par * TW:(par + 1) * TW], pb[4][:],
                                s_dtb[:, mc:mc + 1], 20.0, ALU.add, ALU.min))

                        @blk.scalar
                        def _(s, tw=tw, mc=mc, par=par):
                            s.wait_ge(sV.h, vcnt[mc])
                            s.activation(eb[:, par * TW:(par + 1) * TW],
                                         vb[:, par * TW:(par + 1) * TW], AF.Exp)
                            ecnt[mc] = sS.inc(s.activation(
                                eb[:, par * TW:(par + 1) * TW],
                                eb[:, par * TW:(par + 1) * TW], AF.Ln, bias=1.0))

                        @blk.vector
                        def _(v, tw=tw, mc=mc, par=par):
                            v.wait_ge(sS.h, ecnt[mc])
                            if tw >= 1:
                                v.wait_ge(sS.h, prev_del_consumed)
                            v.tensor_scalar(
                                del_tw[:, mc * TW:(mc + 1) * TW],
                                eb[:, par * TW:(par + 1) * TW],
                                1e-3, 0.1, ALU.max, ALU.min)
                            W(v, u_ld[tw])
                            du_done[tw][mc] = sV.inc(v.tensor_tensor(
                                du_tw[:, mc * TW:(mc + 1) * TW],
                                del_tw[:, mc * TW:(mc + 1) * TW],
                                u_tw[:, mc * TW:(mc + 1) * TW], ALU.mult))

                    if "delta" in taps or "du" in taps:
                        @blk.gpsimd
                        def _(g, tw=tw, t0=t0):
                            g.wait_ge(sV.h, du_done[tw][NCH - 1])
                            dG.rotate()
                            for m in range(NCH):
                                if "delta" in taps:
                                    dG.issue(g.dma_start(
                                        out=taps["delta"][m * 128:(m + 1) * 128, t0:t0 + TW],
                                        in_=del_tw[:, m * TW:(m + 1) * TW]))
                                if "du" in taps:
                                    dG.issue(g.dma_start(
                                        out=taps["du"][m * 128:(m + 1) * 128, t0:t0 + TW],
                                        in_=du_tw[:, m * TW:(m + 1) * TW]))

                    for dh in range(2):
                        for n in range(NS):
                            npar = n % 2

                            @blk.tensor
                            def _(t, tw=tw, dh=dh, n=n):
                                t.wait_ge(sV.h, max(bc_evac, du_done[tw][NCH - 1]))
                                t.matmul(pb[4][:],
                                         s_sel32[:, n * 128:(n + 1) * 128],
                                         BCb[0:32, tw * TW:tw * TW + TW],
                                         start=True, stop=True).then_inc(sT.h, 1)
                            sT.v += 1
                            bbuild = sT.v

                            @blk.vector
                            def _(v, tw=tw, dh=dh, n=n, npar=npar, bbuild=bbuild):
                                v.wait_ge(sT.h, bbuild)
                                nonlocal b_evac
                                b_evac = sV.inc(v.tensor_copy(
                                    brep[:, npar * TW:(npar + 1) * TW], pb[4][:]))
                                for dci in range(4):
                                    dc = dh * 4 + dci
                                    v.tensor_tensor(
                                        dbub[:, dc * TW:(dc + 1) * TW],
                                        du_tw[:, dc * TW:(dc + 1) * TW],
                                        brep[:, npar * TW:(npar + 1) * TW], ALU.mult)
                                dbu_hist[npar] = sV.inc(v.tensor_copy(dbub[:, 0:1], dbub[:, 0:1]))
                            dbu_last = sV.v

                            @blk.tensor
                            def _(t, tw=tw, n=n):
                                t.wait_ge(sV.h, dbu_last)
                                t.matmul(pb[4][:],
                                         s_sel32[:, (16 + n) * 128:(17 + n) * 128],
                                         BCb[0:32, tw * TW:tw * TW + TW],
                                         start=True, stop=True).then_inc(sT.h, 1)
                            sT.v += 1
                            cbuild = sT.v

                            @blk.vector
                            def _(v, n=n, npar=npar, cbuild=cbuild):
                                v.wait_ge(sT.h, cbuild)
                                v.wait_ge(sP.h, sP.v)
                                nonlocal last_Cevac, bc_evac
                                last_Cevac = sV.inc(v.tensor_copy(
                                    crep[:, npar * TW:(npar + 1) * TW], pb[4][:]))
                                bc_evac = last_Cevac

                            @blk.scalar
                            def _(s, tw=tw, dh=dh, n=n):
                                for dci in range(4):
                                    dc = dh * 4 + dci
                                    s.wait_ge(sV.h, last_tts[dc])
                                    last_dA[dc] = sS.inc(s.activation(
                                        dAb[:, dc * TW:(dc + 1) * TW],
                                        del_tw[:, dc * TW:(dc + 1) * TW],
                                        AF.Exp, scale=Acol[:, dc * NS + n:dc * NS + n + 1]))

                            @blk.vector
                            def _(v, tw=tw, dh=dh, n=n, npar=npar):
                                for dci in range(4):
                                    dc = dh * 4 + dci
                                    sl = npar * 8 + dc
                                    v.wait_ge(sS.h, last_dA[dc])
                                    v.wait_ge(sP.h, cs_hist[sl])
                                    ci = n * 8 + dc
                                    init = carr[:, ci:ci + 1] if tw > 0 else 0.0
                                    sV.inc(v.tensor_tensor_scan(
                                        s_bufs[:, sl * TW:(sl + 1) * TW],
                                        dAb[:, dc * TW:(dc + 1) * TW],
                                        dbub[:, dc * TW:(dc + 1) * TW],
                                        init, ALU.mult, ALU.add))
                                v.wait_ge(sV.h, sV.v)
                                c0_ = n * 8 + dh * 4
                                svw = s_bufs.rearrange("p (c w) -> p c w", w=TW)
                                tcnt = sV.inc(v.tensor_copy(
                                    carr[:, c0_:c0_ + 4],
                                    svw[:, npar * 8 + dh * 4:npar * 8 + dh * 4 + 4, TW - 1:TW]))
                                for dci in range(4):
                                    last_tts[dh * 4 + dci] = tcnt

                            @blk.gpsimd
                            def _(g, tw=tw, dh=dh, n=n, npar=npar):
                                for dci in range(4):
                                    dc = dh * 4 + dci
                                    sl = npar * 8 + dc
                                    g.wait_ge(sV.h, max(last_tts[dc], last_Cevac))
                                    g.wait_ge(sT.h, last_nsum[dc])
                                    last_cs[dc] = sP.inc(g.tensor_tensor(
                                        csb[:, dc * TW:(dc + 1) * TW],
                                        s_bufs[:, sl * TW:(sl + 1) * TW],
                                        crep[:, npar * TW:(npar + 1) * TW], ALU.mult))
                                    cs_hist[sl] = last_cs[dc]

                            @blk.tensor
                            def _(t, tw=tw, dh=dh, n=n):
                                for dci in range(4):
                                    dc = dh * 4 + dci
                                    t.wait_ge(sP.h, last_cs[dc])
                                    if n == 0:
                                        t.wait_ge(sV.h, yev_slot[dci])
                                    last_nsum[dc] = sT.inc(t.matmul(
                                        pb[dci][:], s_id[:],
                                        csb[:, dc * TW:(dc + 1) * TW],
                                        start=(n == 0), stop=(n == NS - 1)))
                                    nsum_slot[dci] = last_nsum[dc]

                        for dci in range(4):
                            dc = dh * 4 + dci
                            ypar = dci % 2

                            @blk.vector
                            def _(v, tw=tw, dh=dh, dc=dc, dci=dci, ypar=ypar):
                                v.wait_ge(sT.h, last_nsum[dc])
                                W(v, ydma_slot[ypar])
                                v.wait_ge(sT.h, stat_guard)
                                v.scalar_tensor_tensor(
                                    y_sb[:, ypar * TW:(ypar + 1) * TW],
                                    u_tw[:, dc * TW:(dc + 1) * TW],
                                    s_Dp[:, dc:dc + 1], pb[dci][:],
                                    ALU.mult, ALU.add)
                                v.tensor_copy(ybfb[:], y_sb[:, ypar * TW:(ypar + 1) * TW])
                                yev[dc] = sV.inc(v.tensor_tensor(
                                    ysqb[:], ybfb[:], ybfb[:], ALU.mult))
                                yev_slot[dci] = yev[dc]

                            @blk.gpsimd
                            def _(g, tw=tw, dc=dc, ypar=ypar, t0=t0):
                                g.wait_ge(sV.h, yev[dc])
                                if dc % 4 == 0:
                                    dG.rotate()
                                ydma_slot[ypar] = dG.issue(g.dma_start(
                                    out=y_d[dc * 128:(dc + 1) * 128, t0:t0 + TW],
                                    in_=y_sb[:, ypar * TW:(ypar + 1) * TW]))

                            @blk.tensor
                            def _(t, tw=tw, dh=dh, dc=dc, dci=dci):
                                t.wait_ge(sV.h, yev[dc])
                                if tw > 0 and dh == 0 and dci == 0:
                                    t.wait_ge(sV.h, stat_ev[tw - 1])
                                first = (dh == 0 and dci == 0)
                                last = (dh == 1 and dci == 3)
                                t.matmul(pst1[:], s_onesb[:], ybfb[:],
                                         start=first, stop=last)
                                mmi = t.matmul(pst2[:], s_onesb[:], ysqb[:],
                                               start=first, stop=last)
                                nonlocal stat_mm, stat_guard
                                stat_mm = sT.inc(mmi)
                                stat_guard = stat_mm
                        prev_u_consumed = max(yev)

                    prev_del_consumed = max(last_dA)

                    if "carr" in taps and tw == 0:
                        @blk.gpsimd
                        def _(g):
                            g.wait_ge(sV.h, max(last_tts))
                            dG.rotate()
                            dG.issue(g.dma_start(out=taps["carr"], in_=carr[:]))

                    @blk.vector
                    def _(v, tw=tw):
                        v.wait_ge(sT.h, stat_mm)
                        v.tensor_copy(stev[:, 0:TW], pst1[:])
                        stat_ev[tw] = sV.inc(v.tensor_copy(stev[:, TW:2 * TW], pst2[:]))

                    @blk.gpsimd
                    def _(g, tw=tw, t0=t0):
                        g.wait_ge(sV.h, stat_ev[tw])
                        dG.rotate()
                        dG.issue(g.dma_start(out=cc_st_i.ap()[0:1, t0:t0 + TW],
                                             in_=stev[:, 0:TW]))
                        dG.issue(g.dma_start(out=cc_st_i.ap()[1:2, t0:t0 + TW],
                                             in_=stev[:, TW:2 * TW]))

                mscan_bar = dG.barrier()

                @blk.gpsimd
                def _(g):
                    W(g, mscan_bar)
                    sC.inc(g.collective_compute(
                        "AllReduce", ALU.add, replica_groups=GROUPS,
                        ins=[cc_st_i.ap().opt()], outs=[cc_st_o.ap().opt()]))
                cc_st_done = sC.v

                if "y" in taps or "stats" in taps:
                    @blk.gpsimd
                    def _(g):
                        g.wait_ge(sC.h, cc_st_done)
                        dG.rotate()
                        if "y" in taps:
                            dG.issue(g.dma_start(out=taps["y"], in_=y_d))
                        if "stats" in taps:
                            dG.issue(g.dma_start(out=taps["stats"], in_=cc_st_o.ap()))

        if "ln2" in phases:
            with contextlib.ExitStack() as l2:
                opTb = l2.enter_context(nc.sbuf_tensor("opTb", [128, NCH * D], BF16))
                opstg = l2.enter_context(nc.sbuf_tensor("opstg", [128, 2 * D], F32))
                st0 = l2.enter_context(nc.sbuf_tensor("st0", [1, T], F32))
                st1 = l2.enter_context(nc.sbuf_tensor("st1", [1, T], F32))
                murow = l2.enter_context(nc.sbuf_tensor("murow", [1, T], F32))
                rsrow = l2.enter_context(nc.sbuf_tensor("rsrow", [1, T], F32))
                muhi = l2.enter_context(nc.sbuf_tensor("muhi", [1, T], BF16))
                mulo = l2.enter_context(nc.sbuf_tensor("mulo", [1, T], BF16))
                rshi = l2.enter_context(nc.sbuf_tensor("rshi", [1, T], BF16))
                rslo = l2.enter_context(nc.sbuf_tensor("rslo", [1, T], BF16))
                tmpr = l2.enter_context(nc.sbuf_tensor("tmpr", [1, T], F32))
                yb2 = l2.enter_context(nc.sbuf_tensor("yb2", [128, 2 * TW], F32))
                szb2 = l2.enter_context(nc.sbuf_tensor("szb2", [128, 2 * TW], F32))
                t1b = l2.enter_context(nc.sbuf_tensor("t1b", [128, TW], F32))
                t2b = l2.enter_context(nc.sbuf_tensor("t2b", [128, TW], F32))
                gatedb = l2.enter_context(nc.sbuf_tensor("gatedb", [128, NCH * TW], BF16))
                sgsb = l2.enter_context(nc.sbuf_tensor("sgsb", [128, 2 * TW], F32))

                opld = [None] * NCH
                opcast = [0] * NCH
                ln2_entry_bar = dG.barrier() + dS.barrier()
                for k in range(NCH):
                    @blk.sync
                    def _(q, k=k):
                        if k == 0:
                            W(q, ln2_entry_bar)
                        if k >= 2:
                            q.wait_ge(sV.h, opcast[k - 2])
                        dS.rotate()
                        opld[k] = dS.issue(q.dma_start(
                            out=opstg[:, (k % 2) * D:(k % 2 + 1) * D],
                            in_=opT[k * 128:(k + 1) * 128, :]))

                    @blk.vector
                    def _(v, k=k):
                        W(v, opld[k])
                        opcast[k] = sV.inc(v.tensor_copy(
                            opTb[:, k * D:(k + 1) * D],
                            opstg[:, (k % 2) * D:(k % 2 + 1) * D]))

                @blk.sync
                def _(q):
                    q.wait_ge(sC.h, cc_st_done)
                    dS.rotate()
                    dS.issue(q.dma_start(out=st0[:], in_=cc_st_o.ap()[0:1, :]))
                    dS.issue(q.dma_start(out=st1[:], in_=cc_st_o.ap()[1:2, :]))
                st_ld = dS.barrier()

                @blk.vector
                def _(v):
                    W(v, st_ld)
                    v.tensor_scalar(murow[:], st0[:], 1.0 / 2048.0, None, ALU.mult)
                    v.tensor_scalar(st1[:], st1[:], 1.0 / 2048.0, None, ALU.mult)
                    v.tensor_tensor(rsrow[:], murow[:], murow[:], ALU.mult)
                    sV.inc(v.tensor_tensor(rsrow[:], rsrow[:], st1[:], ALU.subtract))
                negvar_done = sV.v

                @blk.scalar
                def _(s):
                    s.wait_ge(sV.h, negvar_done)
                    s.activation(rsrow[:], rsrow[:], AF.Ln, scale=-1.0,
                                 bias=s_eps[0:1, :])
                    sS.inc(s.activation(rsrow[:], rsrow[:], AF.Exp, scale=-0.5))
                rstd_done = sS.v

                @blk.vector
                def _(v):
                    v.wait_ge(sS.h, rstd_done)
                    v.tensor_copy(muhi[:], murow[:])
                    v.tensor_copy(rshi[:], rsrow[:])
                    sV.inc(v.tensor_tensor(tmpr[:], murow[:], muhi[:], ALU.subtract))
                    v.wait_ge(sV.h, sV.v)
                    v.tensor_copy(mulo[:], tmpr[:])
                    sV.inc(v.tensor_tensor(tmpr[:], rsrow[:], rshi[:], ALU.subtract))
                    v.wait_ge(sV.h, sV.v)
                    sV.inc(v.tensor_copy(rslo[:], tmpr[:]))
                hilo_done = sV.v

                g_done = [0] * NCH
                sg_ev = [0, 0]
                ln2_y_ld = [None] * NCH
                opmm = [0] * 8
                ln2_prev_g = 0

                for tw in range(NTW):
                    t0 = tw * TW

                    @blk.tensor
                    def _(t, tw=tw, t0=t0):
                        t.wait_ge(sV.h, hilo_done)
                        t.wait_ge(sV.h, opcast[NCH - 1])
                        if tw >= 1:
                            t.wait_ge(sV.h, g_done[NCH - 1])
                        t.matmul(pb[2][:], s_ones1b[:], muhi[0:1, t0:t0 + TW],
                                 start=True, stop=False)
                        t.matmul(pb[2][:], s_ones1b[:], mulo[0:1, t0:t0 + TW],
                                 start=False, stop=True)
                        t.matmul(pb[3][:], s_ones1b[:], rshi[0:1, t0:t0 + TW],
                                 start=True, stop=False)
                        t.matmul(pb[3][:], s_ones1b[:], rslo[0:1, t0:t0 + TW],
                                 start=False, stop=True).then_inc(sT.h, 1)
                    sT.v += 1
                    bcast_done = sT.v

                    for dc in range(NCH):
                        par = dc % 2

                        @blk.sync
                        def _(q, tw=tw, dc=dc, par=par, t0=t0):
                            if tw == 0 and dc == 0:
                                q.wait_ge(sC.h, cc_st_done)
                            prev = g_done[dc - 2] if dc >= 2 else ln2_prev_g
                            if tw > 0 or dc >= 2:
                                q.wait_ge(sV.h, prev)
                            if dc % 4 == 0:
                                dS.rotate()
                            dS.issue(q.dma_start(
                                out=yb2[:, par * TW:(par + 1) * TW],
                                in_=y_d[dc * 128:(dc + 1) * 128, t0:t0 + TW]))
                            ln2_y_ld[dc] = dS.issue(q.dma_start(
                                out=szb2[:, par * TW:(par + 1) * TW],
                                in_=sz_d[dc * 128:(dc + 1) * 128, t0:t0 + TW]))

                        @blk.vector
                        def _(v, tw=tw, dc=dc, par=par, bcast_done=bcast_done):
                            W(v, ln2_y_ld[dc])
                            v.wait_ge(sT.h, bcast_done)
                            v.tensor_tensor(t1b[:], yb2[:, par * TW:(par + 1) * TW],
                                            pb[2][:], ALU.subtract)
                            v.tensor_tensor(t2b[:], t1b[:], pb[3][:], ALU.mult)
                            v.tensor_scalar(t1b[:], t2b[:], s_nw[:, dc:dc + 1],
                                            s_nb[:, dc:dc + 1], ALU.mult, ALU.add)
                            g_done[dc] = sV.inc(v.tensor_tensor(
                                gatedb[:, dc * TW:(dc + 1) * TW], t1b[:],
                                szb2[:, par * TW:(par + 1) * TW], ALU.mult))
                    ln2_prev_g = g_done[NCH - 1]
                    if "gated" in taps:
                        @blk.gpsimd
                        def _(g, tw=tw, t0=t0):
                            g.wait_ge(sV.h, g_done[NCH - 1])
                            dG.rotate()
                            for m in range(NCH):
                                dG.issue(g.dma_start(
                                    out=taps["gated"][m * 128:(m + 1) * 128, t0:t0 + TW],
                                    in_=gatedb[:, m * TW:(m + 1) * TW]))

                    for m in range(8):
                        par = m % 2

                        @blk.tensor
                        def _(t, tw=tw, m=m, par=par):
                            t.wait_ge(sV.h, g_done[NCH - 1])
                            if tw * 8 + m >= 2:
                                t.wait_ge(sV.h, sg_ev[par])
                            for k in range(NCH):
                                mmi = t.matmul(
                                    pb[par][:],
                                    opTb[:, k * D + m * 128: k * D + (m + 1) * 128],
                                    gatedb[:, k * TW:(k + 1) * TW],
                                    start=(k == 0), stop=(k == NCH - 1))
                            opmm[m] = sT.inc(mmi)

                        @blk.vector
                        def _(v, tw=tw, m=m, par=par):
                            v.wait_ge(sT.h, opmm[m])
                            sg_ev[par] = sV.inc(v.tensor_copy(
                                sgsb[:, par * TW:(par + 1) * TW], pb[par][:]))

                        @blk.gpsimd
                        def _(g, tw=tw, m=m, par=par, t0=t0):
                            g.wait_ge(sV.h, sg_ev[par])
                            if m == 0:
                                dG.rotate()
                            dG.issue(g.dma_start(
                                out=cc_sg_i.ap()[m * 128:(m + 1) * 128, t0:t0 + TW],
                                in_=sgsb[:, par * TW:(par + 1) * TW]))

        ln2_bar = dG.barrier() + dS.barrier()

        if "ffna" in phases:
            with contextlib.ExitStack() as fa:
                xnT2 = fa.enter_context(nc.sbuf_tensor("xnT2", [128, NCH * T], BF16))
                gwTb = fa.enter_context(nc.sbuf_tensor("gwTb", [128, NCH * 8], BF16))
                gwstg = fa.enter_context(nc.sbuf_tensor("gwstg", [128, NCH * 8], F32))
                rb = fa.enter_context(nc.sbuf_tensor("rb", [128, 32], F32))
                combb = fa.enter_context(nc.sbuf_tensor("combb", [128, 256], BF16))
                w13stg = fa.enter_context(nc.sbuf_tensor("w13stg", [128, 2 * 8 * 512], F32))
                w1b = fa.enter_context(nc.sbuf_tensor("w1b", [128, NCH * 512], BF16))
                w3b = fa.enter_context(nc.sbuf_tensor("w3b", [128, NCH * 512], BF16))
                b1c = fa.enter_context(nc.sbuf_tensor("b1c", [128, 9 * 4], F32))
                b3c = fa.enter_context(nc.sbuf_tensor("b3c", [128, 9 * 4], F32))
                cbf = fa.enter_context(nc.sbuf_tensor("cbf", [128, TW], BF16))
                hsb = fa.enter_context(nc.sbuf_tensor("hsb", [128, 2 * TW], BF16))
                hscb = fa.enter_context(nc.sbuf_tensor("hscb", [128, 2 * TW], BF16))
                hhb = fa.enter_context(nc.sbuf_tensor("hhb", [128, 2 * TW], BF16))

                @blk.sync
                def _(q):
                    W(q, p0_bar)
                    W(q, ln2_bar)
                    dS.rotate()
                    dS.issue(q.dma_start(out=gwstg[:], in_=gwT))
                    dS.issue(q.dma_start(out=b1c[:], in_=b1tI))
                    dS.issue(q.dma_start(out=b3c[:], in_=b3tI))
                    for j in range(NCH):
                        dS.issue(q.dma_start(
                            out=xnT2[:, j * T:(j + 1) * T],
                            in_=xnt_d[j * 128:(j + 1) * 128, :]))
                ffna_ld = dS.barrier()

                @blk.vector
                def _(v):
                    W(v, ffna_ld)
                    sV.inc(v.tensor_copy(gwTb[:], gwstg[:]))
                    sV.inc(v.memset(combT05[:], 0.5))
                router_ready = sV.v

                r_mm = [0] * NTT
                r_soft = [0] * NTT
                tr_cnt = [0] * NTT
                tr_ev = [0] * NTT
                for tt in range(NTT):
                    @blk.tensor
                    def _(t, tt=tt):
                        t.wait_ge(sV.h, router_ready)
                        if tt >= 2:
                            t.wait_ge(sV.h, r_soft[tt - 2])
                        for k in range(NCH):
                            mmi = t.matmul(
                                pb[4][:, 0:8],
                                xnT2[:, k * T + tt * 128: k * T + (tt + 1) * 128],
                                gwTb[:, k * 8:(k + 1) * 8],
                                start=(k == 0), stop=(k == NCH - 1))
                        r_mm[tt] = sT.inc(mmi)

                    @blk.vector
                    def _(v, tt=tt):
                        v.wait_ge(sT.h, r_mm[tt])
                        v.tensor_copy(rb[:, 0:8], pb[4][:, 0:8])
                        sV.inc(v.reduce_max(rb[:, 8:9], rb[:, 0:8], axis=AX.X))
                        v.wait_ge(sV.h, sV.v)
                        sV.inc(v.tensor_scalar(rb[:, 9:10], rb[:, 8:9], -1.0,
                                               None, ALU.mult))
                    sm1 = sV.v

                    @blk.scalar
                    def _(s, tt=tt, sm1=sm1):
                        s.wait_ge(sV.h, sm1)
                        sS.inc(s.activation(rb[:, 16:24], rb[:, 0:8], AF.Exp,
                                            bias=rb[:, 9:10]))
                    sm2 = sS.v

                    @blk.vector
                    def _(v, tt=tt, sm2=sm2):
                        v.wait_ge(sS.h, sm2)
                        sV.inc(v.reduce_sum(rb[:, 10:11], rb[:, 16:24], axis=AX.X))
                        v.wait_ge(sV.h, sV.v)
                        sV.inc(v.reciprocal(rb[:, 11:12], rb[:, 10:11]))
                        v.wait_ge(sV.h, sV.v)
                        sV.inc(v.tensor_scalar(rb[:, 16:24], rb[:, 16:24],
                                               rb[:, 11:12], None, ALU.mult))
                        v.wait_ge(sV.h, sV.v)
                        sV.inc(v.reduce_max(rb[:, 8:9], rb[:, 16:24], axis=AX.X))
                        v.wait_ge(sV.h, sV.v)
                        sV.inc(v.tensor_scalar(rb[:, 24:32], rb[:, 16:24],
                                               rb[:, 8:9], None, ALU.is_equal))
                        v.wait_ge(sV.h, sV.v)
                        sV.inc(v.scalar_tensor_tensor(rb[:, 0:8], rb[:, 24:32], -1e30,
                                                      rb[:, 16:24], ALU.mult, ALU.add))
                        v.wait_ge(sV.h, sV.v)
                        sV.inc(v.reduce_max(rb[:, 9:10], rb[:, 0:8], axis=AX.X))
                        v.wait_ge(sV.h, sV.v)
                        sV.inc(v.tensor_scalar(rb[:, 0:8], rb[:, 0:8], rb[:, 9:10],
                                               None, ALU.is_equal))
                        v.wait_ge(sV.h, sV.v)
                        sV.inc(v.tensor_tensor(rb[:, 24:32], rb[:, 24:32], rb[:, 0:8],
                                               ALU.add))
                        v.wait_ge(sV.h, sV.v)
                        par = tt % 2
                        r_soft[tt] = sV.inc(v.tensor_tensor(
                            combb[:, par * 128:par * 128 + 8],
                            rb[:, 16:24], rb[:, 24:32], ALU.mult))

                    @blk.tensor
                    def _(t, tt=tt):
                        t.wait_ge(sV.h, r_soft[tt])
                        if tt >= 1:
                            t.wait_ge(sS.h, tr_ev[tt - 1])
                        tr_cnt[tt] = sT.inc(t.matmul(
                            ptr2[:, 0:128],
                            combb[:, (tt % 2) * 128:(tt % 2) * 128 + 128],
                            s_id[:], is_transpose=True,
                            start=True, stop=True))

                    @blk.scalar
                    def _(s, tt=tt):
                        s.wait_ge(sT.h, tr_cnt[tt])
                        srcp = ptr2[0:8, 0:128]
                        s.copy(combT[0:8, tt * 128:(tt + 1) * 128], srcp)
                        tr_ev[tt] = sS.inc(s.mul(
                            combT05[0:8, tt * 128:(tt + 1) * 128], srcp, 0.5))
                router_done_s = tr_ev[NTT - 1]
                if "rbt" in taps:
                    @blk.gpsimd
                    def _(g):
                        g.wait_ge(sV.h, r_soft[NTT - 1])
                        dG.rotate()
                        dG.issue(g.dma_start(out=taps["rbt"], in_=rb[:]))
                if "combb" in taps:
                    @blk.gpsimd
                    def _(g):
                        g.wait_ge(sV.h, r_soft[NTT - 1])
                        dG.rotate()
                        dG.issue(g.dma_start(out=taps["combb"], in_=combb[:]))

                if "combt" in taps:
                    with nc.sbuf_tensor("ctf", [9, T], F32) as ctf:
                        @blk.vector
                        def _(v):
                            v.wait_ge(sS.h, router_done_s)
                            sV.inc(v.tensor_copy(ctf[0:9, :], combT05[:]))
                        tapc = sV.v

                        @blk.gpsimd
                        def _(g):
                            g.wait_ge(sV.h, tapc)
                            dG.rotate()
                            dG.issue(g.dma_start(out=taps["combt"], in_=ctf[:]))

                w13ld = [None, None]
                w13cast = [0, 0]
                crep_ev = 0
                hs_done = 0
                hs_done2 = [0, 0]
                h1mm2 = [0, 0]
                h3mm2 = [0, 0]
                hh_done = [0, 0]
                hh_dma = [None, None]
                h1mm = 0
                h3mm = 0

                for br in range(9):
                    for half in range(2):
                        @blk.sync
                        def _(q, br=br, half=half):
                            if br > 0:
                                q.wait_ge(sV.h, w13cast[half])
                            dS.rotate()
                            wsrc = w1I if half == 0 else w3I
                            ho = half * 8 * 512
                            for k in range(NCH):
                                w13ld[half] = dS.issue(q.dma_start(
                                    out=w13stg[:, ho + k * 512: ho + (k + 1) * 512],
                                    in_=wsrc[br * D + k * 128: br * D + (k + 1) * 128, :]))

                        @blk.vector
                        def _(v, br=br, half=half):
                            W(v, w13ld[half])
                            if br > 0:
                                v.wait_ge(sT.h, h1mm if half == 0 else h3mm)
                            dstb = w1b if half == 0 else w3b
                            ho = half * 8 * 512
                            w13cast[half] = sV.inc(v.tensor_copy(
                                dstb[:], w13stg[:, ho:ho + 8 * 512]))

                    for tw in range(NTW):
                        t0 = tw * TW

                        if br < 8:
                            @blk.tensor
                            def _(t, br=br, tw=tw, t0=t0):
                                t.wait_ge(sS.h, router_done_s)
                                t.wait_ge(sV.h, crep_ev)
                                t.matmul(pb[4][:],
                                         s_sel8[:, br * 128:(br + 1) * 128],
                                         combT[0:8, t0:t0 + TW],
                                         start=True, stop=True).then_inc(sT.h, 1)
                            sT.v += 1
                            crep_mm = sT.v

                            @blk.vector
                            def _(v, crep_mm=crep_mm):
                                v.wait_ge(sT.h, crep_mm)
                                nonlocal crep_ev
                                crep_ev = sV.inc(v.tensor_copy(cbf[:], pb[4][:]))

                        for mi in range(4):
                            mpar = mi % 2
                            pA, pB = (0, 1) if mpar == 0 else (2, 3)

                            @blk.tensor
                            def _(t, br=br, tw=tw, mi=mi, t0=t0, pA=pA, pB=pB, mpar=mpar):
                                t.wait_ge(sV.h, max(w13cast))
                                t.wait_ge(sS.h, hs_done2[mpar])
                                t.wait_ge(sV.h, hh_done[mpar])
                                for k in range(NCH):
                                    mmi = t.matmul(
                                        pb[pA][:],
                                        w1b[:, k * 512 + mi * 128: k * 512 + (mi + 1) * 128],
                                        xnT2[:, k * T + t0: k * T + t0 + TW],
                                        start=(k == 0), stop=(k == NCH - 1))
                                nonlocal h1mm
                                h1mm = sT.inc(mmi)
                                h1mm2[mpar] = h1mm
                                for k in range(NCH):
                                    mmi = t.matmul(
                                        pb[pB][:],
                                        w3b[:, k * 512 + mi * 128: k * 512 + (mi + 1) * 128],
                                        xnT2[:, k * T + t0: k * T + t0 + TW],
                                        start=(k == 0), stop=(k == NCH - 1))
                                nonlocal h3mm
                                h3mm = sT.inc(mmi)
                                h3mm2[mpar] = h3mm

                            @blk.scalar
                            def _(s, br=br, tw=tw, mi=mi, pA=pA, mpar=mpar):
                                s.wait_ge(sT.h, h1mm2[mpar])
                                hs_done2[mpar] = sS.inc(s.activation(
                                    hsb[:, mpar * TW:(mpar + 1) * TW], pb[pA][:],
                                    AF.Silu, bias=b1c[:, br * 4 + mi:br * 4 + mi + 1]))

                            @blk.vector
                            def _(v, br=br, tw=tw, mi=mi, pB=pB, mpar=mpar):
                                v.wait_ge(sS.h, hs_done2[mpar])
                                v.wait_ge(sT.h, h3mm2[mpar])
                                if br < 8:
                                    v.wait_ge(sV.h, crep_ev)
                                    v.tensor_tensor(hscb[:, mpar * TW:(mpar + 1) * TW],
                                                    hsb[:, mpar * TW:(mpar + 1) * TW],
                                                    cbf[:], ALU.mult)
                                    hsrc = hscb[:, mpar * TW:(mpar + 1) * TW]
                                else:
                                    hsrc = hsb[:, mpar * TW:(mpar + 1) * TW]
                                W(v, hh_dma[mpar])
                                hh_done[mpar] = sV.inc(v.scalar_tensor_tensor(
                                    hhb[:, mpar * TW:(mpar + 1) * TW], pb[pB][:],
                                    b3c[:, br * 4 + mi:br * 4 + mi + 1], hsrc,
                                    ALU.add, ALU.mult))

                            @blk.gpsimd
                            def _(g, br=br, tw=tw, mi=mi, t0=t0, mpar=mpar):
                                g.wait_ge(sV.h, hh_done[mpar])
                                if mi == 0:
                                    dG.rotate()
                                row = (br * 4 + mi) * 128
                                hh_dma[mpar] = dG.issue(g.dma_start(
                                    out=hh_d[row:row + 128, t0:t0 + TW],
                                    in_=hhb[:, mpar * TW:(mpar + 1) * TW]))

                ffna_bar = dG.barrier()
                if "hh" in taps:
                    @blk.gpsimd
                    def _(g):
                        W(g, ffna_bar)
                        dG.rotate()
                        dG.issue(g.dma_start(out=taps["hh"], in_=hh_d))

        if "ffnb" in phases:
            with contextlib.ExitStack() as fb:
                w2b = fb.enter_context(nc.sbuf_tensor("w2b", [128, 36 * D], BF16))
                w2stg = fb.enter_context(nc.sbuf_tensor("w2stg", [128, 2 * D], F32))
                b2stg = fb.enter_context(nc.sbuf_tensor("b2stg", [9, D], F32))
                b2b = fb.enter_context(nc.sbuf_tensor("b2b", [9, D], BF16))
                hh_sb = fb.enter_context(nc.sbuf_tensor("hh_sb", [128, 2 * 36 * TW], BF16))
                fsb = fb.enter_context(nc.sbuf_tensor("fsb", [128, 2 * TW], F32))

                w2ld = [None] * 36
                w2cast = [0] * 36

                ffnb_entry_bar = dG.barrier() + dS.barrier()

                @blk.sync
                def _(q):
                    W(q, ffnb_entry_bar)
                    dS.rotate()
                    dS.issue(q.dma_start(out=b2stg[:], in_=b2sI))
                b2ld = dS.barrier()

                @blk.vector
                def _(v):
                    W(v, b2ld)
                    sV.inc(v.tensor_copy(b2b[:], b2stg[:]))
                b2c = sV.v

                for j in range(36):
                    @blk.sync
                    def _(q, j=j):
                        if j >= 2:
                            q.wait_ge(sV.h, w2cast[j - 2])
                        dS.rotate()
                        w2ld[j] = dS.issue(q.dma_start(
                            out=w2stg[:, (j % 2) * D:(j % 2 + 1) * D],
                            in_=w2I[j * 128:(j + 1) * 128, :]))

                    @blk.vector
                    def _(v, j=j):
                        W(v, w2ld[j])
                        w2cast[j] = sV.inc(v.tensor_copy(
                            w2b[:, j * D:(j + 1) * D],
                            w2stg[:, (j % 2) * D:(j % 2 + 1) * D]))
                w2_ready = w2cast[35]

                hh_ld = [None, None]
                f_ev = [0, 0]
                f_dma = [None, None]
                fmm = [0, 0]

                for tw in range(NTW):
                    t0 = tw * TW
                    twp = tw % 2

                    @blk.sync
                    def _(q, tw=tw, t0=t0, twp=twp):
                        if tw == 0:
                            W(q, ffna_bar)
                        if tw >= 2:
                            q.wait_ge(sT.h, fmm[twp])
                        dS.rotate()
                        for j in range(36):
                            hh_ld[twp] = dS.issue(q.dma_start(
                                out=hh_sb[:, (twp * 36 + j) * TW:(twp * 36 + j + 1) * TW],
                                in_=hh_d[j * 128:(j + 1) * 128, t0:t0 + TW]))

                    for m in range(8):
                        par = m % 2

                        @blk.tensor
                        def _(t, tw=tw, m=m, par=par, twp=twp, t0=t0):
                            t.wait_ge(sV.h, max(w2_ready, b2c))
                            t.wait_ge(sS.h, router_done_s)
                            W(t, hh_ld[twp])
                            if tw * 8 + m >= 2:
                                t.wait_ge(sV.h, f_ev[par])
                            t.matmul(pb[par][:],
                                     b2b[0:9, m * 128:(m + 1) * 128],
                                     combT05[0:9, t0:t0 + TW],
                                     start=True, stop=False)
                            for j in range(36):
                                mmi = t.matmul(
                                    pb[par][:],
                                    w2b[:, j * D + m * 128: j * D + (m + 1) * 128],
                                    hh_sb[:, (twp * 36 + j) * TW:(twp * 36 + j + 1) * TW],
                                    start=False, stop=(j == 35))
                            fmm[twp] = sT.inc(mmi)

                        @blk.vector
                        def _(v, tw=tw, m=m, par=par, twp=twp):
                            v.wait_ge(sT.h, fmm[twp])
                            W(v, f_dma[par])
                            f_ev[par] = sV.inc(v.tensor_copy(
                                fsb[:, par * TW:(par + 1) * TW], pb[par][:]))

                        @blk.gpsimd
                        def _(g, tw=tw, m=m, par=par, t0=t0):
                            g.wait_ge(sV.h, f_ev[par])
                            if m == 0:
                                dG.rotate()
                            f_dma[par] = dG.issue(g.dma_start(
                                out=cc_sg_i.ap()[D + m * 128:D + (m + 1) * 128, t0:t0 + TW],
                                in_=fsb[:, par * TW:(par + 1) * TW]))

                ffnb_bar = dG.barrier()

                @blk.gpsimd
                def _(g):
                    W(g, ffnb_bar)
                    sC.inc(g.collective_compute(
                        "AllReduce", ALU.add, replica_groups=GROUPS,
                        ins=[cc_sg_i.ap().opt()], outs=[cc_sg_o.ap().opt()]))
                cc_sg_done = sC.v

                if "sg" in taps:
                    @blk.gpsimd
                    def _(g):
                        g.wait_ge(sC.h, cc_sg_done)
                        dG.rotate()
                        dG.issue(g.dma_start(out=taps["sg"], in_=cc_sg_o.ap()))

        if "final" in phases:
            fin_entry_bar = dG.barrier() + dS.barrier()
            with contextlib.ExitStack() as fi:
                fs = fi.enter_context(nc.sbuf_tensor("fs", [128, 2 * TW], F32))
                ff = fi.enter_context(nc.sbuf_tensor("ff", [128, 2 * TW], F32))
                fx = fi.enter_context(nc.sbuf_tensor("fx", [128, 2 * TW], F32))
                fg = fi.enter_context(nc.sbuf_tensor("fg", [128, 2 * TW], BF16))
                fo = fi.enter_context(nc.sbuf_tensor("fo", [128, 2 * TW], F32))

                fin_ld = None
                fin_g = 0
                fin_o = 0
                fin_dma = [None, None]
                for dc in range(NCH):
                    for tw in range(NTW):
                        t0 = tw * TW
                        par = (dc * NTW + tw) % 2

                        @blk.sync
                        def _(q, dc=dc, tw=tw, t0=t0, par=par):
                            q.wait_ge(sC.h, cc_sg_done)
                            if dc == 0 and tw == 0:
                                W(q, fin_entry_bar)
                            if dc * NTW + tw >= 2:
                                q.wait_ge(sV.h, fin_o)
                            dS.rotate()
                            r0 = dc * 128
                            dS.issue(q.dma_start(out=fs[:, par * TW:(par + 1) * TW],
                                                 in_=cc_sg_o.ap()[r0:r0 + 128, t0:t0 + TW]))
                            dS.issue(q.dma_start(out=ff[:, par * TW:(par + 1) * TW],
                                                 in_=cc_sg_o.ap()[D + r0:D + r0 + 128, t0:t0 + TW]))
                            nonlocal fin_ld
                            fin_ld = dS.issue(q.dma_start(
                                out=fx[:, par * TW:(par + 1) * TW],
                                in_=x_fm[r0:r0 + 128, t0:t0 + TW]))

                        @blk.scalar
                        def _(s, dc=dc, tw=tw, par=par):
                            W(s, fin_ld)
                            nonlocal fin_g
                            fin_g = sS.inc(s.activation(
                                fg[:, par * TW:(par + 1) * TW],
                                ff[:, par * TW:(par + 1) * TW], AF.Silu))

                        @blk.vector
                        def _(v, dc=dc, tw=tw, par=par):
                            v.wait_ge(sS.h, fin_g)
                            W(v, fin_dma[par])
                            v.tensor_tensor(fo[:, par * TW:(par + 1) * TW],
                                            fs[:, par * TW:(par + 1) * TW],
                                            fg[:, par * TW:(par + 1) * TW], ALU.mult)
                            nonlocal fin_o
                            fin_o = sV.inc(v.tensor_tensor(
                                fo[:, par * TW:(par + 1) * TW],
                                fo[:, par * TW:(par + 1) * TW],
                                fx[:, par * TW:(par + 1) * TW], ALU.add))

                        @blk.gpsimd
                        def _(g, dc=dc, tw=tw, t0=t0, par=par):
                            g.wait_ge(sV.h, fin_o)
                            if tw == 0:
                                dG.rotate()
                            r0 = dc * 128
                            fin_dma[par] = dG.issue(g.dma_start(
                                out=out_fm[r0:r0 + 128, t0:t0 + TW],
                                in_=fo[:, par * TW:(par + 1) * TW]))

        fin_bar_g = dG.barrier()
        fin_bar_s = dS.barrier()

        @blk.sync
        def _(q):
            W(q, fin_bar_g)
            W(q, fin_bar_s)

        @blk.gpsimd
        def _(g):
            W(g, fin_bar_g)

    return nc


def host_inputs(inp, core):
    b, h = core // 2, core % 2
    dsl = slice(h * 1024, (h + 1) * 1024)
    isl = slice(h * 512, (h + 1) * 512)
    x = np.asarray(inp["L0_attn_x"], np.float32)[b]
    in_proj = np.asarray(inp["L0_attn_in_proj_w"], np.float32)
    conv_w = np.asarray(inp["L0_attn_conv_w"], np.float32)[:, 0, :]
    conv_b = np.asarray(inp["L0_attn_conv_b"], np.float32)
    x_proj = np.asarray(inp["L0_attn_x_proj_w"], np.float32)
    dtwf = np.asarray(inp["L0_attn_dt_proj_w"], np.float32)[:, 0]
    dtbf = np.asarray(inp["L0_attn_dt_proj_b"], np.float32)
    logA = np.asarray(inp["L0_attn_log_A"], np.float32)
    Dpf = np.asarray(inp["L0_attn_D_param"], np.float32)
    nwf = np.asarray(inp["L0_attn_norm_w"], np.float32)
    nbf = np.asarray(inp["L0_attn_norm_b"], np.float32)
    out_proj = np.asarray(inp["L0_attn_out_proj_w"], np.float32)
    gate_w = np.asarray(inp["L0_ffn_gate_w"], np.float32)
    w1 = np.asarray(inp["L0_ffn_w1"], np.float32)
    b1 = np.asarray(inp["L0_ffn_b1"], np.float32)
    w2 = np.asarray(inp["L0_ffn_w2"], np.float32)
    b2 = np.asarray(inp["L0_ffn_b2"], np.float32)
    w3 = np.asarray(inp["L0_ffn_w3"], np.float32)
    b3 = np.asarray(inp["L0_ffn_b3"], np.float32)
    sw1 = np.asarray(inp["L0_ffn_sw1"], np.float32)
    sb1 = np.asarray(inp["L0_ffn_sb1"], np.float32)
    sw2 = np.asarray(inp["L0_ffn_sw2"], np.float32)
    sb2 = np.asarray(inp["L0_ffn_sb2"], np.float32)
    sw3 = np.asarray(inp["L0_ffn_sw3"], np.float32)
    sb3 = np.asarray(inp["L0_ffn_sb3"], np.float32)

    C = np.ascontiguousarray

    def tile8(vec):
        return C(vec.reshape(8, 128).T)

    ipr = np.concatenate([in_proj[dsl], in_proj[2048 + h * 1024: 2048 + (h + 1) * 1024]], 0)
    w1h = np.concatenate([w1[:, :, isl].reshape(8 * 1024, 512), sw1[:, isl]], 0)
    w3h = np.concatenate([w3[:, :, isl].reshape(8 * 1024, 512), sw3[:, isl]], 0)
    w2h = np.concatenate([w2[:, isl, :].reshape(8 * 512, 1024), sw2[isl, :]], 0)
    b1h = np.concatenate([b1[:, isl], sb1[isl][None, :]], 0)
    b3h = np.concatenate([b3[:, isl], sb3[isl][None, :]], 0)
    b2s = np.concatenate([b2, sb2[None, :]], 0)

    sel32 = np.zeros((32, 32 * 128), np.float32)
    for n in range(32):
        sel32[n, n * 128:(n + 1) * 128] = 1.0
    sel8 = np.zeros((8, 8 * 128), np.float32)
    for e in range(8):
        sel8[e, e * 128:(e + 1) * 128] = 1.0

    bf = lambda a: C(a).astype(ml_dtypes.bfloat16)

    return {
        "x_tok": C(x),
        "x_fm": C(x.T),
        "ipT": C(ipr.T),
        "cw": C(conv_w[dsl].reshape(8, 128, 4).transpose(1, 0, 2).reshape(128, 32)),
        "cb": tile8(conv_b[dsl]),
        "xpT": C(x_proj[:, dsl].T.reshape(8, 128, 33).transpose(1, 0, 2).reshape(128, 8 * 33)),
        "dtw": C(dtwf[dsl][None, :]),
        "dtb": tile8(dtbf[dsl]),
        "logA": C(logA[dsl].reshape(8, 128, 16).transpose(1, 0, 2).reshape(128, 128)),
        "Dp": tile8(Dpf[dsl]),
        "nw": tile8(nwf[dsl]),
        "nb": tile8(nbf[dsl]),
        "opT": C(out_proj[:, dsl].T),
        "gwT": C(gate_w.T.reshape(8, 128, 8).transpose(1, 0, 2).reshape(128, 64)),
        "w1": C(w1h), "w3": C(w3h), "w2": C(w2h),
        "b1t": C(b1h.reshape(9, 4, 128).transpose(2, 0, 1).reshape(128, 36)),
        "b3t": C(b3h.reshape(9, 4, 128).transpose(2, 0, 1).reshape(128, 36)),
        "b2s": C(b2s),
        "ident_bf": bf(np.eye(128, dtype=np.float32)),
        "sel32": bf(sel32),
        "sel8": bf(sel8),
        "ones_bf": bf(np.ones((128, 1), np.float32)),
        "ones_f1": np.ones((1, 128), np.float32),
        "ones_r1b": bf(np.ones((1, 128), np.float32)),
        "epsc": np.full((128, 1), 1e-5, np.float32),
    }


def kernel(**inputs):
    nc = build_program()
    in_maps = [host_inputs(inputs, c) for c in range(8)]
    res = run_bass_kernel_spmd(nc, in_maps, core_ids=list(range(8)))
    out = np.stack([np.asarray(res.results[2 * b]["out_fm"]).T for b in range(NB)])
    return np.ascontiguousarray(out.astype(np.float32))


# version 10
# speedup vs baseline: 1.1711x; 1.0738x over previous
import contextlib
import numpy as np
import ml_dtypes
import concourse.bass as bass
import concourse.mybir as mybir
from concourse.bass_utils import run_bass_kernel_spmd

F32 = mybir.dt.float32
BF16 = mybir.dt.bfloat16
AF = mybir.ActivationFunctionType
ALU = mybir.AluOpType
AX = mybir.AxisListType

T = 4096
D = 1024
DIH = 1024
NCH = 8
NS = 16
TW = 512
NTW = T // TW
NTT = T // 128
NB = 4
EPS = 1e-5
GROUPS = [[0, 1], [2, 3], [4, 5], [6, 7]]

DEBUG_TAPS = ()
STOP_AFTER = None


class Sem:

    def __init__(self, es, nc, name):
        self.h = es.enter_context(nc.semaphore(name))
        self.v = 0

    def inc(self, inst, k=1):
        inst.then_inc(self.h, k)
        self.v += k
        return self.v


class RotSem:

    def __init__(self, es, nc, name, n=6):
        self.sems = [es.enter_context(nc.semaphore(f"{name}{i}")) for i in range(n)]
        self.vals = [0] * n
        self.i = 0
        self.n = n

    def rotate(self):
        self.i = (self.i + 1) % self.n

    def issue(self, inst, k=16):
        j = self.i
        inst.then_inc(self.sems[j], k)
        self.vals[j] += k
        return (self.sems[j], self.vals[j])

    def barrier(self):
        return [(self.sems[j], self.vals[j]) for j in range(self.n)]


def W(eng, *toks):
    for t in toks:
        if t is None:
            continue
        if isinstance(t, list):
            for tt in t:
                eng.wait_ge(tt[0], tt[1])
        else:
            eng.wait_ge(t[0], t[1])


def build_program():
    all_phases = ["p0", "mpre", "mscan", "ln2", "ffna", "ffnb", "final"]
    phases = all_phases if STOP_AFTER is None else all_phases[: all_phases.index(STOP_AFTER) + 1]

    nc = bass.Bass("TRN2", target_bir_lowering=False)

    def inp(name, shape, dtype=F32):
        return nc.dram_tensor(name, list(shape), dtype, kind="ExternalInput").ap()

    x_tok = inp("x_tok", [T, D])
    x_fm = inp("x_fm", [D, T])
    ipT = inp("ipT", [D, 2 * DIH])
    cw = inp("cw", [128, NCH * 4])
    cb = inp("cb", [128, NCH])
    xpT = inp("xpT", [128, NCH * 33])
    dtw = inp("dtw", [1, DIH])
    dtb = inp("dtb", [128, NCH])
    logA = inp("logA", [128, NCH * NS])
    DpI = inp("Dp", [128, NCH])
    nwI = inp("nw", [128, NCH])
    nbI = inp("nb", [128, NCH])
    opT = inp("opT", [DIH, D])
    gwT = inp("gwT", [128, NCH * 8])
    w1I = inp("w1", [9 * D, 512])
    w3I = inp("w3", [9 * D, 512])
    w2I = inp("w2", [9 * 512, D])
    b1tI = inp("b1t", [128, 9 * 4])
    b3tI = inp("b3t", [128, 9 * 4])
    b2sI = inp("b2s", [9, D])
    identI = inp("ident_bf", [128, 128], BF16)
    sel32I = inp("sel32", [32, 32 * 128], BF16)
    sel8I = inp("sel8", [8, 8 * 128], BF16)
    onesbI = inp("ones_bf", [128, 1], BF16)
    ones1I = inp("ones_f1", [1, 128])
    ones1bI = inp("ones_r1b", [1, 128], BF16)
    epsI = inp("epsc", [128, 1])

    out_fm = nc.dram_tensor("out_fm", [D, T], F32, kind="ExternalOutput").ap()

    u_d = nc.dram_tensor("u_d", [DIH, T], F32).ap()
    sz_d = nc.dram_tensor("sz_d", [DIH, T], F32).ap()
    y_d = nc.dram_tensor("y_d", [DIH, T], F32).ap()
    xnt_d = nc.dram_tensor("xnt_d", [D, T], BF16).ap()
    hh_d = nc.dram_tensor("hh_d", [36 * 128, T], BF16).ap()
    cc_ssm_i = nc.dram_tensor("cc_ssm_i", [33, T], F32)
    cc_ssm_o = nc.dram_tensor("cc_ssm_o", [33, T], F32)
    cc_st_i = nc.dram_tensor("cc_st_i", [2, T], F32)
    cc_st_o = nc.dram_tensor("cc_st_o", [2, T], F32)
    cc_sg_i = nc.dram_tensor("cc_sg_i", [2 * D, T], F32)
    cc_sg_o = nc.dram_tensor("cc_sg_o", [2 * D, T], F32)

    taps = {}
    tapshape = {"xnt": ([D, T], BF16), "u": ([DIH, T], F32), "sz": ([DIH, T], F32),
                "ssm": ([33, T], F32), "y": ([DIH, T], F32), "stats": ([2, T], F32),
                "sg": ([2 * D, T], F32), "hh": ([36 * 128, T], BF16),
                "combt": ([9, T], F32), "carr": ([128, 128], F32), "delta": ([DIH, T], F32), "du": ([DIH, T], BF16), "gated": ([DIH, T], BF16), "rbt": ([128, 32], F32), "combb": ([128, 256], BF16)}
    for tn in DEBUG_TAPS:
        shp, dt = tapshape[tn]
        taps[tn] = nc.dram_tensor("tap_" + tn, shp, dt, kind="ExternalOutput").ap()

    es = contextlib.ExitStack()
    with es:
        E = es.enter_context

        Acol = E(nc.sbuf_tensor("Acol", [128, NCH * NS], F32))
        s_cw = E(nc.sbuf_tensor("s_cw", [128, NCH * 4], F32))
        s_cb = E(nc.sbuf_tensor("s_cb", [128, NCH], F32))
        s_dtb = E(nc.sbuf_tensor("s_dtb", [128, NCH], F32))
        s_Dp = E(nc.sbuf_tensor("s_Dp", [128, NCH], F32))
        s_nw = E(nc.sbuf_tensor("s_nw", [128, NCH], F32))
        s_nb = E(nc.sbuf_tensor("s_nb", [128, NCH], F32))
        s_logA = E(nc.sbuf_tensor("s_logA", [128, NCH * NS], F32))
        s_id = E(nc.sbuf_tensor("s_id", [128, 128], BF16))
        s_sel32 = E(nc.sbuf_tensor("s_sel32", [32, 32 * 128], BF16))
        s_sel8 = E(nc.sbuf_tensor("s_sel8", [8, 8 * 128], BF16))
        s_onesb = E(nc.sbuf_tensor("s_onesb", [128, 1], BF16))
        s_ones1 = E(nc.sbuf_tensor("s_ones1", [1, 128], F32))
        s_ones1b = E(nc.sbuf_tensor("s_ones1b", [1, 128], BF16))
        s_eps = E(nc.sbuf_tensor("s_eps", [128, 1], F32))
        s_dtwb = E(nc.sbuf_tensor("s_dtwb", [1, DIH], BF16))
        s_dtwf = E(nc.sbuf_tensor("s_dtwf", [1, DIH], F32))
        combT = E(nc.sbuf_tensor("combT", [8, T], BF16))
        combT05 = E(nc.sbuf_tensor("combT05", [9, T], BF16))

        pb = [E(nc.psum_tensor(f"pb{i}", [128, TW], F32)) for i in range(5)]
        pst1 = E(nc.psum_tensor("pst1", [1, TW], F32))
        pst2 = E(nc.psum_tensor("pst2", [1, TW], F32))
        ptr2 = E(nc.psum_tensor("ptr2", [128, 1024], BF16))

        dS = RotSem(es, nc, "dS")
        dG = RotSem(es, nc, "dG")
        sV = Sem(es, nc, "sV")
        sS = Sem(es, nc, "sS")
        sT = Sem(es, nc, "sT")
        sP = Sem(es, nc, "sP")
        sC = Sem(es, nc, "sC")
        sN = Sem(es, nc, "sN")

        blk = E(nc.Block())

        @blk.sync
        def _(q):
            for dst, src in ((s_cw, cw), (s_cb, cb), (s_dtb, dtb), (s_Dp, DpI),
                             (s_nw, nwI), (s_nb, nbI), (s_logA, logA),
                             (s_id, identI), (s_sel32, sel32I), (s_sel8, sel8I),
                             (s_onesb, onesbI), (s_ones1, ones1I),
                             (s_ones1b, ones1bI), (s_eps, epsI),
                             (s_dtwf, dtw)):
                dS.issue(q.dma_start(out=dst[:], in_=src))
        c0 = dS.barrier()
        dS.rotate()

        @blk.scalar
        def _(s):
            W(s, c0)
            sS.inc(s.activation(Acol[:], s_logA[:], AF.Exp))
        aexp = sS.v

        @blk.vector
        def _(v):
            v.wait_ge(sS.h, aexp)
            sV.inc(v.tensor_scalar(Acol[:], Acol[:], -1.0, None, ALU.mult))
            sV.inc(v.tensor_copy(s_dtwb[:], s_dtwf[:]))
        consts_ready = sV.v

        with contextlib.ExitStack() as p0s:
            xnT = p0s.enter_context(nc.sbuf_tensor("xnT", [128, NCH * T], BF16))
            p0x = p0s.enter_context(nc.sbuf_tensor("p0x", [128, 2 * D], F32))
            p0sq = p0s.enter_context(nc.sbuf_tensor("p0sq", [128, D], F32))
            p0xn = p0s.enter_context(nc.sbuf_tensor("p0xn", [128, 2 * D], BF16))
            p0r = p0s.enter_context(nc.sbuf_tensor("p0r", [128, 16], F32))

            xload = [None] * NTT
            xncnt = [0] * NTT
            rstdcnt = [0] * NTT
            trcnt = [[0] * NCH for _ in range(NTT)]
            evaccnt = [[0] * NCH for _ in range(NTT)]

            for tt in range(NTT):
                par = tt % 2
                co = par * 8

                @blk.sync
                def _(q, tt=tt, par=par):
                    if tt >= 2:
                        q.wait_ge(sV.h, xncnt[tt - 2])
                    dS.rotate()
                    xload[tt] = dS.issue(q.dma_start(
                        out=p0x[:, par * D:(par + 1) * D],
                        in_=x_tok[tt * 128:(tt + 1) * 128, :]))

                @blk.vector
                def _(v, tt=tt, par=par, co=co):
                    xs = p0x[:, par * D:(par + 1) * D]
                    W(v, xload[tt])
                    if tt >= 2:
                        v.wait_ge(sS.h, rstdcnt[tt - 2])
                    v.reduce_sum(p0r[:, co:co + 1], xs, axis=AX.X)
                    v.tensor_tensor(p0sq[:], xs, xs, ALU.mult)
                    sV.inc(v.reduce_sum(p0r[:, co + 1:co + 2], p0sq[:], axis=AX.X))
                    v.wait_ge(sV.h, sV.v)
                    v.tensor_scalar(p0r[:, co + 2:co + 3], p0r[:, co:co + 1],
                                    1.0 / D, None, ALU.mult)
                    v.tensor_scalar(p0r[:, co + 3:co + 4], p0r[:, co + 1:co + 2],
                                    1.0 / D, None, ALU.mult)
                    sV.inc(v.scalar_tensor_tensor(
                        p0r[:, co + 4:co + 5], p0r[:, co + 2:co + 3],
                        p0r[:, co + 2:co + 3], p0r[:, co + 3:co + 4],
                        ALU.mult, ALU.subtract))
                nv = sV.v

                @blk.scalar
                def _(s, tt=tt, co=co, nv=nv):
                    s.wait_ge(sV.h, nv)
                    s.activation(p0r[:, co + 5:co + 6], p0r[:, co + 4:co + 5],
                                 AF.Ln, scale=-1.0, bias=s_eps[:])
                    rstdcnt[tt] = sS.inc(s.activation(
                        p0r[:, co + 6:co + 7], p0r[:, co + 5:co + 6],
                        AF.Exp, scale=-0.5))

                @blk.vector
                def _(v, tt=tt, par=par, co=co):
                    v.wait_ge(sS.h, rstdcnt[tt])
                    if tt >= 2:
                        v.wait_ge(sT.h, trcnt[tt - 2][NCH - 1])
                    xncnt[tt] = sV.inc(v.tensor_scalar(
                        p0xn[:, par * D:(par + 1) * D],
                        p0x[:, par * D:(par + 1) * D],
                        p0r[:, co + 2:co + 3], p0r[:, co + 6:co + 7],
                        ALU.subtract, ALU.mult))

                @blk.tensor
                def _(t, tt=tt, par=par):
                    t.wait_ge(sV.h, xncnt[tt])
                    t.wait_ge(sV.h, consts_ready)
                    if tt > 0:
                        t.wait_ge(sS.h, evaccnt[tt - 1][NCH - 1])
                    for j in range(NCH):
                        trcnt[tt][j] = sT.inc(t.matmul(
                            ptr2[:, j * 128:(j + 1) * 128],
                            p0xn[:, par * D + j * 128: par * D + (j + 1) * 128],
                            s_id[:], is_transpose=True,
                            start=(j == 0), stop=(j == NCH - 1)))

                @blk.scalar
                def _(s, tt=tt):
                    for j in range(NCH):
                        s.wait_ge(sT.h, trcnt[tt][j])
                        evaccnt[tt][j] = sS.inc(s.copy(
                            xnT[:, j * T + tt * 128: j * T + (tt + 1) * 128],
                            ptr2[:, j * 128:(j + 1) * 128]))

                @blk.gpsimd
                def _(g, tt=tt):
                    g.wait_ge(sS.h, evaccnt[tt][NCH - 1])
                    if tt % 4 == 0:
                        dG.rotate()
                    for j in range(NCH):
                        dG.issue(g.dma_start(
                            out=xnt_d[j * 128:(j + 1) * 128, tt * 128:(tt + 1) * 128],
                            in_=xnT[:, j * T + tt * 128: j * T + (tt + 1) * 128]))
            p0_done_s = evaccnt[NTT - 1][NCH - 1]
            p0_bar = dG.barrier()

            if "xnt" in taps:
                @blk.gpsimd
                def _(g):
                    W(g, p0_bar)
                    dG.rotate()
                    dG.issue(g.dma_start(out=taps["xnt"], in_=xnt_d))

            if "mpre" in phases:
                with contextlib.ExitStack() as mp:
                    ipTb = mp.enter_context(nc.sbuf_tensor("ipTb", [128, NCH * 2 * DIH], BF16))
                    ipstg = mp.enter_context(nc.sbuf_tensor("ipstg", [128, 2 * 2 * DIH], F32))
                    xpTb = mp.enter_context(nc.sbuf_tensor("xpTb", [128, NCH * 33], BF16))
                    xpstg = mp.enter_context(nc.sbuf_tensor("xpstg", [128, NCH * 33], F32))
                    xp_sb = mp.enter_context(nc.sbuf_tensor("xp_sb", [128, TW + 3], F32))
                    cvac = mp.enter_context(nc.sbuf_tensor("cvac", [128, 2 * TW], F32))
                    tails = mp.enter_context(nc.sbuf_tensor("tails", [128, NCH * 3], F32))
                    u_sb = mp.enter_context(nc.sbuf_tensor("u_sb", [128, 2 * TW], F32))
                    sz_sb = mp.enter_context(nc.sbuf_tensor("sz_sb", [128, 2 * TW], F32))
                    ubf = mp.enter_context(nc.sbuf_tensor("ubf", [128, NCH * TW], BF16))
                    ssm_sb = mp.enter_context(nc.sbuf_tensor("ssm_sb", [33, TW], F32))

                    ipld = [None] * NCH
                    ipcast = [0] * NCH

                    @blk.sync
                    def _(q):
                        dS.rotate()
                        dS.issue(q.dma_start(out=xpstg[:], in_=xpT))
                    xp_ld = dS.barrier()

                    for k in range(NCH):
                        @blk.sync
                        def _(q, k=k):
                            if k >= 2:
                                q.wait_ge(sV.h, ipcast[k - 2])
                            dS.rotate()
                            ipld[k] = dS.issue(q.dma_start(
                                out=ipstg[:, (k % 2) * 2 * DIH:(k % 2 + 1) * 2 * DIH],
                                in_=ipT[k * 128:(k + 1) * 128, :]))

                        @blk.vector
                        def _(v, k=k):
                            W(v, ipld[k])
                            ipcast[k] = sV.inc(v.tensor_copy(
                                ipTb[:, k * 2 * DIH:(k + 1) * 2 * DIH],
                                ipstg[:, (k % 2) * 2 * DIH:(k % 2 + 1) * 2 * DIH]))

                    @blk.vector
                    def _(v):
                        W(v, xp_ld)
                        sV.inc(v.tensor_copy(xpTb[:], xpstg[:]))
                        sV.inc(v.memset(tails[:], 0.0))
                    mpre_ready = sV.v

                    mm_ev = [[0] * 16 for _ in range(NTW)]
                    u_done = [[0] * NCH for _ in range(NTW)]
                    ubf_done = [[0] * NCH for _ in range(NTW)]
                    ssm_ev = [0] * NTW
                    mmcnt = [[0] * 16 for _ in range(NTW)]
                    xprojcnt = [0] * NTW

                    for tw in range(NTW):
                        t0 = tw * TW
                        for m in range(16):
                            par = m % 2

                            @blk.tensor
                            def _(t, tw=tw, m=m, par=par, t0=t0):
                                if tw == 0 and m == 0:
                                    t.wait_ge(sV.h, mpre_ready)
                                    t.wait_ge(sS.h, p0_done_s)
                                pm = (tw, m - 2) if m >= 2 else (tw - 1, 14 + m)
                                if pm[0] >= 0:
                                    cnt = mm_ev[pm[0]][pm[1]]
                                    if pm[1] < 8:
                                        t.wait_ge(sV.h, cnt)
                                    else:
                                        t.wait_ge(sS.h, cnt)
                                if m == 0:
                                    t.wait_ge(sV.h, ipcast[NCH - 1])
                                for k in range(NCH):
                                    mmi = t.matmul(
                                        pb[par][:],
                                        ipTb[:, k * 2 * DIH + m * 128: k * 2 * DIH + (m + 1) * 128],
                                        xnT[:, k * T + t0: k * T + t0 + TW],
                                        start=(k == 0), stop=(k == NCH - 1))
                                mmcnt[tw][m] = sT.inc(mmi)

                            if m < 8:
                                @blk.vector
                                def _(v, tw=tw, m=m, par=par):
                                    v.wait_ge(sT.h, mmcnt[tw][m])
                                    v.tensor_copy(xp_sb[:, 0:3], tails[:, m * 3:(m + 1) * 3])
                                    mm_ev[tw][m] = sV.inc(v.tensor_copy(
                                        xp_sb[:, 3:TW + 3], pb[par][:]))
                                    v.wait_ge(sV.h, sV.v)
                                    v.tensor_copy(tails[:, m * 3:(m + 1) * 3],
                                                  xp_sb[:, TW:TW + 3])
                                    ac = cvac[:, par * TW:(par + 1) * TW]
                                    v.tensor_scalar(ac, xp_sb[:, 0:TW],
                                                    s_cw[:, m * 4:m * 4 + 1], None, ALU.mult)
                                    for kk in range(1, 4):
                                        v.scalar_tensor_tensor(
                                            ac, xp_sb[:, kk:kk + TW],
                                            s_cw[:, m * 4 + kk:m * 4 + kk + 1],
                                            ac, ALU.mult, ALU.add)
                                    sV.inc(v.tensor_copy(xp_sb[:, 0:1], xp_sb[:, 0:1]))
                                cvdone = sV.v

                                @blk.scalar
                                def _(s, tw=tw, m=m, par=par, cvdone=cvdone):
                                    s.wait_ge(sV.h, cvdone)
                                    u_done[tw][m] = sS.inc(s.activation(
                                        u_sb[:, par * TW:(par + 1) * TW],
                                        cvac[:, par * TW:(par + 1) * TW],
                                        AF.Silu, bias=s_cb[:, m:m + 1]))

                                @blk.vector
                                def _(v, tw=tw, m=m, par=par):
                                    v.wait_ge(sS.h, u_done[tw][m])
                                    ubf_done[tw][m] = sV.inc(v.tensor_copy(
                                        ubf[:, m * TW:(m + 1) * TW],
                                        u_sb[:, par * TW:(par + 1) * TW]))

                                @blk.gpsimd
                                def _(g, tw=tw, m=m, par=par, t0=t0):
                                    g.wait_ge(sS.h, u_done[tw][m])
                                    if m == 0:
                                        dG.rotate()
                                    dG.issue(g.dma_start(
                                        out=u_d[m * 128:(m + 1) * 128, t0:t0 + TW],
                                        in_=u_sb[:, par * TW:(par + 1) * TW]))
                            else:
                                @blk.scalar
                                def _(s, tw=tw, m=m, par=par):
                                    s.wait_ge(sT.h, mmcnt[tw][m])
                                    mm_ev[tw][m] = sS.inc(s.activation(
                                        sz_sb[:, par * TW:(par + 1) * TW],
                                        pb[par][:], AF.Silu))

                                @blk.gpsimd
                                def _(g, tw=tw, m=m, par=par, t0=t0):
                                    g.wait_ge(sS.h, mm_ev[tw][m])
                                    if m == 8:
                                        dG.rotate()
                                    dG.issue(g.dma_start(
                                        out=sz_d[(m - 8) * 128:(m - 7) * 128, t0:t0 + TW],
                                        in_=sz_sb[:, par * TW:(par + 1) * TW]))

                        @blk.tensor
                        def _(t, tw=tw, t0=t0):
                            t.wait_ge(sV.h, ubf_done[tw][NCH - 1])
                            if tw >= 1:
                                t.wait_ge(sV.h, ssm_ev[tw - 1])
                            for k in range(NCH):
                                mmi = t.matmul(
                                    pb[2][0:33, :],
                                    xpTb[:, k * 33:(k + 1) * 33],
                                    ubf[:, k * TW:(k + 1) * TW],
                                    start=(k == 0), stop=(k == NCH - 1))
                            xprojcnt[tw] = sT.inc(mmi)

                        @blk.vector
                        def _(v, tw=tw):
                            v.wait_ge(sT.h, xprojcnt[tw])
                            ssm_ev[tw] = sV.inc(v.tensor_copy(ssm_sb[:], pb[2][0:33, :]))

                        @blk.gpsimd
                        def _(g, tw=tw, t0=t0):
                            g.wait_ge(sV.h, ssm_ev[tw])
                            dG.rotate()
                            dG.issue(g.dma_start(out=cc_ssm_i.ap()[:, t0:t0 + TW],
                                                 in_=ssm_sb[:]))

                    mpre_bar = dG.barrier()

                    @blk.gpsimd
                    def _(g):
                        W(g, mpre_bar)
                        sC.inc(g.collective_compute(
                            "AllReduce", ALU.add, replica_groups=GROUPS,
                            ins=[cc_ssm_i.ap().opt()], outs=[cc_ssm_o.ap().opt()]))
                    cc_ssm_done = sC.v

                    if "u" in taps or "sz" in taps or "ssm" in taps:
                        @blk.gpsimd
                        def _(g):
                            g.wait_ge(sC.h, cc_ssm_done)
                            dG.rotate()
                            if "u" in taps:
                                dG.issue(g.dma_start(out=taps["u"], in_=u_d))
                            if "sz" in taps:
                                dG.issue(g.dma_start(out=taps["sz"], in_=sz_d))
                            if "ssm" in taps:
                                dG.issue(g.dma_start(out=taps["ssm"], in_=cc_ssm_o.ap()))

        if "mscan" in phases:
            with contextlib.ExitStack() as ms:
                BCb = ms.enter_context(nc.sbuf_tensor("BCb", [32, T], BF16))
                rawb = ms.enter_context(nc.sbuf_tensor("rawb", [1, T], BF16))
                u_tw = ms.enter_context(nc.sbuf_tensor("u_tw", [128, NCH * TW], F32))
                del_tw = ms.enter_context(nc.sbuf_tensor("del_tw", [128, NCH * TW], F32))
                du_tw = ms.enter_context(nc.sbuf_tensor("du_tw", [128, NCH * TW], BF16))
                vb = ms.enter_context(nc.sbuf_tensor("vb", [128, 2 * TW], F32))
                eb = ms.enter_context(nc.sbuf_tensor("eb", [128, 2 * TW], F32))
                dAb = ms.enter_context(nc.sbuf_tensor("dAb", [128, 8 * TW], F32))
                dbub = ms.enter_context(nc.sbuf_tensor("dbub", [128, 8 * TW], BF16))
                s_bufs = ms.enter_context(nc.sbuf_tensor("s_bufs", [128, 16 * TW], F32))
                csb = ms.enter_context(nc.sbuf_tensor("csb", [128, 8 * TW], BF16))
                crep = ms.enter_context(nc.sbuf_tensor("crep", [128, 2 * TW], BF16))
                brep = ms.enter_context(nc.sbuf_tensor("brep", [128, 2 * TW], BF16))
                carr = ms.enter_context(nc.sbuf_tensor("carr", [128, 128], F32))
                y_sb = ms.enter_context(nc.sbuf_tensor("y_sb", [128, 2 * TW], F32))
                ybfb = ms.enter_context(nc.sbuf_tensor("ybfb", [128, TW], BF16))
                ysqb = ms.enter_context(nc.sbuf_tensor("ysqb", [128, TW], BF16))
                stev = ms.enter_context(nc.sbuf_tensor("stev", [1, 2 * TW], F32))

                mscan_entry_bar = dG.barrier() + dS.barrier()

                with contextlib.ExitStack() as bcs:
                    BCf = bcs.enter_context(nc.sbuf_tensor("BCf", [33, 2 * 1024], F32))
                    bch_ld = [None] * 4
                    bch_cast = [0] * 4
                    for ch in range(4):
                        @blk.sync
                        def _(q, ch=ch):
                            if ch == 0:
                                q.wait_ge(sC.h, cc_ssm_done)
                                W(q, mscan_entry_bar)
                            if ch >= 2:
                                q.wait_ge(sV.h, bch_cast[ch - 2])
                            dS.rotate()
                            co = (ch % 2) * 1024
                            dS.issue(q.dma_start(
                                out=BCf[0:32, co:co + 1024],
                                in_=cc_ssm_o.ap()[0:32, ch * 1024:(ch + 1) * 1024]))
                            bch_ld[ch] = dS.issue(q.dma_start(
                                out=BCf[32:33, co:co + 1024],
                                in_=cc_ssm_o.ap()[32:33, ch * 1024:(ch + 1) * 1024]))

                        @blk.vector
                        def _(v, ch=ch):
                            W(v, bch_ld[ch])
                            co = (ch % 2) * 1024
                            v.tensor_copy(BCb[:, ch * 1024:(ch + 1) * 1024],
                                          BCf[0:32, co:co + 1024])
                            bch_cast[ch] = sV.inc(v.tensor_copy(
                                rawb[:, ch * 1024:(ch + 1) * 1024],
                                BCf[32:33, co:co + 1024]))

                    @blk.vector
                    def _(v):
                        v.wait_ge(sV.h, bch_cast[3])
                        sV.inc(v.memset(carr[:], 0.0))
                    scan_ready = sV.v

                last_tts = [0] * NCH
                last_cs = [0] * NCH
                last_nsum = [0] * NCH
                last_dA = [0] * NCH
                tts_slot = [0] * 4
                cs_slot = [0] * 4
                cs_hist = [0] * 16
                nsum_slot = [0] * 4
                yev_slot = [0] * 4
                ydma_slot = [None, None]
                stat_guard = 0
                last_Cevac = 0
                bc_evac = 0
                b_evac = 0
                dbu_hist = [0, 0]
                u_ld = [None] * NTW
                del_done = [[0] * NCH for _ in range(NTW)]
                du_done = [[0] * NCH for _ in range(NTW)]
                yev = [0] * NCH
                stat_mm = 0
                stat_ev = [0] * NTW
                prev_del_consumed = 0
                prev_u_consumed = 0

                vcnt = [0] * NCH
                ecnt = [0] * NCH
                for tw in range(NTW):
                    t0 = tw * TW

                    @blk.sync
                    def _(q, tw=tw, t0=t0):
                        if tw == 0:
                            q.wait_ge(sC.h, cc_ssm_done)
                            q.wait_ge(sV.h, scan_ready)
                        else:
                            q.wait_ge(sV.h, prev_u_consumed)
                        dS.rotate()
                        for m in range(NCH):
                            u_ld[tw] = dS.issue(q.dma_start(
                                out=u_tw[:, m * TW:(m + 1) * TW],
                                in_=u_d[m * 128:(m + 1) * 128, t0:t0 + TW]))

                    for mc in range(NCH):
                        par = mc % 2

                        @blk.tensor
                        def _(t, tw=tw, mc=mc, t0=t0):
                            if tw == 0 and mc == 0:
                                t.wait_ge(sV.h, scan_ready)
                            t.wait_ge(sV.h, max(vcnt[(mc - 1) % NCH], bc_evac))
                            mmi = t.matmul(pb[4][:],
                                           s_dtwb[0:1, mc * 128:(mc + 1) * 128],
                                           rawb[0:1, t0:t0 + TW],
                                           start=True, stop=True)
                            del_done[tw][mc] = sT.inc(mmi)

                        @blk.vector
                        def _(v, tw=tw, mc=mc, par=par):
                            v.wait_ge(sT.h, del_done[tw][mc])
                            v.wait_ge(sS.h, ecnt[(mc - 2) % NCH])
                            vcnt[mc] = sV.inc(v.tensor_scalar(
                                vb[:, par * TW:(par + 1) * TW], pb[4][:],
                                s_dtb[:, mc:mc + 1], 20.0, ALU.add, ALU.min))

                        @blk.scalar
                        def _(s, tw=tw, mc=mc, par=par):
                            s.wait_ge(sV.h, vcnt[mc])
                            s.activation(eb[:, par * TW:(par + 1) * TW],
                                         vb[:, par * TW:(par + 1) * TW], AF.Exp)
                            ecnt[mc] = sS.inc(s.activation(
                                eb[:, par * TW:(par + 1) * TW],
                                eb[:, par * TW:(par + 1) * TW], AF.Ln, bias=1.0))

                        @blk.vector
                        def _(v, tw=tw, mc=mc, par=par):
                            v.wait_ge(sS.h, ecnt[mc])
                            if tw >= 1:
                                v.wait_ge(sS.h, prev_del_consumed)
                            v.tensor_scalar(
                                del_tw[:, mc * TW:(mc + 1) * TW],
                                eb[:, par * TW:(par + 1) * TW],
                                1e-3, 0.1, ALU.max, ALU.min)
                            W(v, u_ld[tw])
                            du_done[tw][mc] = sV.inc(v.tensor_tensor(
                                du_tw[:, mc * TW:(mc + 1) * TW],
                                del_tw[:, mc * TW:(mc + 1) * TW],
                                u_tw[:, mc * TW:(mc + 1) * TW], ALU.mult))

                    if "delta" in taps or "du" in taps:
                        @blk.gpsimd
                        def _(g, tw=tw, t0=t0):
                            g.wait_ge(sV.h, du_done[tw][NCH - 1])
                            dG.rotate()
                            for m in range(NCH):
                                if "delta" in taps:
                                    dG.issue(g.dma_start(
                                        out=taps["delta"][m * 128:(m + 1) * 128, t0:t0 + TW],
                                        in_=del_tw[:, m * TW:(m + 1) * TW]))
                                if "du" in taps:
                                    dG.issue(g.dma_start(
                                        out=taps["du"][m * 128:(m + 1) * 128, t0:t0 + TW],
                                        in_=du_tw[:, m * TW:(m + 1) * TW]))

                    for dh in range(2):
                        base_sS = sS.v

                        @blk.scalar
                        def _(s, tw=tw, dh=dh):
                            for n in range(NS):
                                giter = (tw * 2 + dh) * NS + n
                                if n == 0:
                                    s.wait_ge(sV.h, du_done[tw][NCH - 1])
                                    gprev = giter - 2 * NS + NS
                                    if tw > 0:
                                        s.wait_ge(sN.h, (tw - 1) * 2 * NS + dh * NS + NS)
                                else:
                                    s.wait_ge(sN.h, (tw * 2 + dh) * NS + n)
                                for dci in range(4):
                                    dc = dh * 4 + dci
                                    sS.inc(s.activation(
                                        dAb[:, dc * TW:(dc + 1) * TW],
                                        del_tw[:, dc * TW:(dc + 1) * TW],
                                        AF.Exp, scale=Acol[:, dc * NS + n:dc * NS + n + 1]))

                        @blk.tensor
                        def _(t, tw=tw, dh=dh):
                            t.wait_ge(sV.h, max(bc_evac, du_done[tw][NCH - 1]))
                            t.matmul(pb[4][:],
                                     s_sel32[:, 0:128],
                                     BCb[0:32, tw * TW:tw * TW + TW],
                                     start=True, stop=True).then_inc(sT.h, 1)
                        sT.v += 1
                        bbuild = sT.v

                        for n in range(NS):
                            npar = n % 2
                            giter = (tw * 2 + dh) * NS + n
                            dA_cnt = [base_sS + n * 4 + dci + 1 for dci in range(4)]

                            @blk.vector
                            def _(v, tw=tw, dh=dh, n=n, npar=npar, bbuild=bbuild):
                                v.wait_ge(sT.h, bbuild)
                                nonlocal b_evac
                                b_evac = sV.inc(v.tensor_copy(
                                    brep[:, npar * TW:(npar + 1) * TW], pb[4][:]))
                                for dci in range(4):
                                    dc = dh * 4 + dci
                                    v.tensor_tensor(
                                        dbub[:, dc * TW:(dc + 1) * TW],
                                        du_tw[:, dc * TW:(dc + 1) * TW],
                                        brep[:, npar * TW:(npar + 1) * TW], ALU.mult)
                                dbu_hist[npar] = sV.inc(v.tensor_copy(dbub[:, 0:1], dbub[:, 0:1]))
                            dbu_last = sV.v

                            @blk.tensor
                            def _(t, tw=tw, n=n, dbu_last=dbu_last):
                                t.wait_ge(sV.h, dbu_last)
                                t.matmul(pb[4][:],
                                         s_sel32[:, (16 + n) * 128:(17 + n) * 128],
                                         BCb[0:32, tw * TW:tw * TW + TW],
                                         start=True, stop=True).then_inc(sT.h, 1)
                            sT.v += 1
                            cbuild = sT.v

                            @blk.vector
                            def _(v, tw=tw, dh=dh, n=n, npar=npar, cbuild=cbuild,
                                  dA_cnt=dA_cnt, giter=giter):
                                v.wait_ge(sT.h, cbuild)
                                v.wait_ge(sP.h, sP.v)
                                nonlocal last_Cevac, bc_evac
                                last_Cevac = sV.inc(v.tensor_copy(
                                    crep[:, npar * TW:(npar + 1) * TW], pb[4][:]))
                                bc_evac = last_Cevac
                                for dci in range(4):
                                    dc = dh * 4 + dci
                                    sl = npar * 8 + dc
                                    v.wait_ge(sS.h, dA_cnt[dci])
                                    v.wait_ge(sP.h, cs_hist[sl])
                                    ci = n * 8 + dc
                                    init = carr[:, ci:ci + 1] if tw > 0 else 0.0
                                    sV.inc(v.tensor_tensor_scan(
                                        s_bufs[:, sl * TW:(sl + 1) * TW],
                                        dAb[:, dc * TW:(dc + 1) * TW],
                                        dbub[:, dc * TW:(dc + 1) * TW],
                                        init, ALU.mult, ALU.add))
                                v.wait_ge(sV.h, sV.v)
                                c0_ = n * 8 + dh * 4
                                svw = s_bufs.rearrange("p (c w) -> p c w", w=TW)
                                tcnt = sV.inc(v.tensor_copy(
                                    carr[:, c0_:c0_ + 4],
                                    svw[:, npar * 8 + dh * 4:npar * 8 + dh * 4 + 4, TW - 1:TW]))
                                sN.inc(v.tensor_copy(carr[:, c0_:c0_ + 1],
                                                     carr[:, c0_:c0_ + 1]))
                                for dci in range(4):
                                    last_tts[dh * 4 + dci] = tcnt

                            @blk.tensor
                            def _(t, tw=tw, dh=dh, n=n):
                                if n < NS - 1:
                                    t.wait_ge(sV.h, last_Cevac)
                                    t.matmul(pb[4][:],
                                             s_sel32[:, (n + 1) * 128:(n + 2) * 128],
                                             BCb[0:32, tw * TW:tw * TW + TW],
                                             start=True, stop=True).then_inc(sT.h, 1)
                            if n < NS - 1:
                                sT.v += 1
                                bbuild = sT.v

                            @blk.gpsimd
                            def _(g, tw=tw, dh=dh, n=n, npar=npar):
                                for dci in range(4):
                                    dc = dh * 4 + dci
                                    sl = npar * 8 + dc
                                    g.wait_ge(sV.h, max(last_tts[dc], last_Cevac))
                                    g.wait_ge(sT.h, last_nsum[dc])
                                    last_cs[dc] = sP.inc(g.tensor_tensor(
                                        csb[:, dc * TW:(dc + 1) * TW],
                                        s_bufs[:, sl * TW:(sl + 1) * TW],
                                        crep[:, npar * TW:(npar + 1) * TW], ALU.mult))
                                    cs_hist[sl] = last_cs[dc]

                            @blk.tensor
                            def _(t, tw=tw, dh=dh, n=n):
                                for dci in range(4):
                                    dc = dh * 4 + dci
                                    t.wait_ge(sP.h, last_cs[dc])
                                    if n == 0:
                                        t.wait_ge(sV.h, yev_slot[dci])
                                    last_nsum[dc] = sT.inc(t.matmul(
                                        pb[dci][:], s_id[:],
                                        csb[:, dc * TW:(dc + 1) * TW],
                                        start=(n == 0), stop=(n == NS - 1)))
                                    nsum_slot[dci] = last_nsum[dc]

                        for dci in range(4):
                            dc = dh * 4 + dci
                            ypar = dci % 2

                            @blk.vector
                            def _(v, tw=tw, dh=dh, dc=dc, dci=dci, ypar=ypar):
                                v.wait_ge(sT.h, last_nsum[dc])
                                W(v, ydma_slot[ypar])
                                v.wait_ge(sT.h, stat_guard)
                                v.scalar_tensor_tensor(
                                    y_sb[:, ypar * TW:(ypar + 1) * TW],
                                    u_tw[:, dc * TW:(dc + 1) * TW],
                                    s_Dp[:, dc:dc + 1], pb[dci][:],
                                    ALU.mult, ALU.add)
                                v.tensor_copy(ybfb[:], y_sb[:, ypar * TW:(ypar + 1) * TW])
                                yev[dc] = sV.inc(v.tensor_tensor(
                                    ysqb[:], ybfb[:], ybfb[:], ALU.mult))
                                yev_slot[dci] = yev[dc]

                            @blk.gpsimd
                            def _(g, tw=tw, dc=dc, ypar=ypar, t0=t0):
                                g.wait_ge(sV.h, yev[dc])
                                if dc % 4 == 0:
                                    dG.rotate()
                                ydma_slot[ypar] = dG.issue(g.dma_start(
                                    out=y_d[dc * 128:(dc + 1) * 128, t0:t0 + TW],
                                    in_=y_sb[:, ypar * TW:(ypar + 1) * TW]))

                            @blk.tensor
                            def _(t, tw=tw, dh=dh, dc=dc, dci=dci):
                                t.wait_ge(sV.h, yev[dc])
                                if tw > 0 and dh == 0 and dci == 0:
                                    t.wait_ge(sV.h, stat_ev[tw - 1])
                                first = (dh == 0 and dci == 0)
                                last = (dh == 1 and dci == 3)
                                t.matmul(pst1[:], s_onesb[:], ybfb[:],
                                         start=first, stop=last)
                                mmi = t.matmul(pst2[:], s_onesb[:], ysqb[:],
                                               start=first, stop=last)
                                nonlocal stat_mm, stat_guard
                                stat_mm = sT.inc(mmi)
                                stat_guard = stat_mm
                        prev_u_consumed = max(yev)

                    prev_del_consumed = max(last_dA)

                    if "carr" in taps and tw == 0:
                        @blk.gpsimd
                        def _(g):
                            g.wait_ge(sV.h, max(last_tts))
                            dG.rotate()
                            dG.issue(g.dma_start(out=taps["carr"], in_=carr[:]))

                    @blk.vector
                    def _(v, tw=tw):
                        v.wait_ge(sT.h, stat_mm)
                        v.tensor_copy(stev[:, 0:TW], pst1[:])
                        stat_ev[tw] = sV.inc(v.tensor_copy(stev[:, TW:2 * TW], pst2[:]))

                    @blk.gpsimd
                    def _(g, tw=tw, t0=t0):
                        g.wait_ge(sV.h, stat_ev[tw])
                        dG.rotate()
                        dG.issue(g.dma_start(out=cc_st_i.ap()[0:1, t0:t0 + TW],
                                             in_=stev[:, 0:TW]))
                        dG.issue(g.dma_start(out=cc_st_i.ap()[1:2, t0:t0 + TW],
                                             in_=stev[:, TW:2 * TW]))

                mscan_bar = dG.barrier()

                @blk.gpsimd
                def _(g):
                    W(g, mscan_bar)
                    sC.inc(g.collective_compute(
                        "AllReduce", ALU.add, replica_groups=GROUPS,
                        ins=[cc_st_i.ap().opt()], outs=[cc_st_o.ap().opt()]))
                cc_st_done = sC.v

                if "y" in taps or "stats" in taps:
                    @blk.gpsimd
                    def _(g):
                        g.wait_ge(sC.h, cc_st_done)
                        dG.rotate()
                        if "y" in taps:
                            dG.issue(g.dma_start(out=taps["y"], in_=y_d))
                        if "stats" in taps:
                            dG.issue(g.dma_start(out=taps["stats"], in_=cc_st_o.ap()))

        if "ln2" in phases:
            with contextlib.ExitStack() as l2:
                opTb = l2.enter_context(nc.sbuf_tensor("opTb", [128, NCH * D], BF16))
                opstg = l2.enter_context(nc.sbuf_tensor("opstg", [128, 2 * D], F32))
                st0 = l2.enter_context(nc.sbuf_tensor("st0", [1, T], F32))
                st1 = l2.enter_context(nc.sbuf_tensor("st1", [1, T], F32))
                murow = l2.enter_context(nc.sbuf_tensor("murow", [1, T], F32))
                rsrow = l2.enter_context(nc.sbuf_tensor("rsrow", [1, T], F32))
                muhi = l2.enter_context(nc.sbuf_tensor("muhi", [1, T], BF16))
                mulo = l2.enter_context(nc.sbuf_tensor("mulo", [1, T], BF16))
                rshi = l2.enter_context(nc.sbuf_tensor("rshi", [1, T], BF16))
                rslo = l2.enter_context(nc.sbuf_tensor("rslo", [1, T], BF16))
                tmpr = l2.enter_context(nc.sbuf_tensor("tmpr", [1, T], F32))
                yb2 = l2.enter_context(nc.sbuf_tensor("yb2", [128, 2 * TW], F32))
                szb2 = l2.enter_context(nc.sbuf_tensor("szb2", [128, 2 * TW], F32))
                t1b = l2.enter_context(nc.sbuf_tensor("t1b", [128, TW], F32))
                t2b = l2.enter_context(nc.sbuf_tensor("t2b", [128, TW], F32))
                gatedb = l2.enter_context(nc.sbuf_tensor("gatedb", [128, NCH * TW], BF16))
                sgsb = l2.enter_context(nc.sbuf_tensor("sgsb", [128, 2 * TW], F32))

                opld = [None] * NCH
                opcast = [0] * NCH
                ln2_entry_bar = dG.barrier() + dS.barrier()
                for k in range(NCH):
                    @blk.sync
                    def _(q, k=k):
                        if k == 0:
                            W(q, ln2_entry_bar)
                        if k >= 2:
                            q.wait_ge(sV.h, opcast[k - 2])
                        dS.rotate()
                        opld[k] = dS.issue(q.dma_start(
                            out=opstg[:, (k % 2) * D:(k % 2 + 1) * D],
                            in_=opT[k * 128:(k + 1) * 128, :]))

                    @blk.vector
                    def _(v, k=k):
                        W(v, opld[k])
                        opcast[k] = sV.inc(v.tensor_copy(
                            opTb[:, k * D:(k + 1) * D],
                            opstg[:, (k % 2) * D:(k % 2 + 1) * D]))

                @blk.sync
                def _(q):
                    q.wait_ge(sC.h, cc_st_done)
                    dS.rotate()
                    dS.issue(q.dma_start(out=st0[:], in_=cc_st_o.ap()[0:1, :]))
                    dS.issue(q.dma_start(out=st1[:], in_=cc_st_o.ap()[1:2, :]))
                st_ld = dS.barrier()

                @blk.vector
                def _(v):
                    W(v, st_ld)
                    v.tensor_scalar(murow[:], st0[:], 1.0 / 2048.0, None, ALU.mult)
                    v.tensor_scalar(st1[:], st1[:], 1.0 / 2048.0, None, ALU.mult)
                    v.tensor_tensor(rsrow[:], murow[:], murow[:], ALU.mult)
                    sV.inc(v.tensor_tensor(rsrow[:], rsrow[:], st1[:], ALU.subtract))
                negvar_done = sV.v

                @blk.scalar
                def _(s):
                    s.wait_ge(sV.h, negvar_done)
                    s.activation(rsrow[:], rsrow[:], AF.Ln, scale=-1.0,
                                 bias=s_eps[0:1, :])
                    sS.inc(s.activation(rsrow[:], rsrow[:], AF.Exp, scale=-0.5))
                rstd_done = sS.v

                @blk.vector
                def _(v):
                    v.wait_ge(sS.h, rstd_done)
                    v.tensor_copy(muhi[:], murow[:])
                    v.tensor_copy(rshi[:], rsrow[:])
                    sV.inc(v.tensor_tensor(tmpr[:], murow[:], muhi[:], ALU.subtract))
                    v.wait_ge(sV.h, sV.v)
                    v.tensor_copy(mulo[:], tmpr[:])
                    sV.inc(v.tensor_tensor(tmpr[:], rsrow[:], rshi[:], ALU.subtract))
                    v.wait_ge(sV.h, sV.v)
                    sV.inc(v.tensor_copy(rslo[:], tmpr[:]))
                hilo_done = sV.v

                g_done = [0] * NCH
                sg_ev = [0, 0]
                ln2_y_ld = [None] * NCH
                opmm = [0] * 8
                ln2_prev_g = 0

                for tw in range(NTW):
                    t0 = tw * TW

                    @blk.tensor
                    def _(t, tw=tw, t0=t0):
                        t.wait_ge(sV.h, hilo_done)
                        t.wait_ge(sV.h, opcast[NCH - 1])
                        if tw >= 1:
                            t.wait_ge(sV.h, g_done[NCH - 1])
                        t.matmul(pb[2][:], s_ones1b[:], muhi[0:1, t0:t0 + TW],
                                 start=True, stop=False)
                        t.matmul(pb[2][:], s_ones1b[:], mulo[0:1, t0:t0 + TW],
                                 start=False, stop=True)
                        t.matmul(pb[3][:], s_ones1b[:], rshi[0:1, t0:t0 + TW],
                                 start=True, stop=False)
                        t.matmul(pb[3][:], s_ones1b[:], rslo[0:1, t0:t0 + TW],
                                 start=False, stop=True).then_inc(sT.h, 1)
                    sT.v += 1
                    bcast_done = sT.v

                    for dc in range(NCH):
                        par = dc % 2

                        @blk.sync
                        def _(q, tw=tw, dc=dc, par=par, t0=t0):
                            if tw == 0 and dc == 0:
                                q.wait_ge(sC.h, cc_st_done)
                            prev = g_done[dc - 2] if dc >= 2 else ln2_prev_g
                            if tw > 0 or dc >= 2:
                                q.wait_ge(sV.h, prev)
                            if dc % 4 == 0:
                                dS.rotate()
                            dS.issue(q.dma_start(
                                out=yb2[:, par * TW:(par + 1) * TW],
                                in_=y_d[dc * 128:(dc + 1) * 128, t0:t0 + TW]))
                            ln2_y_ld[dc] = dS.issue(q.dma_start(
                                out=szb2[:, par * TW:(par + 1) * TW],
                                in_=sz_d[dc * 128:(dc + 1) * 128, t0:t0 + TW]))

                        @blk.vector
                        def _(v, tw=tw, dc=dc, par=par, bcast_done=bcast_done):
                            W(v, ln2_y_ld[dc])
                            v.wait_ge(sT.h, bcast_done)
                            v.tensor_tensor(t1b[:], yb2[:, par * TW:(par + 1) * TW],
                                            pb[2][:], ALU.subtract)
                            v.tensor_tensor(t2b[:], t1b[:], pb[3][:], ALU.mult)
                            v.tensor_scalar(t1b[:], t2b[:], s_nw[:, dc:dc + 1],
                                            s_nb[:, dc:dc + 1], ALU.mult, ALU.add)
                            g_done[dc] = sV.inc(v.tensor_tensor(
                                gatedb[:, dc * TW:(dc + 1) * TW], t1b[:],
                                szb2[:, par * TW:(par + 1) * TW], ALU.mult))
                    ln2_prev_g = g_done[NCH - 1]
                    if "gated" in taps:
                        @blk.gpsimd
                        def _(g, tw=tw, t0=t0):
                            g.wait_ge(sV.h, g_done[NCH - 1])
                            dG.rotate()
                            for m in range(NCH):
                                dG.issue(g.dma_start(
                                    out=taps["gated"][m * 128:(m + 1) * 128, t0:t0 + TW],
                                    in_=gatedb[:, m * TW:(m + 1) * TW]))

                    for m in range(8):
                        par = m % 2

                        @blk.tensor
                        def _(t, tw=tw, m=m, par=par):
                            t.wait_ge(sV.h, g_done[NCH - 1])
                            if tw * 8 + m >= 2:
                                t.wait_ge(sV.h, sg_ev[par])
                            for k in range(NCH):
                                mmi = t.matmul(
                                    pb[par][:],
                                    opTb[:, k * D + m * 128: k * D + (m + 1) * 128],
                                    gatedb[:, k * TW:(k + 1) * TW],
                                    start=(k == 0), stop=(k == NCH - 1))
                            opmm[m] = sT.inc(mmi)

                        @blk.vector
                        def _(v, tw=tw, m=m, par=par):
                            v.wait_ge(sT.h, opmm[m])
                            sg_ev[par] = sV.inc(v.tensor_copy(
                                sgsb[:, par * TW:(par + 1) * TW], pb[par][:]))

                        @blk.gpsimd
                        def _(g, tw=tw, m=m, par=par, t0=t0):
                            g.wait_ge(sV.h, sg_ev[par])
                            if m == 0:
                                dG.rotate()
                            dG.issue(g.dma_start(
                                out=cc_sg_i.ap()[m * 128:(m + 1) * 128, t0:t0 + TW],
                                in_=sgsb[:, par * TW:(par + 1) * TW]))

        ln2_bar = dG.barrier() + dS.barrier()

        if "ffna" in phases:
            with contextlib.ExitStack() as fa:
                xnT2 = fa.enter_context(nc.sbuf_tensor("xnT2", [128, NCH * T], BF16))
                gwTb = fa.enter_context(nc.sbuf_tensor("gwTb", [128, NCH * 8], BF16))
                gwstg = fa.enter_context(nc.sbuf_tensor("gwstg", [128, NCH * 8], F32))
                rb = fa.enter_context(nc.sbuf_tensor("rb", [128, 32], F32))
                combb = fa.enter_context(nc.sbuf_tensor("combb", [128, 256], BF16))
                w13stg = fa.enter_context(nc.sbuf_tensor("w13stg", [128, 2 * 8 * 512], F32))
                w1b = fa.enter_context(nc.sbuf_tensor("w1b", [128, NCH * 512], BF16))
                w3b = fa.enter_context(nc.sbuf_tensor("w3b", [128, NCH * 512], BF16))
                b1c = fa.enter_context(nc.sbuf_tensor("b1c", [128, 9 * 4], F32))
                b3c = fa.enter_context(nc.sbuf_tensor("b3c", [128, 9 * 4], F32))
                cbf = fa.enter_context(nc.sbuf_tensor("cbf", [128, TW], BF16))
                hsb = fa.enter_context(nc.sbuf_tensor("hsb", [128, 2 * TW], BF16))
                hscb = fa.enter_context(nc.sbuf_tensor("hscb", [128, 2 * TW], BF16))
                hhb = fa.enter_context(nc.sbuf_tensor("hhb", [128, 2 * TW], BF16))

                @blk.sync
                def _(q):
                    W(q, p0_bar)
                    W(q, ln2_bar)
                    dS.rotate()
                    dS.issue(q.dma_start(out=gwstg[:], in_=gwT))
                    dS.issue(q.dma_start(out=b1c[:], in_=b1tI))
                    dS.issue(q.dma_start(out=b3c[:], in_=b3tI))
                    for j in range(NCH):
                        dS.issue(q.dma_start(
                            out=xnT2[:, j * T:(j + 1) * T],
                            in_=xnt_d[j * 128:(j + 1) * 128, :]))
                ffna_ld = dS.barrier()

                @blk.vector
                def _(v):
                    W(v, ffna_ld)
                    sV.inc(v.tensor_copy(gwTb[:], gwstg[:]))
                    sV.inc(v.memset(combT05[:], 0.5))
                router_ready = sV.v

                r_mm = [0] * NTT
                r_soft = [0] * NTT
                tr_cnt = [0] * NTT
                tr_ev = [0] * NTT
                for tt in range(NTT):
                    @blk.tensor
                    def _(t, tt=tt):
                        t.wait_ge(sV.h, router_ready)
                        if tt >= 2:
                            t.wait_ge(sV.h, r_soft[tt - 2])
                        for k in range(NCH):
                            mmi = t.matmul(
                                pb[4][:, 0:8],
                                xnT2[:, k * T + tt * 128: k * T + (tt + 1) * 128],
                                gwTb[:, k * 8:(k + 1) * 8],
                                start=(k == 0), stop=(k == NCH - 1))
                        r_mm[tt] = sT.inc(mmi)

                    @blk.vector
                    def _(v, tt=tt):
                        v.wait_ge(sT.h, r_mm[tt])
                        v.tensor_copy(rb[:, 0:8], pb[4][:, 0:8])
                        sV.inc(v.reduce_max(rb[:, 8:9], rb[:, 0:8], axis=AX.X))
                        v.wait_ge(sV.h, sV.v)
                        sV.inc(v.tensor_scalar(rb[:, 9:10], rb[:, 8:9], -1.0,
                                               None, ALU.mult))
                    sm1 = sV.v

                    @blk.scalar
                    def _(s, tt=tt, sm1=sm1):
                        s.wait_ge(sV.h, sm1)
                        sS.inc(s.activation(rb[:, 16:24], rb[:, 0:8], AF.Exp,
                                            bias=rb[:, 9:10]))
                    sm2 = sS.v

                    @blk.vector
                    def _(v, tt=tt, sm2=sm2):
                        v.wait_ge(sS.h, sm2)
                        sV.inc(v.reduce_sum(rb[:, 10:11], rb[:, 16:24], axis=AX.X))
                        v.wait_ge(sV.h, sV.v)
                        sV.inc(v.reciprocal(rb[:, 11:12], rb[:, 10:11]))
                        v.wait_ge(sV.h, sV.v)
                        sV.inc(v.tensor_scalar(rb[:, 16:24], rb[:, 16:24],
                                               rb[:, 11:12], None, ALU.mult))
                        v.wait_ge(sV.h, sV.v)
                        sV.inc(v.reduce_max(rb[:, 8:9], rb[:, 16:24], axis=AX.X))
                        v.wait_ge(sV.h, sV.v)
                        sV.inc(v.tensor_scalar(rb[:, 24:32], rb[:, 16:24],
                                               rb[:, 8:9], None, ALU.is_equal))
                        v.wait_ge(sV.h, sV.v)
                        sV.inc(v.scalar_tensor_tensor(rb[:, 0:8], rb[:, 24:32], -1e30,
                                                      rb[:, 16:24], ALU.mult, ALU.add))
                        v.wait_ge(sV.h, sV.v)
                        sV.inc(v.reduce_max(rb[:, 9:10], rb[:, 0:8], axis=AX.X))
                        v.wait_ge(sV.h, sV.v)
                        sV.inc(v.tensor_scalar(rb[:, 0:8], rb[:, 0:8], rb[:, 9:10],
                                               None, ALU.is_equal))
                        v.wait_ge(sV.h, sV.v)
                        sV.inc(v.tensor_tensor(rb[:, 24:32], rb[:, 24:32], rb[:, 0:8],
                                               ALU.add))
                        v.wait_ge(sV.h, sV.v)
                        par = tt % 2
                        r_soft[tt] = sV.inc(v.tensor_tensor(
                            combb[:, par * 128:par * 128 + 8],
                            rb[:, 16:24], rb[:, 24:32], ALU.mult))

                    @blk.tensor
                    def _(t, tt=tt):
                        t.wait_ge(sV.h, r_soft[tt])
                        if tt >= 1:
                            t.wait_ge(sS.h, tr_ev[tt - 1])
                        tr_cnt[tt] = sT.inc(t.matmul(
                            ptr2[:, 0:128],
                            combb[:, (tt % 2) * 128:(tt % 2) * 128 + 128],
                            s_id[:], is_transpose=True,
                            start=True, stop=True))

                    @blk.scalar
                    def _(s, tt=tt):
                        s.wait_ge(sT.h, tr_cnt[tt])
                        srcp = ptr2[0:8, 0:128]
                        s.copy(combT[0:8, tt * 128:(tt + 1) * 128], srcp)
                        tr_ev[tt] = sS.inc(s.mul(
                            combT05[0:8, tt * 128:(tt + 1) * 128], srcp, 0.5))
                router_done_s = tr_ev[NTT - 1]
                if "rbt" in taps:
                    @blk.gpsimd
                    def _(g):
                        g.wait_ge(sV.h, r_soft[NTT - 1])
                        dG.rotate()
                        dG.issue(g.dma_start(out=taps["rbt"], in_=rb[:]))
                if "combb" in taps:
                    @blk.gpsimd
                    def _(g):
                        g.wait_ge(sV.h, r_soft[NTT - 1])
                        dG.rotate()
                        dG.issue(g.dma_start(out=taps["combb"], in_=combb[:]))

                if "combt" in taps:
                    with nc.sbuf_tensor("ctf", [9, T], F32) as ctf:
                        @blk.vector
                        def _(v):
                            v.wait_ge(sS.h, router_done_s)
                            sV.inc(v.tensor_copy(ctf[0:9, :], combT05[:]))
                        tapc = sV.v

                        @blk.gpsimd
                        def _(g):
                            g.wait_ge(sV.h, tapc)
                            dG.rotate()
                            dG.issue(g.dma_start(out=taps["combt"], in_=ctf[:]))

                w13ld = [None, None]
                w13cast = [0, 0]
                crep_ev = 0
                hs_done = 0
                hs_done2 = [0, 0]
                h1mm2 = [0, 0]
                h3mm2 = [0, 0]
                hh_done = [0, 0]
                hh_dma = [None, None]
                h1mm = 0
                h3mm = 0

                for br in range(9):
                    for half in range(2):
                        @blk.sync
                        def _(q, br=br, half=half):
                            if br > 0:
                                q.wait_ge(sV.h, w13cast[half])
                            dS.rotate()
                            wsrc = w1I if half == 0 else w3I
                            ho = half * 8 * 512
                            for k in range(NCH):
                                w13ld[half] = dS.issue(q.dma_start(
                                    out=w13stg[:, ho + k * 512: ho + (k + 1) * 512],
                                    in_=wsrc[br * D + k * 128: br * D + (k + 1) * 128, :]))

                        @blk.vector
                        def _(v, br=br, half=half):
                            W(v, w13ld[half])
                            if br > 0:
                                v.wait_ge(sT.h, h1mm if half == 0 else h3mm)
                            dstb = w1b if half == 0 else w3b
                            ho = half * 8 * 512
                            w13cast[half] = sV.inc(v.tensor_copy(
                                dstb[:], w13stg[:, ho:ho + 8 * 512]))

                    for tw in range(NTW):
                        t0 = tw * TW

                        if br < 8:
                            @blk.tensor
                            def _(t, br=br, tw=tw, t0=t0):
                                t.wait_ge(sS.h, router_done_s)
                                t.wait_ge(sV.h, crep_ev)
                                t.matmul(pb[4][:],
                                         s_sel8[:, br * 128:(br + 1) * 128],
                                         combT[0:8, t0:t0 + TW],
                                         start=True, stop=True).then_inc(sT.h, 1)
                            sT.v += 1
                            crep_mm = sT.v

                            @blk.vector
                            def _(v, crep_mm=crep_mm):
                                v.wait_ge(sT.h, crep_mm)
                                nonlocal crep_ev
                                crep_ev = sV.inc(v.tensor_copy(cbf[:], pb[4][:]))

                        for mi in range(4):
                            mpar = mi % 2
                            pA, pB = (0, 1) if mpar == 0 else (2, 3)

                            @blk.tensor
                            def _(t, br=br, tw=tw, mi=mi, t0=t0, pA=pA, pB=pB, mpar=mpar):
                                t.wait_ge(sV.h, max(w13cast))
                                t.wait_ge(sS.h, hs_done2[mpar])
                                t.wait_ge(sV.h, hh_done[mpar])
                                for k in range(NCH):
                                    mmi = t.matmul(
                                        pb[pA][:],
                                        w1b[:, k * 512 + mi * 128: k * 512 + (mi + 1) * 128],
                                        xnT2[:, k * T + t0: k * T + t0 + TW],
                                        start=(k == 0), stop=(k == NCH - 1))
                                nonlocal h1mm
                                h1mm = sT.inc(mmi)
                                h1mm2[mpar] = h1mm
                                for k in range(NCH):
                                    mmi = t.matmul(
                                        pb[pB][:],
                                        w3b[:, k * 512 + mi * 128: k * 512 + (mi + 1) * 128],
                                        xnT2[:, k * T + t0: k * T + t0 + TW],
                                        start=(k == 0), stop=(k == NCH - 1))
                                nonlocal h3mm
                                h3mm = sT.inc(mmi)
                                h3mm2[mpar] = h3mm

                            @blk.scalar
                            def _(s, br=br, tw=tw, mi=mi, pA=pA, mpar=mpar):
                                s.wait_ge(sT.h, h1mm2[mpar])
                                hs_done2[mpar] = sS.inc(s.activation(
                                    hsb[:, mpar * TW:(mpar + 1) * TW], pb[pA][:],
                                    AF.Silu, bias=b1c[:, br * 4 + mi:br * 4 + mi + 1]))

                            @blk.vector
                            def _(v, br=br, tw=tw, mi=mi, pB=pB, mpar=mpar):
                                v.wait_ge(sS.h, hs_done2[mpar])
                                v.wait_ge(sT.h, h3mm2[mpar])
                                if br < 8:
                                    v.wait_ge(sV.h, crep_ev)
                                    v.tensor_tensor(hscb[:, mpar * TW:(mpar + 1) * TW],
                                                    hsb[:, mpar * TW:(mpar + 1) * TW],
                                                    cbf[:], ALU.mult)
                                    hsrc = hscb[:, mpar * TW:(mpar + 1) * TW]
                                else:
                                    hsrc = hsb[:, mpar * TW:(mpar + 1) * TW]
                                W(v, hh_dma[mpar])
                                hh_done[mpar] = sV.inc(v.scalar_tensor_tensor(
                                    hhb[:, mpar * TW:(mpar + 1) * TW], pb[pB][:],
                                    b3c[:, br * 4 + mi:br * 4 + mi + 1], hsrc,
                                    ALU.add, ALU.mult))

                            @blk.gpsimd
                            def _(g, br=br, tw=tw, mi=mi, t0=t0, mpar=mpar):
                                g.wait_ge(sV.h, hh_done[mpar])
                                if mi == 0:
                                    dG.rotate()
                                row = (br * 4 + mi) * 128
                                hh_dma[mpar] = dG.issue(g.dma_start(
                                    out=hh_d[row:row + 128, t0:t0 + TW],
                                    in_=hhb[:, mpar * TW:(mpar + 1) * TW]))

                ffna_bar = dG.barrier()
                if "hh" in taps:
                    @blk.gpsimd
                    def _(g):
                        W(g, ffna_bar)
                        dG.rotate()
                        dG.issue(g.dma_start(out=taps["hh"], in_=hh_d))

        if "ffnb" in phases:
            with contextlib.ExitStack() as fb:
                w2b = fb.enter_context(nc.sbuf_tensor("w2b", [128, 36 * D], BF16))
                w2stg = fb.enter_context(nc.sbuf_tensor("w2stg", [128, 2 * D], F32))
                b2stg = fb.enter_context(nc.sbuf_tensor("b2stg", [9, D], F32))
                b2b = fb.enter_context(nc.sbuf_tensor("b2b", [9, D], BF16))
                hh_sb = fb.enter_context(nc.sbuf_tensor("hh_sb", [128, 2 * 36 * TW], BF16))
                fsb = fb.enter_context(nc.sbuf_tensor("fsb", [128, 2 * TW], F32))

                w2ld = [None] * 36
                w2cast = [0] * 36

                ffnb_entry_bar = dG.barrier() + dS.barrier()

                @blk.sync
                def _(q):
                    W(q, ffnb_entry_bar)
                    dS.rotate()
                    dS.issue(q.dma_start(out=b2stg[:], in_=b2sI))
                b2ld = dS.barrier()

                @blk.vector
                def _(v):
                    W(v, b2ld)
                    sV.inc(v.tensor_copy(b2b[:], b2stg[:]))
                b2c = sV.v

                for j in range(36):
                    @blk.sync
                    def _(q, j=j):
                        if j >= 2:
                            q.wait_ge(sV.h, w2cast[j - 2])
                        dS.rotate()
                        w2ld[j] = dS.issue(q.dma_start(
                            out=w2stg[:, (j % 2) * D:(j % 2 + 1) * D],
                            in_=w2I[j * 128:(j + 1) * 128, :]))

                    @blk.vector
                    def _(v, j=j):
                        W(v, w2ld[j])
                        w2cast[j] = sV.inc(v.tensor_copy(
                            w2b[:, j * D:(j + 1) * D],
                            w2stg[:, (j % 2) * D:(j % 2 + 1) * D]))
                w2_ready = w2cast[35]

                hh_ld = [None, None]
                f_ev = [0, 0]
                f_dma = [None, None]
                fmm = [0, 0]

                for tw in range(NTW):
                    t0 = tw * TW
                    twp = tw % 2

                    @blk.sync
                    def _(q, tw=tw, t0=t0, twp=twp):
                        if tw == 0:
                            W(q, ffna_bar)
                        if tw >= 2:
                            q.wait_ge(sT.h, fmm[twp])
                        dS.rotate()
                        for j in range(36):
                            hh_ld[twp] = dS.issue(q.dma_start(
                                out=hh_sb[:, (twp * 36 + j) * TW:(twp * 36 + j + 1) * TW],
                                in_=hh_d[j * 128:(j + 1) * 128, t0:t0 + TW]))

                    for m in range(8):
                        par = m % 2

                        @blk.tensor
                        def _(t, tw=tw, m=m, par=par, twp=twp, t0=t0):
                            t.wait_ge(sV.h, max(w2_ready, b2c))
                            t.wait_ge(sS.h, router_done_s)
                            W(t, hh_ld[twp])
                            if tw * 8 + m >= 2:
                                t.wait_ge(sV.h, f_ev[par])
                            t.matmul(pb[par][:],
                                     b2b[0:9, m * 128:(m + 1) * 128],
                                     combT05[0:9, t0:t0 + TW],
                                     start=True, stop=False)
                            for j in range(36):
                                mmi = t.matmul(
                                    pb[par][:],
                                    w2b[:, j * D + m * 128: j * D + (m + 1) * 128],
                                    hh_sb[:, (twp * 36 + j) * TW:(twp * 36 + j + 1) * TW],
                                    start=False, stop=(j == 35))
                            fmm[twp] = sT.inc(mmi)

                        @blk.vector
                        def _(v, tw=tw, m=m, par=par, twp=twp):
                            v.wait_ge(sT.h, fmm[twp])
                            W(v, f_dma[par])
                            f_ev[par] = sV.inc(v.tensor_copy(
                                fsb[:, par * TW:(par + 1) * TW], pb[par][:]))

                        @blk.gpsimd
                        def _(g, tw=tw, m=m, par=par, t0=t0):
                            g.wait_ge(sV.h, f_ev[par])
                            if m == 0:
                                dG.rotate()
                            f_dma[par] = dG.issue(g.dma_start(
                                out=cc_sg_i.ap()[D + m * 128:D + (m + 1) * 128, t0:t0 + TW],
                                in_=fsb[:, par * TW:(par + 1) * TW]))

                ffnb_bar = dG.barrier()

                @blk.gpsimd
                def _(g):
                    W(g, ffnb_bar)
                    sC.inc(g.collective_compute(
                        "AllReduce", ALU.add, replica_groups=GROUPS,
                        ins=[cc_sg_i.ap().opt()], outs=[cc_sg_o.ap().opt()]))
                cc_sg_done = sC.v

                if "sg" in taps:
                    @blk.gpsimd
                    def _(g):
                        g.wait_ge(sC.h, cc_sg_done)
                        dG.rotate()
                        dG.issue(g.dma_start(out=taps["sg"], in_=cc_sg_o.ap()))

        if "final" in phases:
            fin_entry_bar = dG.barrier() + dS.barrier()
            with contextlib.ExitStack() as fi:
                fs = fi.enter_context(nc.sbuf_tensor("fs", [128, 2 * TW], F32))
                ff = fi.enter_context(nc.sbuf_tensor("ff", [128, 2 * TW], F32))
                fx = fi.enter_context(nc.sbuf_tensor("fx", [128, 2 * TW], F32))
                fg = fi.enter_context(nc.sbuf_tensor("fg", [128, 2 * TW], BF16))
                fo = fi.enter_context(nc.sbuf_tensor("fo", [128, 2 * TW], F32))

                fin_ld = None
                fin_g = 0
                fin_o = 0
                fin_dma = [None, None]
                for dc in range(NCH):
                    for tw in range(NTW):
                        t0 = tw * TW
                        par = (dc * NTW + tw) % 2

                        @blk.sync
                        def _(q, dc=dc, tw=tw, t0=t0, par=par):
                            q.wait_ge(sC.h, cc_sg_done)
                            if dc == 0 and tw == 0:
                                W(q, fin_entry_bar)
                            if dc * NTW + tw >= 2:
                                q.wait_ge(sV.h, fin_o)
                            dS.rotate()
                            r0 = dc * 128
                            dS.issue(q.dma_start(out=fs[:, par * TW:(par + 1) * TW],
                                                 in_=cc_sg_o.ap()[r0:r0 + 128, t0:t0 + TW]))
                            dS.issue(q.dma_start(out=ff[:, par * TW:(par + 1) * TW],
                                                 in_=cc_sg_o.ap()[D + r0:D + r0 + 128, t0:t0 + TW]))
                            nonlocal fin_ld
                            fin_ld = dS.issue(q.dma_start(
                                out=fx[:, par * TW:(par + 1) * TW],
                                in_=x_fm[r0:r0 + 128, t0:t0 + TW]))

                        @blk.scalar
                        def _(s, dc=dc, tw=tw, par=par):
                            W(s, fin_ld)
                            nonlocal fin_g
                            fin_g = sS.inc(s.activation(
                                fg[:, par * TW:(par + 1) * TW],
                                ff[:, par * TW:(par + 1) * TW], AF.Silu))

                        @blk.vector
                        def _(v, dc=dc, tw=tw, par=par):
                            v.wait_ge(sS.h, fin_g)
                            W(v, fin_dma[par])
                            v.tensor_tensor(fo[:, par * TW:(par + 1) * TW],
                                            fs[:, par * TW:(par + 1) * TW],
                                            fg[:, par * TW:(par + 1) * TW], ALU.mult)
                            nonlocal fin_o
                            fin_o = sV.inc(v.tensor_tensor(
                                fo[:, par * TW:(par + 1) * TW],
                                fo[:, par * TW:(par + 1) * TW],
                                fx[:, par * TW:(par + 1) * TW], ALU.add))

                        @blk.gpsimd
                        def _(g, dc=dc, tw=tw, t0=t0, par=par):
                            g.wait_ge(sV.h, fin_o)
                            if tw == 0:
                                dG.rotate()
                            r0 = dc * 128
                            fin_dma[par] = dG.issue(g.dma_start(
                                out=out_fm[r0:r0 + 128, t0:t0 + TW],
                                in_=fo[:, par * TW:(par + 1) * TW]))

        fin_bar_g = dG.barrier()
        fin_bar_s = dS.barrier()

        @blk.sync
        def _(q):
            W(q, fin_bar_g)
            W(q, fin_bar_s)

        @blk.gpsimd
        def _(g):
            W(g, fin_bar_g)

    return nc


def host_inputs(inp, core):
    b, h = core // 2, core % 2
    dsl = slice(h * 1024, (h + 1) * 1024)
    isl = slice(h * 512, (h + 1) * 512)
    x = np.asarray(inp["L0_attn_x"], np.float32)[b]
    in_proj = np.asarray(inp["L0_attn_in_proj_w"], np.float32)
    conv_w = np.asarray(inp["L0_attn_conv_w"], np.float32)[:, 0, :]
    conv_b = np.asarray(inp["L0_attn_conv_b"], np.float32)
    x_proj = np.asarray(inp["L0_attn_x_proj_w"], np.float32)
    dtwf = np.asarray(inp["L0_attn_dt_proj_w"], np.float32)[:, 0]
    dtbf = np.asarray(inp["L0_attn_dt_proj_b"], np.float32)
    logA = np.asarray(inp["L0_attn_log_A"], np.float32)
    Dpf = np.asarray(inp["L0_attn_D_param"], np.float32)
    nwf = np.asarray(inp["L0_attn_norm_w"], np.float32)
    nbf = np.asarray(inp["L0_attn_norm_b"], np.float32)
    out_proj = np.asarray(inp["L0_attn_out_proj_w"], np.float32)
    gate_w = np.asarray(inp["L0_ffn_gate_w"], np.float32)
    w1 = np.asarray(inp["L0_ffn_w1"], np.float32)
    b1 = np.asarray(inp["L0_ffn_b1"], np.float32)
    w2 = np.asarray(inp["L0_ffn_w2"], np.float32)
    b2 = np.asarray(inp["L0_ffn_b2"], np.float32)
    w3 = np.asarray(inp["L0_ffn_w3"], np.float32)
    b3 = np.asarray(inp["L0_ffn_b3"], np.float32)
    sw1 = np.asarray(inp["L0_ffn_sw1"], np.float32)
    sb1 = np.asarray(inp["L0_ffn_sb1"], np.float32)
    sw2 = np.asarray(inp["L0_ffn_sw2"], np.float32)
    sb2 = np.asarray(inp["L0_ffn_sb2"], np.float32)
    sw3 = np.asarray(inp["L0_ffn_sw3"], np.float32)
    sb3 = np.asarray(inp["L0_ffn_sb3"], np.float32)

    C = np.ascontiguousarray

    def tile8(vec):
        return C(vec.reshape(8, 128).T)

    ipr = np.concatenate([in_proj[dsl], in_proj[2048 + h * 1024: 2048 + (h + 1) * 1024]], 0)
    w1h = np.concatenate([w1[:, :, isl].reshape(8 * 1024, 512), sw1[:, isl]], 0)
    w3h = np.concatenate([w3[:, :, isl].reshape(8 * 1024, 512), sw3[:, isl]], 0)
    w2h = np.concatenate([w2[:, isl, :].reshape(8 * 512, 1024), sw2[isl, :]], 0)
    b1h = np.concatenate([b1[:, isl], sb1[isl][None, :]], 0)
    b3h = np.concatenate([b3[:, isl], sb3[isl][None, :]], 0)
    b2s = np.concatenate([b2, sb2[None, :]], 0)

    sel32 = np.zeros((32, 32 * 128), np.float32)
    for n in range(32):
        sel32[n, n * 128:(n + 1) * 128] = 1.0
    sel8 = np.zeros((8, 8 * 128), np.float32)
    for e in range(8):
        sel8[e, e * 128:(e + 1) * 128] = 1.0

    bf = lambda a: C(a).astype(ml_dtypes.bfloat16)

    return {
        "x_tok": C(x),
        "x_fm": C(x.T),
        "ipT": C(ipr.T),
        "cw": C(conv_w[dsl].reshape(8, 128, 4).transpose(1, 0, 2).reshape(128, 32)),
        "cb": tile8(conv_b[dsl]),
        "xpT": C(x_proj[:, dsl].T.reshape(8, 128, 33).transpose(1, 0, 2).reshape(128, 8 * 33)),
        "dtw": C(dtwf[dsl][None, :]),
        "dtb": tile8(dtbf[dsl]),
        "logA": C(logA[dsl].reshape(8, 128, 16).transpose(1, 0, 2).reshape(128, 128)),
        "Dp": tile8(Dpf[dsl]),
        "nw": tile8(nwf[dsl]),
        "nb": tile8(nbf[dsl]),
        "opT": C(out_proj[:, dsl].T),
        "gwT": C(gate_w.T.reshape(8, 128, 8).transpose(1, 0, 2).reshape(128, 64)),
        "w1": C(w1h), "w3": C(w3h), "w2": C(w2h),
        "b1t": C(b1h.reshape(9, 4, 128).transpose(2, 0, 1).reshape(128, 36)),
        "b3t": C(b3h.reshape(9, 4, 128).transpose(2, 0, 1).reshape(128, 36)),
        "b2s": C(b2s),
        "ident_bf": bf(np.eye(128, dtype=np.float32)),
        "sel32": bf(sel32),
        "sel8": bf(sel8),
        "ones_bf": bf(np.ones((128, 1), np.float32)),
        "ones_f1": np.ones((1, 128), np.float32),
        "ones_r1b": bf(np.ones((1, 128), np.float32)),
        "epsc": np.full((128, 1), 1e-5, np.float32),
    }


def kernel(**inputs):
    nc = build_program()
    in_maps = [host_inputs(inputs, c) for c in range(8)]
    res = run_bass_kernel_spmd(nc, in_maps, core_ids=list(range(8)))
    out = np.stack([np.asarray(res.results[2 * b]["out_fm"]).T for b in range(NB)])
    return np.ascontiguousarray(out.astype(np.float32))


# version 11
# speedup vs baseline: 1.2625x; 1.0781x over previous
import contextlib
import numpy as np
import ml_dtypes
import concourse.bass as bass
import concourse.mybir as mybir
from concourse.bass_utils import run_bass_kernel_spmd

F32 = mybir.dt.float32
BF16 = mybir.dt.bfloat16
AF = mybir.ActivationFunctionType
ALU = mybir.AluOpType
AX = mybir.AxisListType

T = 4096
D = 1024
DIH = 1024
NCH = 8
NS = 16
TW = 512
NTW = T // TW
NTT = T // 128
NB = 4
EPS = 1e-5
GROUPS = [[0, 1], [2, 3], [4, 5], [6, 7]]

DEBUG_TAPS = ()
STOP_AFTER = None


class Sem:

    def __init__(self, es, nc, name):
        self.h = es.enter_context(nc.semaphore(name))
        self.v = 0

    def inc(self, inst, k=1):
        inst.then_inc(self.h, k)
        self.v += k
        return self.v


class RotSem:

    def __init__(self, es, nc, name, n=6):
        self.sems = [es.enter_context(nc.semaphore(f"{name}{i}")) for i in range(n)]
        self.vals = [0] * n
        self.i = 0
        self.n = n

    def rotate(self):
        self.i = (self.i + 1) % self.n

    def issue(self, inst, k=16):
        j = self.i
        inst.then_inc(self.sems[j], k)
        self.vals[j] += k
        return (self.sems[j], self.vals[j])

    def barrier(self):
        return [(self.sems[j], self.vals[j]) for j in range(self.n)]


def W(eng, *toks):
    for t in toks:
        if t is None:
            continue
        if isinstance(t, list):
            for tt in t:
                eng.wait_ge(tt[0], tt[1])
        else:
            eng.wait_ge(t[0], t[1])


def build_program():
    all_phases = ["p0", "mpre", "mscan", "ln2", "ffna", "ffnb", "final"]
    phases = all_phases if STOP_AFTER is None else all_phases[: all_phases.index(STOP_AFTER) + 1]

    nc = bass.Bass("TRN2", target_bir_lowering=False)

    def inp(name, shape, dtype=F32):
        return nc.dram_tensor(name, list(shape), dtype, kind="ExternalInput").ap()

    x_tok = inp("x_tok", [T, D])
    x_fm = inp("x_fm", [D, T])
    ipT = inp("ipT", [D, 2 * DIH])
    cw = inp("cw", [128, NCH * 4])
    cb = inp("cb", [128, NCH])
    xpT = inp("xpT", [128, NCH * 33])
    dtw = inp("dtw", [1, DIH])
    dtb = inp("dtb", [128, NCH])
    logA = inp("logA", [128, NCH * NS])
    DpI = inp("Dp", [128, NCH])
    nwI = inp("nw", [128, NCH])
    nbI = inp("nb", [128, NCH])
    opT = inp("opT", [DIH, D])
    gwT = inp("gwT", [128, NCH * 8])
    w1I = inp("w1", [9 * D, 512])
    w3I = inp("w3", [9 * D, 512])
    w2I = inp("w2", [9 * 512, D])
    b1tI = inp("b1t", [128, 9 * 4])
    b3tI = inp("b3t", [128, 9 * 4])
    b2sI = inp("b2s", [9, D])
    identI = inp("ident_bf", [128, 128], BF16)
    sel32I = inp("sel32", [32, 32 * 128], BF16)
    sel8I = inp("sel8", [8, 8 * 128], BF16)
    onesbI = inp("ones_bf", [128, 1], BF16)
    ones1I = inp("ones_f1", [1, 128])
    ones1bI = inp("ones_r1b", [1, 128], BF16)
    epsI = inp("epsc", [128, 1])

    out_fm = nc.dram_tensor("out_fm", [D, T], F32, kind="ExternalOutput").ap()

    u_d = nc.dram_tensor("u_d", [DIH, T], F32).ap()
    sz_d = nc.dram_tensor("sz_d", [DIH, T], F32).ap()
    y_d = nc.dram_tensor("y_d", [DIH, T], F32).ap()
    xnt_d = nc.dram_tensor("xnt_d", [D, T], BF16).ap()
    hh_d = nc.dram_tensor("hh_d", [36 * 128, T], BF16).ap()
    cc_ssm_i = nc.dram_tensor("cc_ssm_i", [33, T], F32)
    cc_ssm_o = nc.dram_tensor("cc_ssm_o", [33, T], F32)
    cc_st_i = nc.dram_tensor("cc_st_i", [2, T], F32)
    cc_st_o = nc.dram_tensor("cc_st_o", [2, T], F32)
    cc_sg_i = nc.dram_tensor("cc_sg_i", [2 * D, T], F32)
    cc_sg_o = nc.dram_tensor("cc_sg_o", [2 * D, T], F32)
    cc_f_i = nc.dram_tensor("cc_f_i", [D, T], F32)
    cc_f_o = nc.dram_tensor("cc_f_o", [D, T], F32)

    taps = {}
    tapshape = {"xnt": ([D, T], BF16), "u": ([DIH, T], F32), "sz": ([DIH, T], F32),
                "ssm": ([33, T], F32), "y": ([DIH, T], F32), "stats": ([2, T], F32),
                "sg": ([2 * D, T], F32), "hh": ([36 * 128, T], BF16),
                "combt": ([9, T], F32), "carr": ([128, 128], F32), "delta": ([DIH, T], F32), "du": ([DIH, T], BF16), "gated": ([DIH, T], BF16), "rbt": ([128, 32], F32), "combb": ([128, 256], BF16)}
    for tn in DEBUG_TAPS:
        shp, dt = tapshape[tn]
        taps[tn] = nc.dram_tensor("tap_" + tn, shp, dt, kind="ExternalOutput").ap()

    es = contextlib.ExitStack()
    with es:
        E = es.enter_context

        Acol = E(nc.sbuf_tensor("Acol", [128, NCH * NS], F32))
        s_cw = E(nc.sbuf_tensor("s_cw", [128, NCH * 4], F32))
        s_cb = E(nc.sbuf_tensor("s_cb", [128, NCH], F32))
        s_dtb = E(nc.sbuf_tensor("s_dtb", [128, NCH], F32))
        s_Dp = E(nc.sbuf_tensor("s_Dp", [128, NCH], F32))
        s_nw = E(nc.sbuf_tensor("s_nw", [128, NCH], F32))
        s_nb = E(nc.sbuf_tensor("s_nb", [128, NCH], F32))
        s_logA = E(nc.sbuf_tensor("s_logA", [128, NCH * NS], F32))
        s_id = E(nc.sbuf_tensor("s_id", [128, 128], BF16))
        s_sel32 = E(nc.sbuf_tensor("s_sel32", [32, 32 * 128], BF16))
        s_sel8 = E(nc.sbuf_tensor("s_sel8", [8, 8 * 128], BF16))
        s_onesb = E(nc.sbuf_tensor("s_onesb", [128, 1], BF16))
        s_ones1 = E(nc.sbuf_tensor("s_ones1", [1, 128], F32))
        s_ones1b = E(nc.sbuf_tensor("s_ones1b", [1, 128], BF16))
        s_eps = E(nc.sbuf_tensor("s_eps", [128, 1], F32))
        s_dtwb = E(nc.sbuf_tensor("s_dtwb", [1, DIH], BF16))
        s_dtwf = E(nc.sbuf_tensor("s_dtwf", [1, DIH], F32))
        combT = E(nc.sbuf_tensor("combT", [8, T], BF16))
        combT05 = E(nc.sbuf_tensor("combT05", [9, T], BF16))

        pb = [E(nc.psum_tensor(f"pb{i}", [128, TW], F32)) for i in range(5)]
        pst1 = E(nc.psum_tensor("pst1", [1, TW], F32))
        pst2 = E(nc.psum_tensor("pst2", [1, TW], F32))
        ptr2 = E(nc.psum_tensor("ptr2", [128, 1024], BF16))

        dS = RotSem(es, nc, "dS")
        dG = RotSem(es, nc, "dG")
        sV = Sem(es, nc, "sV")
        sS = Sem(es, nc, "sS")
        sT = Sem(es, nc, "sT")
        sP = Sem(es, nc, "sP")
        sC = Sem(es, nc, "sC")
        sN = Sem(es, nc, "sN")

        blk = E(nc.Block())

        @blk.sync
        def _(q):
            for dst, src in ((s_cw, cw), (s_cb, cb), (s_dtb, dtb), (s_Dp, DpI),
                             (s_nw, nwI), (s_nb, nbI), (s_logA, logA),
                             (s_id, identI), (s_sel32, sel32I), (s_sel8, sel8I),
                             (s_onesb, onesbI), (s_ones1, ones1I),
                             (s_ones1b, ones1bI), (s_eps, epsI),
                             (s_dtwf, dtw)):
                dS.issue(q.dma_start(out=dst[:], in_=src))
        c0 = dS.barrier()
        dS.rotate()

        @blk.scalar
        def _(s):
            W(s, c0)
            sS.inc(s.activation(Acol[:], s_logA[:], AF.Exp))
        aexp = sS.v

        @blk.vector
        def _(v):
            v.wait_ge(sS.h, aexp)
            sV.inc(v.tensor_scalar(Acol[:], Acol[:], -1.0, None, ALU.mult))
            sV.inc(v.tensor_copy(s_dtwb[:], s_dtwf[:]))
        consts_ready = sV.v

        with contextlib.ExitStack() as p0s:
            xnT = p0s.enter_context(nc.sbuf_tensor("xnT", [128, NCH * T], BF16))
            p0x = p0s.enter_context(nc.sbuf_tensor("p0x", [128, 2 * D], F32))
            p0sq = p0s.enter_context(nc.sbuf_tensor("p0sq", [128, D], F32))
            p0xn = p0s.enter_context(nc.sbuf_tensor("p0xn", [128, 2 * D], BF16))
            p0r = p0s.enter_context(nc.sbuf_tensor("p0r", [128, 16], F32))

            xload = [None] * NTT
            xncnt = [0] * NTT
            rstdcnt = [0] * NTT
            trcnt = [[0] * NCH for _ in range(NTT)]
            evaccnt = [[0] * NCH for _ in range(NTT)]

            for tt in range(NTT):
                par = tt % 2
                co = par * 8

                @blk.sync
                def _(q, tt=tt, par=par):
                    if tt >= 2:
                        q.wait_ge(sV.h, xncnt[tt - 2])
                    dS.rotate()
                    xload[tt] = dS.issue(q.dma_start(
                        out=p0x[:, par * D:(par + 1) * D],
                        in_=x_tok[tt * 128:(tt + 1) * 128, :]))

                @blk.vector
                def _(v, tt=tt, par=par, co=co):
                    xs = p0x[:, par * D:(par + 1) * D]
                    W(v, xload[tt])
                    if tt >= 2:
                        v.wait_ge(sS.h, rstdcnt[tt - 2])
                    v.reduce_sum(p0r[:, co:co + 1], xs, axis=AX.X)
                    v.tensor_tensor(p0sq[:], xs, xs, ALU.mult)
                    sV.inc(v.reduce_sum(p0r[:, co + 1:co + 2], p0sq[:], axis=AX.X))
                    v.wait_ge(sV.h, sV.v)
                    v.tensor_scalar(p0r[:, co + 2:co + 3], p0r[:, co:co + 1],
                                    1.0 / D, None, ALU.mult)
                    v.tensor_scalar(p0r[:, co + 3:co + 4], p0r[:, co + 1:co + 2],
                                    1.0 / D, None, ALU.mult)
                    sV.inc(v.scalar_tensor_tensor(
                        p0r[:, co + 4:co + 5], p0r[:, co + 2:co + 3],
                        p0r[:, co + 2:co + 3], p0r[:, co + 3:co + 4],
                        ALU.mult, ALU.subtract))
                nv = sV.v

                @blk.scalar
                def _(s, tt=tt, co=co, nv=nv):
                    s.wait_ge(sV.h, nv)
                    s.activation(p0r[:, co + 5:co + 6], p0r[:, co + 4:co + 5],
                                 AF.Ln, scale=-1.0, bias=s_eps[:])
                    rstdcnt[tt] = sS.inc(s.activation(
                        p0r[:, co + 6:co + 7], p0r[:, co + 5:co + 6],
                        AF.Exp, scale=-0.5))

                @blk.vector
                def _(v, tt=tt, par=par, co=co):
                    v.wait_ge(sS.h, rstdcnt[tt])
                    if tt >= 2:
                        v.wait_ge(sT.h, trcnt[tt - 2][NCH - 1])
                    xncnt[tt] = sV.inc(v.tensor_scalar(
                        p0xn[:, par * D:(par + 1) * D],
                        p0x[:, par * D:(par + 1) * D],
                        p0r[:, co + 2:co + 3], p0r[:, co + 6:co + 7],
                        ALU.subtract, ALU.mult))

                @blk.tensor
                def _(t, tt=tt, par=par):
                    t.wait_ge(sV.h, xncnt[tt])
                    t.wait_ge(sV.h, consts_ready)
                    if tt > 0:
                        t.wait_ge(sS.h, evaccnt[tt - 1][NCH - 1])
                    for j in range(NCH):
                        trcnt[tt][j] = sT.inc(t.matmul(
                            ptr2[:, j * 128:(j + 1) * 128],
                            p0xn[:, par * D + j * 128: par * D + (j + 1) * 128],
                            s_id[:], is_transpose=True,
                            start=(j == 0), stop=(j == NCH - 1)))

                @blk.scalar
                def _(s, tt=tt):
                    for j in range(NCH):
                        s.wait_ge(sT.h, trcnt[tt][j])
                        evaccnt[tt][j] = sS.inc(s.copy(
                            xnT[:, j * T + tt * 128: j * T + (tt + 1) * 128],
                            ptr2[:, j * 128:(j + 1) * 128]))

                @blk.gpsimd
                def _(g, tt=tt):
                    g.wait_ge(sS.h, evaccnt[tt][NCH - 1])
                    if tt % 4 == 0:
                        dG.rotate()
                    for j in range(NCH):
                        dG.issue(g.dma_start(
                            out=xnt_d[j * 128:(j + 1) * 128, tt * 128:(tt + 1) * 128],
                            in_=xnT[:, j * T + tt * 128: j * T + (tt + 1) * 128]))
            p0_done_s = evaccnt[NTT - 1][NCH - 1]
            p0_bar = dG.barrier()

            if "xnt" in taps:
                @blk.gpsimd
                def _(g):
                    W(g, p0_bar)
                    dG.rotate()
                    dG.issue(g.dma_start(out=taps["xnt"], in_=xnt_d))

            if "mpre" in phases:
                with contextlib.ExitStack() as mp:
                    ipTb = mp.enter_context(nc.sbuf_tensor("ipTb", [128, NCH * 2 * DIH], BF16))
                    ipstg = mp.enter_context(nc.sbuf_tensor("ipstg", [128, 2 * 2 * DIH], F32))
                    xpTb = mp.enter_context(nc.sbuf_tensor("xpTb", [128, NCH * 33], BF16))
                    xpstg = mp.enter_context(nc.sbuf_tensor("xpstg", [128, NCH * 33], F32))
                    xp_sb = mp.enter_context(nc.sbuf_tensor("xp_sb", [128, TW + 3], F32))
                    cvac = mp.enter_context(nc.sbuf_tensor("cvac", [128, 2 * TW], F32))
                    tails = mp.enter_context(nc.sbuf_tensor("tails", [128, NCH * 3], F32))
                    u_sb = mp.enter_context(nc.sbuf_tensor("u_sb", [128, 2 * TW], F32))
                    sz_sb = mp.enter_context(nc.sbuf_tensor("sz_sb", [128, 2 * TW], F32))
                    ubf = mp.enter_context(nc.sbuf_tensor("ubf", [128, NCH * TW], BF16))
                    ssm_sb = mp.enter_context(nc.sbuf_tensor("ssm_sb", [33, TW], F32))

                    ipld = [None] * NCH
                    ipcast = [0] * NCH

                    @blk.sync
                    def _(q):
                        dS.rotate()
                        dS.issue(q.dma_start(out=xpstg[:], in_=xpT))
                    xp_ld = dS.barrier()

                    for k in range(NCH):
                        @blk.sync
                        def _(q, k=k):
                            if k >= 2:
                                q.wait_ge(sV.h, ipcast[k - 2])
                            dS.rotate()
                            ipld[k] = dS.issue(q.dma_start(
                                out=ipstg[:, (k % 2) * 2 * DIH:(k % 2 + 1) * 2 * DIH],
                                in_=ipT[k * 128:(k + 1) * 128, :]))

                        @blk.vector
                        def _(v, k=k):
                            W(v, ipld[k])
                            ipcast[k] = sV.inc(v.tensor_copy(
                                ipTb[:, k * 2 * DIH:(k + 1) * 2 * DIH],
                                ipstg[:, (k % 2) * 2 * DIH:(k % 2 + 1) * 2 * DIH]))

                    @blk.vector
                    def _(v):
                        W(v, xp_ld)
                        sV.inc(v.tensor_copy(xpTb[:], xpstg[:]))
                        sV.inc(v.memset(tails[:], 0.0))
                    mpre_ready = sV.v

                    mm_ev = [[0] * 16 for _ in range(NTW)]
                    u_done = [[0] * NCH for _ in range(NTW)]
                    ubf_done = [[0] * NCH for _ in range(NTW)]
                    ssm_ev = [0] * NTW
                    mmcnt = [[0] * 16 for _ in range(NTW)]
                    xprojcnt = [0] * NTW

                    for tw in range(NTW):
                        t0 = tw * TW
                        for m in range(16):
                            par = m % 2

                            @blk.tensor
                            def _(t, tw=tw, m=m, par=par, t0=t0):
                                if tw == 0 and m == 0:
                                    t.wait_ge(sV.h, mpre_ready)
                                    t.wait_ge(sS.h, p0_done_s)
                                pm = (tw, m - 2) if m >= 2 else (tw - 1, 14 + m)
                                if pm[0] >= 0:
                                    cnt = mm_ev[pm[0]][pm[1]]
                                    if pm[1] < 8:
                                        t.wait_ge(sV.h, cnt)
                                    else:
                                        t.wait_ge(sS.h, cnt)
                                if m == 0:
                                    t.wait_ge(sV.h, ipcast[NCH - 1])
                                for k in range(NCH):
                                    mmi = t.matmul(
                                        pb[par][:],
                                        ipTb[:, k * 2 * DIH + m * 128: k * 2 * DIH + (m + 1) * 128],
                                        xnT[:, k * T + t0: k * T + t0 + TW],
                                        start=(k == 0), stop=(k == NCH - 1))
                                mmcnt[tw][m] = sT.inc(mmi)

                            if m < 8:
                                @blk.vector
                                def _(v, tw=tw, m=m, par=par):
                                    v.wait_ge(sT.h, mmcnt[tw][m])
                                    v.tensor_copy(xp_sb[:, 0:3], tails[:, m * 3:(m + 1) * 3])
                                    mm_ev[tw][m] = sV.inc(v.tensor_copy(
                                        xp_sb[:, 3:TW + 3], pb[par][:]))
                                    v.wait_ge(sV.h, sV.v)
                                    v.tensor_copy(tails[:, m * 3:(m + 1) * 3],
                                                  xp_sb[:, TW:TW + 3])
                                    ac = cvac[:, par * TW:(par + 1) * TW]
                                    v.tensor_scalar(ac, xp_sb[:, 0:TW],
                                                    s_cw[:, m * 4:m * 4 + 1], None, ALU.mult)
                                    for kk in range(1, 4):
                                        v.scalar_tensor_tensor(
                                            ac, xp_sb[:, kk:kk + TW],
                                            s_cw[:, m * 4 + kk:m * 4 + kk + 1],
                                            ac, ALU.mult, ALU.add)
                                    sV.inc(v.tensor_copy(xp_sb[:, 0:1], xp_sb[:, 0:1]))
                                cvdone = sV.v

                                @blk.scalar
                                def _(s, tw=tw, m=m, par=par, cvdone=cvdone):
                                    s.wait_ge(sV.h, cvdone)
                                    u_done[tw][m] = sS.inc(s.activation(
                                        u_sb[:, par * TW:(par + 1) * TW],
                                        cvac[:, par * TW:(par + 1) * TW],
                                        AF.Silu, bias=s_cb[:, m:m + 1]))

                                @blk.vector
                                def _(v, tw=tw, m=m, par=par):
                                    v.wait_ge(sS.h, u_done[tw][m])
                                    ubf_done[tw][m] = sV.inc(v.tensor_copy(
                                        ubf[:, m * TW:(m + 1) * TW],
                                        u_sb[:, par * TW:(par + 1) * TW]))

                                @blk.gpsimd
                                def _(g, tw=tw, m=m, par=par, t0=t0):
                                    g.wait_ge(sS.h, u_done[tw][m])
                                    if m == 0:
                                        dG.rotate()
                                    dG.issue(g.dma_start(
                                        out=u_d[m * 128:(m + 1) * 128, t0:t0 + TW],
                                        in_=u_sb[:, par * TW:(par + 1) * TW]))
                            else:
                                @blk.scalar
                                def _(s, tw=tw, m=m, par=par):
                                    s.wait_ge(sT.h, mmcnt[tw][m])
                                    mm_ev[tw][m] = sS.inc(s.activation(
                                        sz_sb[:, par * TW:(par + 1) * TW],
                                        pb[par][:], AF.Silu))

                                @blk.gpsimd
                                def _(g, tw=tw, m=m, par=par, t0=t0):
                                    g.wait_ge(sS.h, mm_ev[tw][m])
                                    if m == 8:
                                        dG.rotate()
                                    dG.issue(g.dma_start(
                                        out=sz_d[(m - 8) * 128:(m - 7) * 128, t0:t0 + TW],
                                        in_=sz_sb[:, par * TW:(par + 1) * TW]))

                        @blk.tensor
                        def _(t, tw=tw, t0=t0):
                            t.wait_ge(sV.h, ubf_done[tw][NCH - 1])
                            if tw >= 1:
                                t.wait_ge(sV.h, ssm_ev[tw - 1])
                            for k in range(NCH):
                                mmi = t.matmul(
                                    pb[2][0:33, :],
                                    xpTb[:, k * 33:(k + 1) * 33],
                                    ubf[:, k * TW:(k + 1) * TW],
                                    start=(k == 0), stop=(k == NCH - 1))
                            xprojcnt[tw] = sT.inc(mmi)

                        @blk.vector
                        def _(v, tw=tw):
                            v.wait_ge(sT.h, xprojcnt[tw])
                            ssm_ev[tw] = sV.inc(v.tensor_copy(ssm_sb[:], pb[2][0:33, :]))

                        @blk.gpsimd
                        def _(g, tw=tw, t0=t0):
                            g.wait_ge(sV.h, ssm_ev[tw])
                            dG.rotate()
                            dG.issue(g.dma_start(out=cc_ssm_i.ap()[:, t0:t0 + TW],
                                                 in_=ssm_sb[:]))

                    mpre_bar = dG.barrier()

                    @blk.gpsimd
                    def _(g):
                        W(g, mpre_bar)
                        sC.inc(g.collective_compute(
                            "AllReduce", ALU.add, replica_groups=GROUPS,
                            ins=[cc_ssm_i.ap().opt()], outs=[cc_ssm_o.ap().opt()]))
                    cc_ssm_done = sC.v

                    if "u" in taps or "sz" in taps or "ssm" in taps:
                        @blk.gpsimd
                        def _(g):
                            g.wait_ge(sC.h, cc_ssm_done)
                            dG.rotate()
                            if "u" in taps:
                                dG.issue(g.dma_start(out=taps["u"], in_=u_d))
                            if "sz" in taps:
                                dG.issue(g.dma_start(out=taps["sz"], in_=sz_d))
                            if "ssm" in taps:
                                dG.issue(g.dma_start(out=taps["ssm"], in_=cc_ssm_o.ap()))

        if "mscan" in phases:
            with contextlib.ExitStack() as ms:
                BCb = ms.enter_context(nc.sbuf_tensor("BCb", [32, T], BF16))
                rawb = ms.enter_context(nc.sbuf_tensor("rawb", [1, T], BF16))
                u_tw = ms.enter_context(nc.sbuf_tensor("u_tw", [128, NCH * TW], F32))
                del_tw = ms.enter_context(nc.sbuf_tensor("del_tw", [128, NCH * TW], F32))
                du_tw = ms.enter_context(nc.sbuf_tensor("du_tw", [128, NCH * TW], BF16))
                vb = ms.enter_context(nc.sbuf_tensor("vb", [128, 2 * TW], F32))
                eb = ms.enter_context(nc.sbuf_tensor("eb", [128, 2 * TW], F32))
                dAb = ms.enter_context(nc.sbuf_tensor("dAb", [128, 8 * TW], F32))
                dbub = ms.enter_context(nc.sbuf_tensor("dbub", [128, 8 * TW], BF16))
                s_bufs = ms.enter_context(nc.sbuf_tensor("s_bufs", [128, 16 * TW], F32))
                csb = ms.enter_context(nc.sbuf_tensor("csb", [128, 8 * TW], BF16))
                crep = ms.enter_context(nc.sbuf_tensor("crep", [128, 2 * TW], BF16))
                carr = ms.enter_context(nc.sbuf_tensor("carr", [128, 128], F32))
                y_sb = ms.enter_context(nc.sbuf_tensor("y_sb", [128, 2 * TW], F32))
                ybfb = ms.enter_context(nc.sbuf_tensor("ybfb", [128, TW], BF16))
                ysqb = ms.enter_context(nc.sbuf_tensor("ysqb", [128, TW], BF16))
                stev = ms.enter_context(nc.sbuf_tensor("stev", [1, 2 * TW], F32))

                mscan_entry_bar = dG.barrier() + dS.barrier()

                with contextlib.ExitStack() as bcs:
                    BCf = bcs.enter_context(nc.sbuf_tensor("BCf", [33, 2 * 1024], F32))
                    bch_ld = [None] * 4
                    bch_cast = [0] * 4
                    for ch in range(4):
                        @blk.sync
                        def _(q, ch=ch):
                            if ch == 0:
                                q.wait_ge(sC.h, cc_ssm_done)
                                W(q, mscan_entry_bar)
                            if ch >= 2:
                                q.wait_ge(sV.h, bch_cast[ch - 2])
                            dS.rotate()
                            co = (ch % 2) * 1024
                            dS.issue(q.dma_start(
                                out=BCf[0:32, co:co + 1024],
                                in_=cc_ssm_o.ap()[0:32, ch * 1024:(ch + 1) * 1024]))
                            bch_ld[ch] = dS.issue(q.dma_start(
                                out=BCf[32:33, co:co + 1024],
                                in_=cc_ssm_o.ap()[32:33, ch * 1024:(ch + 1) * 1024]))

                        @blk.vector
                        def _(v, ch=ch):
                            W(v, bch_ld[ch])
                            co = (ch % 2) * 1024
                            v.tensor_copy(BCb[:, ch * 1024:(ch + 1) * 1024],
                                          BCf[0:32, co:co + 1024])
                            bch_cast[ch] = sV.inc(v.tensor_copy(
                                rawb[:, ch * 1024:(ch + 1) * 1024],
                                BCf[32:33, co:co + 1024]))

                    @blk.vector
                    def _(v):
                        v.wait_ge(sV.h, bch_cast[3])
                        sV.inc(v.memset(carr[:], 0.0))
                    scan_ready = sV.v

                last_tts = [0] * NCH
                last_cs = [0] * NCH
                last_nsum = [0] * NCH
                last_dA = [0] * NCH
                tts_slot = [0] * 4
                cs_slot = [0] * 4
                cs_hist = [0] * 16
                nsum_slot = [0] * 4
                yev_slot = [0] * 4
                ydma_slot = [None, None]
                stat_guard = 0
                last_Cevac = 0
                bc_evac = 0
                b_evac = 0
                dbu_hist = [0, 0]
                u_ld = [None] * NTW
                del_done = [[0] * NCH for _ in range(NTW)]
                du_done = [[0] * NCH for _ in range(NTW)]
                yev = [0] * NCH
                stat_mm = 0
                stat_ev = [0] * NTW
                prev_del_consumed = 0
                prev_u_consumed = 0

                vcnt = [0] * NCH
                ecnt = [0] * NCH
                for tw in range(NTW):
                    t0 = tw * TW

                    @blk.sync
                    def _(q, tw=tw, t0=t0):
                        if tw == 0:
                            q.wait_ge(sC.h, cc_ssm_done)
                            q.wait_ge(sV.h, scan_ready)
                        else:
                            q.wait_ge(sV.h, prev_u_consumed)
                        dS.rotate()
                        for m in range(NCH):
                            u_ld[tw] = dS.issue(q.dma_start(
                                out=u_tw[:, m * TW:(m + 1) * TW],
                                in_=u_d[m * 128:(m + 1) * 128, t0:t0 + TW]))

                    for mc in range(NCH):
                        par = mc % 2

                        @blk.tensor
                        def _(t, tw=tw, mc=mc, t0=t0):
                            if tw == 0 and mc == 0:
                                t.wait_ge(sV.h, scan_ready)
                            t.wait_ge(sV.h, max(vcnt[(mc - 1) % NCH], bc_evac))
                            mmi = t.matmul(pb[4][:],
                                           s_dtwb[0:1, mc * 128:(mc + 1) * 128],
                                           rawb[0:1, t0:t0 + TW],
                                           start=True, stop=True)
                            del_done[tw][mc] = sT.inc(mmi)

                        @blk.vector
                        def _(v, tw=tw, mc=mc, par=par):
                            v.wait_ge(sT.h, del_done[tw][mc])
                            v.wait_ge(sS.h, ecnt[(mc - 2) % NCH])
                            vcnt[mc] = sV.inc(v.tensor_scalar(
                                vb[:, par * TW:(par + 1) * TW], pb[4][:],
                                s_dtb[:, mc:mc + 1], 20.0, ALU.add, ALU.min))

                        @blk.scalar
                        def _(s, tw=tw, mc=mc, par=par):
                            s.wait_ge(sV.h, vcnt[mc])
                            s.activation(eb[:, par * TW:(par + 1) * TW],
                                         vb[:, par * TW:(par + 1) * TW], AF.Exp)
                            ecnt[mc] = sS.inc(s.activation(
                                eb[:, par * TW:(par + 1) * TW],
                                eb[:, par * TW:(par + 1) * TW], AF.Ln, bias=1.0))

                        @blk.vector
                        def _(v, tw=tw, mc=mc, par=par):
                            v.wait_ge(sS.h, ecnt[mc])
                            if tw >= 1:
                                v.wait_ge(sS.h, prev_del_consumed)
                            v.tensor_scalar(
                                del_tw[:, mc * TW:(mc + 1) * TW],
                                eb[:, par * TW:(par + 1) * TW],
                                1e-3, 0.1, ALU.max, ALU.min)
                            W(v, u_ld[tw])
                            du_done[tw][mc] = sV.inc(v.tensor_tensor(
                                du_tw[:, mc * TW:(mc + 1) * TW],
                                del_tw[:, mc * TW:(mc + 1) * TW],
                                u_tw[:, mc * TW:(mc + 1) * TW], ALU.mult))

                    if "delta" in taps or "du" in taps:
                        @blk.gpsimd
                        def _(g, tw=tw, t0=t0):
                            g.wait_ge(sV.h, du_done[tw][NCH - 1])
                            dG.rotate()
                            for m in range(NCH):
                                if "delta" in taps:
                                    dG.issue(g.dma_start(
                                        out=taps["delta"][m * 128:(m + 1) * 128, t0:t0 + TW],
                                        in_=del_tw[:, m * TW:(m + 1) * TW]))
                                if "du" in taps:
                                    dG.issue(g.dma_start(
                                        out=taps["du"][m * 128:(m + 1) * 128, t0:t0 + TW],
                                        in_=du_tw[:, m * TW:(m + 1) * TW]))

                    for dh in range(2):
                        base_sS = sS.v

                        @blk.scalar
                        def _(s, tw=tw, dh=dh):
                            for n in range(NS):
                                giter = (tw * 2 + dh) * NS + n
                                if n == 0:
                                    s.wait_ge(sV.h, du_done[tw][NCH - 1])
                                    gprev = giter - 2 * NS + NS
                                    if tw > 0:
                                        s.wait_ge(sN.h, (tw - 1) * 2 * NS + dh * NS + NS)
                                else:
                                    s.wait_ge(sN.h, (tw * 2 + dh) * NS + n)
                                for dci in range(4):
                                    dc = dh * 4 + dci
                                    sS.inc(s.activation(
                                        dAb[:, dc * TW:(dc + 1) * TW],
                                        del_tw[:, dc * TW:(dc + 1) * TW],
                                        AF.Exp, scale=Acol[:, dc * NS + n:dc * NS + n + 1]))

                        @blk.tensor
                        def _(t, tw=tw, dh=dh):
                            t.wait_ge(sV.h, max(bc_evac, du_done[tw][NCH - 1]))
                            t.matmul(pb[4][:],
                                     s_sel32[:, 0:128],
                                     BCb[0:32, tw * TW:tw * TW + TW],
                                     start=True, stop=True).then_inc(sT.h, 1)
                        sT.v += 1
                        bbuild = sT.v

                        for n in range(NS):
                            npar = n % 2
                            giter = (tw * 2 + dh) * NS + n
                            dA_cnt = [base_sS + n * 4 + dci + 1 for dci in range(4)]

                            @blk.vector
                            def _(v, tw=tw, dh=dh, n=n, npar=npar, bbuild=bbuild):
                                v.wait_ge(sT.h, bbuild)
                                for dci in range(4):
                                    dc = dh * 4 + dci
                                    v.tensor_tensor(
                                        dbub[:, dc * TW:(dc + 1) * TW],
                                        du_tw[:, dc * TW:(dc + 1) * TW],
                                        pb[4][:], ALU.mult)
                                dbu_hist[npar] = sV.inc(v.tensor_copy(dbub[:, 0:1], dbub[:, 0:1]))
                            dbu_last = sV.v

                            @blk.tensor
                            def _(t, tw=tw, n=n, dbu_last=dbu_last):
                                t.wait_ge(sV.h, dbu_last)
                                t.matmul(pb[4][:],
                                         s_sel32[:, (16 + n) * 128:(17 + n) * 128],
                                         BCb[0:32, tw * TW:tw * TW + TW],
                                         start=True, stop=True).then_inc(sT.h, 1)
                            sT.v += 1
                            cbuild = sT.v

                            @blk.vector
                            def _(v, tw=tw, dh=dh, n=n, npar=npar, cbuild=cbuild,
                                  dA_cnt=dA_cnt, giter=giter):
                                v.wait_ge(sT.h, cbuild)
                                v.wait_ge(sP.h, sP.v)
                                nonlocal last_Cevac, bc_evac
                                last_Cevac = sV.inc(v.tensor_copy(
                                    crep[:, npar * TW:(npar + 1) * TW], pb[4][:]))
                                bc_evac = last_Cevac
                                for dci in range(4):
                                    dc = dh * 4 + dci
                                    sl = npar * 8 + dc
                                    v.wait_ge(sS.h, dA_cnt[dci])
                                    v.wait_ge(sP.h, cs_hist[sl])
                                    ci = n * 8 + dc
                                    init = carr[:, ci:ci + 1] if tw > 0 else 0.0
                                    sV.inc(v.tensor_tensor_scan(
                                        s_bufs[:, sl * TW:(sl + 1) * TW],
                                        dAb[:, dc * TW:(dc + 1) * TW],
                                        dbub[:, dc * TW:(dc + 1) * TW],
                                        init, ALU.mult, ALU.add))
                                v.wait_ge(sV.h, sV.v)
                                c0_ = n * 8 + dh * 4
                                svw = s_bufs.rearrange("p (c w) -> p c w", w=TW)
                                tcnt = sV.inc(v.tensor_copy(
                                    carr[:, c0_:c0_ + 4],
                                    svw[:, npar * 8 + dh * 4:npar * 8 + dh * 4 + 4, TW - 1:TW]))
                                sN.inc(v.tensor_copy(carr[:, c0_:c0_ + 1],
                                                     carr[:, c0_:c0_ + 1]))
                                for dci in range(4):
                                    last_tts[dh * 4 + dci] = tcnt

                            @blk.tensor
                            def _(t, tw=tw, dh=dh, n=n):
                                if n < NS - 1:
                                    t.wait_ge(sV.h, last_Cevac)
                                    t.matmul(pb[4][:],
                                             s_sel32[:, (n + 1) * 128:(n + 2) * 128],
                                             BCb[0:32, tw * TW:tw * TW + TW],
                                             start=True, stop=True).then_inc(sT.h, 1)
                            if n < NS - 1:
                                sT.v += 1
                                bbuild = sT.v

                            @blk.gpsimd
                            def _(g, tw=tw, dh=dh, n=n, npar=npar):
                                for dci in range(4):
                                    dc = dh * 4 + dci
                                    sl = npar * 8 + dc
                                    g.wait_ge(sV.h, max(last_tts[dc], last_Cevac))
                                    g.wait_ge(sT.h, last_nsum[dc])
                                    last_cs[dc] = sP.inc(g.tensor_tensor(
                                        csb[:, dc * TW:(dc + 1) * TW],
                                        s_bufs[:, sl * TW:(sl + 1) * TW],
                                        crep[:, npar * TW:(npar + 1) * TW], ALU.mult))
                                    cs_hist[sl] = last_cs[dc]

                            @blk.tensor
                            def _(t, tw=tw, dh=dh, n=n):
                                for dci in range(4):
                                    dc = dh * 4 + dci
                                    t.wait_ge(sP.h, last_cs[dc])
                                    if n == 0:
                                        t.wait_ge(sV.h, yev_slot[dci])
                                    last_nsum[dc] = sT.inc(t.matmul(
                                        pb[dci][:], s_id[:],
                                        csb[:, dc * TW:(dc + 1) * TW],
                                        start=(n == 0), stop=(n == NS - 1)))
                                    nsum_slot[dci] = last_nsum[dc]

                        for dci in range(4):
                            dc = dh * 4 + dci
                            ypar = dci % 2

                            @blk.vector
                            def _(v, tw=tw, dh=dh, dc=dc, dci=dci, ypar=ypar):
                                v.wait_ge(sT.h, last_nsum[dc])
                                W(v, ydma_slot[ypar])
                                v.wait_ge(sT.h, stat_guard)
                                v.scalar_tensor_tensor(
                                    y_sb[:, ypar * TW:(ypar + 1) * TW],
                                    u_tw[:, dc * TW:(dc + 1) * TW],
                                    s_Dp[:, dc:dc + 1], pb[dci][:],
                                    ALU.mult, ALU.add)
                                v.tensor_copy(ybfb[:], y_sb[:, ypar * TW:(ypar + 1) * TW])
                                yev[dc] = sV.inc(v.tensor_tensor(
                                    ysqb[:], ybfb[:], ybfb[:], ALU.mult))
                                yev_slot[dci] = yev[dc]

                            @blk.gpsimd
                            def _(g, tw=tw, dc=dc, ypar=ypar, t0=t0):
                                g.wait_ge(sV.h, yev[dc])
                                if dc % 4 == 0:
                                    dG.rotate()
                                ydma_slot[ypar] = dG.issue(g.dma_start(
                                    out=y_d[dc * 128:(dc + 1) * 128, t0:t0 + TW],
                                    in_=y_sb[:, ypar * TW:(ypar + 1) * TW]))

                            @blk.tensor
                            def _(t, tw=tw, dh=dh, dc=dc, dci=dci):
                                t.wait_ge(sV.h, yev[dc])
                                if tw > 0 and dh == 0 and dci == 0:
                                    t.wait_ge(sV.h, stat_ev[tw - 1])
                                first = (dh == 0 and dci == 0)
                                last = (dh == 1 and dci == 3)
                                t.matmul(pst1[:], s_onesb[:], ybfb[:],
                                         start=first, stop=last)
                                mmi = t.matmul(pst2[:], s_onesb[:], ysqb[:],
                                               start=first, stop=last)
                                nonlocal stat_mm, stat_guard
                                stat_mm = sT.inc(mmi)
                                stat_guard = stat_mm
                        prev_u_consumed = max(yev)

                    prev_del_consumed = max(last_dA)

                    if "carr" in taps and tw == 0:
                        @blk.gpsimd
                        def _(g):
                            g.wait_ge(sV.h, max(last_tts))
                            dG.rotate()
                            dG.issue(g.dma_start(out=taps["carr"], in_=carr[:]))

                    @blk.vector
                    def _(v, tw=tw):
                        v.wait_ge(sT.h, stat_mm)
                        v.tensor_copy(stev[:, 0:TW], pst1[:])
                        stat_ev[tw] = sV.inc(v.tensor_copy(stev[:, TW:2 * TW], pst2[:]))

                    @blk.gpsimd
                    def _(g, tw=tw, t0=t0):
                        g.wait_ge(sV.h, stat_ev[tw])
                        dG.rotate()
                        dG.issue(g.dma_start(out=cc_st_i.ap()[0:1, t0:t0 + TW],
                                             in_=stev[:, 0:TW]))
                        dG.issue(g.dma_start(out=cc_st_i.ap()[1:2, t0:t0 + TW],
                                             in_=stev[:, TW:2 * TW]))

                mscan_bar = dG.barrier()

                @blk.gpsimd
                def _(g):
                    W(g, mscan_bar)
                    sC.inc(g.collective_compute(
                        "AllReduce", ALU.add, replica_groups=GROUPS,
                        ins=[cc_st_i.ap().opt()], outs=[cc_st_o.ap().opt()]))
                cc_st_done = sC.v

                if "y" in taps or "stats" in taps:
                    @blk.gpsimd
                    def _(g):
                        g.wait_ge(sC.h, cc_st_done)
                        dG.rotate()
                        if "y" in taps:
                            dG.issue(g.dma_start(out=taps["y"], in_=y_d))
                        if "stats" in taps:
                            dG.issue(g.dma_start(out=taps["stats"], in_=cc_st_o.ap()))

        if "ln2" in phases:
            with contextlib.ExitStack() as l2:
                opTb = l2.enter_context(nc.sbuf_tensor("opTb", [128, NCH * D], BF16))
                opstg = l2.enter_context(nc.sbuf_tensor("opstg", [128, 2 * D], F32))
                st0 = l2.enter_context(nc.sbuf_tensor("st0", [1, T], F32))
                st1 = l2.enter_context(nc.sbuf_tensor("st1", [1, T], F32))
                murow = l2.enter_context(nc.sbuf_tensor("murow", [1, T], F32))
                rsrow = l2.enter_context(nc.sbuf_tensor("rsrow", [1, T], F32))
                muhi = l2.enter_context(nc.sbuf_tensor("muhi", [1, T], BF16))
                mulo = l2.enter_context(nc.sbuf_tensor("mulo", [1, T], BF16))
                rshi = l2.enter_context(nc.sbuf_tensor("rshi", [1, T], BF16))
                rslo = l2.enter_context(nc.sbuf_tensor("rslo", [1, T], BF16))
                tmpr = l2.enter_context(nc.sbuf_tensor("tmpr", [1, T], F32))
                yb2 = l2.enter_context(nc.sbuf_tensor("yb2", [128, 2 * TW], F32))
                szb2 = l2.enter_context(nc.sbuf_tensor("szb2", [128, 2 * TW], F32))
                t1b = l2.enter_context(nc.sbuf_tensor("t1b", [128, TW], F32))
                t2b = l2.enter_context(nc.sbuf_tensor("t2b", [128, TW], F32))
                gatedb = l2.enter_context(nc.sbuf_tensor("gatedb", [128, NCH * TW], BF16))
                sgsb = l2.enter_context(nc.sbuf_tensor("sgsb", [128, 2 * TW], F32))

                opld = [None] * NCH
                opcast = [0] * NCH
                ln2_entry_bar = dG.barrier() + dS.barrier()
                for k in range(NCH):
                    @blk.sync
                    def _(q, k=k):
                        if k == 0:
                            W(q, ln2_entry_bar)
                        if k >= 2:
                            q.wait_ge(sV.h, opcast[k - 2])
                        dS.rotate()
                        opld[k] = dS.issue(q.dma_start(
                            out=opstg[:, (k % 2) * D:(k % 2 + 1) * D],
                            in_=opT[k * 128:(k + 1) * 128, :]))

                    @blk.vector
                    def _(v, k=k):
                        W(v, opld[k])
                        opcast[k] = sV.inc(v.tensor_copy(
                            opTb[:, k * D:(k + 1) * D],
                            opstg[:, (k % 2) * D:(k % 2 + 1) * D]))

                @blk.sync
                def _(q):
                    q.wait_ge(sC.h, cc_st_done)
                    dS.rotate()
                    dS.issue(q.dma_start(out=st0[:], in_=cc_st_o.ap()[0:1, :]))
                    dS.issue(q.dma_start(out=st1[:], in_=cc_st_o.ap()[1:2, :]))
                st_ld = dS.barrier()

                @blk.vector
                def _(v):
                    W(v, st_ld)
                    v.tensor_scalar(murow[:], st0[:], 1.0 / 2048.0, None, ALU.mult)
                    v.tensor_scalar(st1[:], st1[:], 1.0 / 2048.0, None, ALU.mult)
                    v.tensor_tensor(rsrow[:], murow[:], murow[:], ALU.mult)
                    sV.inc(v.tensor_tensor(rsrow[:], rsrow[:], st1[:], ALU.subtract))
                negvar_done = sV.v

                @blk.scalar
                def _(s):
                    s.wait_ge(sV.h, negvar_done)
                    s.activation(rsrow[:], rsrow[:], AF.Ln, scale=-1.0,
                                 bias=s_eps[0:1, :])
                    sS.inc(s.activation(rsrow[:], rsrow[:], AF.Exp, scale=-0.5))
                rstd_done = sS.v

                @blk.vector
                def _(v):
                    v.wait_ge(sS.h, rstd_done)
                    v.tensor_copy(muhi[:], murow[:])
                    v.tensor_copy(rshi[:], rsrow[:])
                    sV.inc(v.tensor_tensor(tmpr[:], murow[:], muhi[:], ALU.subtract))
                    v.wait_ge(sV.h, sV.v)
                    v.tensor_copy(mulo[:], tmpr[:])
                    sV.inc(v.tensor_tensor(tmpr[:], rsrow[:], rshi[:], ALU.subtract))
                    v.wait_ge(sV.h, sV.v)
                    sV.inc(v.tensor_copy(rslo[:], tmpr[:]))
                hilo_done = sV.v

                g_done = [0] * NCH
                sg_ev = [0, 0]
                ln2_y_ld = [None] * NCH
                opmm = [0] * 8
                ln2_prev_g = 0

                for tw in range(NTW):
                    t0 = tw * TW

                    @blk.tensor
                    def _(t, tw=tw, t0=t0):
                        t.wait_ge(sV.h, hilo_done)
                        t.wait_ge(sV.h, opcast[NCH - 1])
                        if tw >= 1:
                            t.wait_ge(sV.h, g_done[NCH - 1])
                        t.matmul(pb[2][:], s_ones1b[:], muhi[0:1, t0:t0 + TW],
                                 start=True, stop=False)
                        t.matmul(pb[2][:], s_ones1b[:], mulo[0:1, t0:t0 + TW],
                                 start=False, stop=True)
                        t.matmul(pb[3][:], s_ones1b[:], rshi[0:1, t0:t0 + TW],
                                 start=True, stop=False)
                        t.matmul(pb[3][:], s_ones1b[:], rslo[0:1, t0:t0 + TW],
                                 start=False, stop=True).then_inc(sT.h, 1)
                    sT.v += 1
                    bcast_done = sT.v

                    for dc in range(NCH):
                        par = dc % 2

                        @blk.sync
                        def _(q, tw=tw, dc=dc, par=par, t0=t0):
                            if tw == 0 and dc == 0:
                                q.wait_ge(sC.h, cc_st_done)
                            prev = g_done[dc - 2] if dc >= 2 else ln2_prev_g
                            if tw > 0 or dc >= 2:
                                q.wait_ge(sV.h, prev)
                            if dc % 4 == 0:
                                dS.rotate()
                            dS.issue(q.dma_start(
                                out=yb2[:, par * TW:(par + 1) * TW],
                                in_=y_d[dc * 128:(dc + 1) * 128, t0:t0 + TW]))
                            ln2_y_ld[dc] = dS.issue(q.dma_start(
                                out=szb2[:, par * TW:(par + 1) * TW],
                                in_=sz_d[dc * 128:(dc + 1) * 128, t0:t0 + TW]))

                        @blk.vector
                        def _(v, tw=tw, dc=dc, par=par, bcast_done=bcast_done):
                            W(v, ln2_y_ld[dc])
                            v.wait_ge(sT.h, bcast_done)
                            v.tensor_tensor(t1b[:], yb2[:, par * TW:(par + 1) * TW],
                                            pb[2][:], ALU.subtract)
                            v.tensor_tensor(t2b[:], t1b[:], pb[3][:], ALU.mult)
                            v.tensor_scalar(t1b[:], t2b[:], s_nw[:, dc:dc + 1],
                                            s_nb[:, dc:dc + 1], ALU.mult, ALU.add)
                            g_done[dc] = sV.inc(v.tensor_tensor(
                                gatedb[:, dc * TW:(dc + 1) * TW], t1b[:],
                                szb2[:, par * TW:(par + 1) * TW], ALU.mult))
                    ln2_prev_g = g_done[NCH - 1]
                    if "gated" in taps:
                        @blk.gpsimd
                        def _(g, tw=tw, t0=t0):
                            g.wait_ge(sV.h, g_done[NCH - 1])
                            dG.rotate()
                            for m in range(NCH):
                                dG.issue(g.dma_start(
                                    out=taps["gated"][m * 128:(m + 1) * 128, t0:t0 + TW],
                                    in_=gatedb[:, m * TW:(m + 1) * TW]))

                    for m in range(8):
                        par = m % 2

                        @blk.tensor
                        def _(t, tw=tw, m=m, par=par):
                            t.wait_ge(sV.h, g_done[NCH - 1])
                            if tw * 8 + m >= 2:
                                t.wait_ge(sV.h, sg_ev[par])
                            for k in range(NCH):
                                mmi = t.matmul(
                                    pb[par][:],
                                    opTb[:, k * D + m * 128: k * D + (m + 1) * 128],
                                    gatedb[:, k * TW:(k + 1) * TW],
                                    start=(k == 0), stop=(k == NCH - 1))
                            opmm[m] = sT.inc(mmi)

                        @blk.vector
                        def _(v, tw=tw, m=m, par=par):
                            v.wait_ge(sT.h, opmm[m])
                            sg_ev[par] = sV.inc(v.tensor_copy(
                                sgsb[:, par * TW:(par + 1) * TW], pb[par][:]))

                        @blk.gpsimd
                        def _(g, tw=tw, m=m, par=par, t0=t0):
                            g.wait_ge(sV.h, sg_ev[par])
                            if m == 0:
                                dG.rotate()
                            dG.issue(g.dma_start(
                                out=cc_sg_i.ap()[m * 128:(m + 1) * 128, t0:t0 + TW],
                                in_=sgsb[:, par * TW:(par + 1) * TW]))

        if "ln2" in phases:
            ln2_dma_bar = dG.barrier()

            @blk.gpsimd
            def _(g):
                W(g, ln2_dma_bar)
                sC.inc(g.collective_compute(
                    "AllReduce", ALU.add, replica_groups=GROUPS,
                    ins=[cc_sg_i.ap().opt()], outs=[cc_sg_o.ap().opt()]))
            cc_s_done = sC.v

        ln2_bar = dG.barrier() + dS.barrier()

        if "ffna" in phases:
            with contextlib.ExitStack() as fa:
                xnT2 = fa.enter_context(nc.sbuf_tensor("xnT2", [128, NCH * T], BF16))
                gwTb = fa.enter_context(nc.sbuf_tensor("gwTb", [128, NCH * 8], BF16))
                gwstg = fa.enter_context(nc.sbuf_tensor("gwstg", [128, NCH * 8], F32))
                rb = fa.enter_context(nc.sbuf_tensor("rb", [128, 32], F32))
                combb = fa.enter_context(nc.sbuf_tensor("combb", [128, 256], BF16))
                w13stg = fa.enter_context(nc.sbuf_tensor("w13stg", [128, 2 * 8 * 512], F32))
                w1b = fa.enter_context(nc.sbuf_tensor("w1b", [128, NCH * 512], BF16))
                w3b = fa.enter_context(nc.sbuf_tensor("w3b", [128, NCH * 512], BF16))
                b1c = fa.enter_context(nc.sbuf_tensor("b1c", [128, 9 * 4], F32))
                b3c = fa.enter_context(nc.sbuf_tensor("b3c", [128, 9 * 4], F32))
                cbf = fa.enter_context(nc.sbuf_tensor("cbf", [128, TW], BF16))
                hsb = fa.enter_context(nc.sbuf_tensor("hsb", [128, 2 * TW], BF16))
                hscb = fa.enter_context(nc.sbuf_tensor("hscb", [128, 2 * TW], BF16))
                hhb = fa.enter_context(nc.sbuf_tensor("hhb", [128, 2 * TW], BF16))

                @blk.sync
                def _(q):
                    W(q, p0_bar)
                    W(q, ln2_bar)
                    dS.rotate()
                    dS.issue(q.dma_start(out=gwstg[:], in_=gwT))
                    dS.issue(q.dma_start(out=b1c[:], in_=b1tI))
                    dS.issue(q.dma_start(out=b3c[:], in_=b3tI))
                    for j in range(NCH):
                        dS.issue(q.dma_start(
                            out=xnT2[:, j * T:(j + 1) * T],
                            in_=xnt_d[j * 128:(j + 1) * 128, :]))
                ffna_ld = dS.barrier()

                @blk.vector
                def _(v):
                    W(v, ffna_ld)
                    sV.inc(v.tensor_copy(gwTb[:], gwstg[:]))
                    sV.inc(v.memset(combT05[:], 0.5))
                router_ready = sV.v

                r_mm = [0] * NTT
                r_soft = [0] * NTT
                tr_cnt = [0] * NTT
                tr_ev = [0] * NTT
                for tt in range(NTT):
                    @blk.tensor
                    def _(t, tt=tt):
                        t.wait_ge(sV.h, router_ready)
                        if tt >= 2:
                            t.wait_ge(sV.h, r_soft[tt - 2])
                        for k in range(NCH):
                            mmi = t.matmul(
                                pb[4][:, 0:8],
                                xnT2[:, k * T + tt * 128: k * T + (tt + 1) * 128],
                                gwTb[:, k * 8:(k + 1) * 8],
                                start=(k == 0), stop=(k == NCH - 1))
                        r_mm[tt] = sT.inc(mmi)

                    @blk.vector
                    def _(v, tt=tt):
                        v.wait_ge(sT.h, r_mm[tt])
                        v.tensor_copy(rb[:, 0:8], pb[4][:, 0:8])
                        sV.inc(v.reduce_max(rb[:, 8:9], rb[:, 0:8], axis=AX.X))
                        v.wait_ge(sV.h, sV.v)
                        sV.inc(v.tensor_scalar(rb[:, 9:10], rb[:, 8:9], -1.0,
                                               None, ALU.mult))
                    sm1 = sV.v

                    @blk.scalar
                    def _(s, tt=tt, sm1=sm1):
                        s.wait_ge(sV.h, sm1)
                        sS.inc(s.activation(rb[:, 16:24], rb[:, 0:8], AF.Exp,
                                            bias=rb[:, 9:10]))
                    sm2 = sS.v

                    @blk.vector
                    def _(v, tt=tt, sm2=sm2):
                        v.wait_ge(sS.h, sm2)
                        sV.inc(v.reduce_sum(rb[:, 10:11], rb[:, 16:24], axis=AX.X))
                        v.wait_ge(sV.h, sV.v)
                        sV.inc(v.reciprocal(rb[:, 11:12], rb[:, 10:11]))
                        v.wait_ge(sV.h, sV.v)
                        sV.inc(v.tensor_scalar(rb[:, 16:24], rb[:, 16:24],
                                               rb[:, 11:12], None, ALU.mult))
                        v.wait_ge(sV.h, sV.v)
                        sV.inc(v.reduce_max(rb[:, 8:9], rb[:, 16:24], axis=AX.X))
                        v.wait_ge(sV.h, sV.v)
                        sV.inc(v.tensor_scalar(rb[:, 24:32], rb[:, 16:24],
                                               rb[:, 8:9], None, ALU.is_equal))
                        v.wait_ge(sV.h, sV.v)
                        sV.inc(v.scalar_tensor_tensor(rb[:, 0:8], rb[:, 24:32], -1e30,
                                                      rb[:, 16:24], ALU.mult, ALU.add))
                        v.wait_ge(sV.h, sV.v)
                        sV.inc(v.reduce_max(rb[:, 9:10], rb[:, 0:8], axis=AX.X))
                        v.wait_ge(sV.h, sV.v)
                        sV.inc(v.tensor_scalar(rb[:, 0:8], rb[:, 0:8], rb[:, 9:10],
                                               None, ALU.is_equal))
                        v.wait_ge(sV.h, sV.v)
                        sV.inc(v.tensor_tensor(rb[:, 24:32], rb[:, 24:32], rb[:, 0:8],
                                               ALU.add))
                        v.wait_ge(sV.h, sV.v)
                        par = tt % 2
                        r_soft[tt] = sV.inc(v.tensor_tensor(
                            combb[:, par * 128:par * 128 + 8],
                            rb[:, 16:24], rb[:, 24:32], ALU.mult))

                    @blk.tensor
                    def _(t, tt=tt):
                        t.wait_ge(sV.h, r_soft[tt])
                        if tt >= 1:
                            t.wait_ge(sS.h, tr_ev[tt - 1])
                        tr_cnt[tt] = sT.inc(t.matmul(
                            ptr2[:, 0:128],
                            combb[:, (tt % 2) * 128:(tt % 2) * 128 + 128],
                            s_id[:], is_transpose=True,
                            start=True, stop=True))

                    @blk.scalar
                    def _(s, tt=tt):
                        s.wait_ge(sT.h, tr_cnt[tt])
                        srcp = ptr2[0:8, 0:128]
                        s.copy(combT[0:8, tt * 128:(tt + 1) * 128], srcp)
                        tr_ev[tt] = sS.inc(s.mul(
                            combT05[0:8, tt * 128:(tt + 1) * 128], srcp, 0.5))
                router_done_s = tr_ev[NTT - 1]
                if "rbt" in taps:
                    @blk.gpsimd
                    def _(g):
                        g.wait_ge(sV.h, r_soft[NTT - 1])
                        dG.rotate()
                        dG.issue(g.dma_start(out=taps["rbt"], in_=rb[:]))
                if "combb" in taps:
                    @blk.gpsimd
                    def _(g):
                        g.wait_ge(sV.h, r_soft[NTT - 1])
                        dG.rotate()
                        dG.issue(g.dma_start(out=taps["combb"], in_=combb[:]))

                if "combt" in taps:
                    with nc.sbuf_tensor("ctf", [9, T], F32) as ctf:
                        @blk.vector
                        def _(v):
                            v.wait_ge(sS.h, router_done_s)
                            sV.inc(v.tensor_copy(ctf[0:9, :], combT05[:]))
                        tapc = sV.v

                        @blk.gpsimd
                        def _(g):
                            g.wait_ge(sV.h, tapc)
                            dG.rotate()
                            dG.issue(g.dma_start(out=taps["combt"], in_=ctf[:]))

                w13ld = [None, None]
                w13cast = [0, 0]
                crep_ev = 0
                hs_done = 0
                hs_done2 = [0, 0]
                h1mm2 = [0, 0]
                h3mm2 = [0, 0]
                hh_done = [0, 0]
                hh_dma = [None, None]
                h1mm = 0
                h3mm = 0

                for br in range(9):
                    for half in range(2):
                        @blk.sync
                        def _(q, br=br, half=half):
                            if br > 0:
                                q.wait_ge(sV.h, w13cast[half])
                            dS.rotate()
                            wsrc = w1I if half == 0 else w3I
                            ho = half * 8 * 512
                            for k in range(NCH):
                                w13ld[half] = dS.issue(q.dma_start(
                                    out=w13stg[:, ho + k * 512: ho + (k + 1) * 512],
                                    in_=wsrc[br * D + k * 128: br * D + (k + 1) * 128, :]))

                        @blk.vector
                        def _(v, br=br, half=half):
                            W(v, w13ld[half])
                            if br > 0:
                                v.wait_ge(sT.h, h1mm if half == 0 else h3mm)
                            dstb = w1b if half == 0 else w3b
                            ho = half * 8 * 512
                            w13cast[half] = sV.inc(v.tensor_copy(
                                dstb[:], w13stg[:, ho:ho + 8 * 512]))

                    for tw in range(NTW):
                        t0 = tw * TW

                        if br < 8:
                            @blk.tensor
                            def _(t, br=br, tw=tw, t0=t0):
                                t.wait_ge(sS.h, router_done_s)
                                t.wait_ge(sV.h, crep_ev)
                                t.matmul(pb[4][:],
                                         s_sel8[:, br * 128:(br + 1) * 128],
                                         combT[0:8, t0:t0 + TW],
                                         start=True, stop=True).then_inc(sT.h, 1)
                            sT.v += 1
                            crep_mm = sT.v

                            @blk.vector
                            def _(v, crep_mm=crep_mm):
                                v.wait_ge(sT.h, crep_mm)
                                nonlocal crep_ev
                                crep_ev = sV.inc(v.tensor_copy(cbf[:], pb[4][:]))

                        for mi in range(4):
                            mpar = mi % 2
                            pA, pB = (0, 1) if mpar == 0 else (2, 3)

                            @blk.tensor
                            def _(t, br=br, tw=tw, mi=mi, t0=t0, pA=pA, pB=pB, mpar=mpar):
                                t.wait_ge(sV.h, max(w13cast))
                                t.wait_ge(sS.h, hs_done2[mpar])
                                t.wait_ge(sV.h, hh_done[mpar])
                                for k in range(NCH):
                                    mmi = t.matmul(
                                        pb[pA][:],
                                        w1b[:, k * 512 + mi * 128: k * 512 + (mi + 1) * 128],
                                        xnT2[:, k * T + t0: k * T + t0 + TW],
                                        start=(k == 0), stop=(k == NCH - 1))
                                nonlocal h1mm
                                h1mm = sT.inc(mmi)
                                h1mm2[mpar] = h1mm
                                for k in range(NCH):
                                    mmi = t.matmul(
                                        pb[pB][:],
                                        w3b[:, k * 512 + mi * 128: k * 512 + (mi + 1) * 128],
                                        xnT2[:, k * T + t0: k * T + t0 + TW],
                                        start=(k == 0), stop=(k == NCH - 1))
                                nonlocal h3mm
                                h3mm = sT.inc(mmi)
                                h3mm2[mpar] = h3mm

                            @blk.scalar
                            def _(s, br=br, tw=tw, mi=mi, pA=pA, mpar=mpar):
                                s.wait_ge(sT.h, h1mm2[mpar])
                                hs_done2[mpar] = sS.inc(s.activation(
                                    hsb[:, mpar * TW:(mpar + 1) * TW], pb[pA][:],
                                    AF.Silu, bias=b1c[:, br * 4 + mi:br * 4 + mi + 1]))

                            @blk.vector
                            def _(v, br=br, tw=tw, mi=mi, pB=pB, mpar=mpar):
                                v.wait_ge(sS.h, hs_done2[mpar])
                                v.wait_ge(sT.h, h3mm2[mpar])
                                if br < 8:
                                    v.wait_ge(sV.h, crep_ev)
                                    v.tensor_tensor(hscb[:, mpar * TW:(mpar + 1) * TW],
                                                    hsb[:, mpar * TW:(mpar + 1) * TW],
                                                    cbf[:], ALU.mult)
                                    hsrc = hscb[:, mpar * TW:(mpar + 1) * TW]
                                else:
                                    hsrc = hsb[:, mpar * TW:(mpar + 1) * TW]
                                W(v, hh_dma[mpar])
                                hh_done[mpar] = sV.inc(v.scalar_tensor_tensor(
                                    hhb[:, mpar * TW:(mpar + 1) * TW], pb[pB][:],
                                    b3c[:, br * 4 + mi:br * 4 + mi + 1], hsrc,
                                    ALU.add, ALU.mult))

                            @blk.gpsimd
                            def _(g, br=br, tw=tw, mi=mi, t0=t0, mpar=mpar):
                                g.wait_ge(sV.h, hh_done[mpar])
                                if mi == 0:
                                    dG.rotate()
                                row = (br * 4 + mi) * 128
                                hh_dma[mpar] = dG.issue(g.dma_start(
                                    out=hh_d[row:row + 128, t0:t0 + TW],
                                    in_=hhb[:, mpar * TW:(mpar + 1) * TW]))

                ffna_bar = dG.barrier()
                if "hh" in taps:
                    @blk.gpsimd
                    def _(g):
                        W(g, ffna_bar)
                        dG.rotate()
                        dG.issue(g.dma_start(out=taps["hh"], in_=hh_d))

        if "ffnb" in phases:
            with contextlib.ExitStack() as fb:
                w2b = fb.enter_context(nc.sbuf_tensor("w2b", [128, 36 * D], BF16))
                w2stg = fb.enter_context(nc.sbuf_tensor("w2stg", [128, 2 * D], F32))
                b2stg = fb.enter_context(nc.sbuf_tensor("b2stg", [9, D], F32))
                b2b = fb.enter_context(nc.sbuf_tensor("b2b", [9, D], BF16))
                hh_sb = fb.enter_context(nc.sbuf_tensor("hh_sb", [128, 2 * 36 * TW], BF16))
                fsb = fb.enter_context(nc.sbuf_tensor("fsb", [128, 2 * TW], F32))

                w2ld = [None] * 36
                w2cast = [0] * 36

                ffnb_entry_bar = dG.barrier() + dS.barrier()

                @blk.sync
                def _(q):
                    W(q, ffnb_entry_bar)
                    dS.rotate()
                    dS.issue(q.dma_start(out=b2stg[:], in_=b2sI))
                b2ld = dS.barrier()

                @blk.vector
                def _(v):
                    W(v, b2ld)
                    sV.inc(v.tensor_copy(b2b[:], b2stg[:]))
                b2c = sV.v

                for j in range(36):
                    @blk.sync
                    def _(q, j=j):
                        if j >= 2:
                            q.wait_ge(sV.h, w2cast[j - 2])
                        dS.rotate()
                        w2ld[j] = dS.issue(q.dma_start(
                            out=w2stg[:, (j % 2) * D:(j % 2 + 1) * D],
                            in_=w2I[j * 128:(j + 1) * 128, :]))

                    @blk.vector
                    def _(v, j=j):
                        W(v, w2ld[j])
                        w2cast[j] = sV.inc(v.tensor_copy(
                            w2b[:, j * D:(j + 1) * D],
                            w2stg[:, (j % 2) * D:(j % 2 + 1) * D]))
                w2_ready = w2cast[35]

                hh_ld = [None, None]
                f_ev = [0, 0]
                f_dma = [None, None]
                fmm = [0, 0]

                for tw in range(NTW):
                    t0 = tw * TW
                    twp = tw % 2

                    @blk.sync
                    def _(q, tw=tw, t0=t0, twp=twp):
                        if tw == 0:
                            W(q, ffna_bar)
                        if tw >= 2:
                            q.wait_ge(sT.h, fmm[twp])
                        dS.rotate()
                        for j in range(36):
                            hh_ld[twp] = dS.issue(q.dma_start(
                                out=hh_sb[:, (twp * 36 + j) * TW:(twp * 36 + j + 1) * TW],
                                in_=hh_d[j * 128:(j + 1) * 128, t0:t0 + TW]))

                    for m in range(8):
                        par = m % 2

                        @blk.tensor
                        def _(t, tw=tw, m=m, par=par, twp=twp, t0=t0):
                            t.wait_ge(sV.h, max(w2_ready, b2c))
                            t.wait_ge(sS.h, router_done_s)
                            W(t, hh_ld[twp])
                            if tw * 8 + m >= 2:
                                t.wait_ge(sV.h, f_ev[par])
                            t.matmul(pb[par][:],
                                     b2b[0:9, m * 128:(m + 1) * 128],
                                     combT05[0:9, t0:t0 + TW],
                                     start=True, stop=False)
                            for j in range(36):
                                mmi = t.matmul(
                                    pb[par][:],
                                    w2b[:, j * D + m * 128: j * D + (m + 1) * 128],
                                    hh_sb[:, (twp * 36 + j) * TW:(twp * 36 + j + 1) * TW],
                                    start=False, stop=(j == 35))
                            fmm[twp] = sT.inc(mmi)

                        @blk.vector
                        def _(v, tw=tw, m=m, par=par, twp=twp):
                            v.wait_ge(sT.h, fmm[twp])
                            W(v, f_dma[par])
                            f_ev[par] = sV.inc(v.tensor_copy(
                                fsb[:, par * TW:(par + 1) * TW], pb[par][:]))

                        @blk.gpsimd
                        def _(g, tw=tw, m=m, par=par, t0=t0):
                            g.wait_ge(sV.h, f_ev[par])
                            if m == 0:
                                dG.rotate()
                            f_dma[par] = dG.issue(g.dma_start(
                                out=cc_f_i.ap()[m * 128:(m + 1) * 128, t0:t0 + TW],
                                in_=fsb[:, par * TW:(par + 1) * TW]))

                ffnb_bar = dG.barrier()

                @blk.gpsimd
                def _(g):
                    W(g, ffnb_bar)
                    sC.inc(g.collective_compute(
                        "AllReduce", ALU.add, replica_groups=GROUPS,
                        ins=[cc_f_i.ap().opt()], outs=[cc_f_o.ap().opt()]))
                cc_sg_done = sC.v

                if "sg" in taps:
                    @blk.gpsimd
                    def _(g):
                        g.wait_ge(sC.h, cc_sg_done)
                        dG.rotate()
                        dG.issue(g.dma_start(out=taps["sg"][0:D, :], in_=cc_sg_o.ap()[0:D, :]))
                        dG.issue(g.dma_start(out=taps["sg"][D:2 * D, :], in_=cc_f_o.ap()))

        if "final" in phases:
            fin_entry_bar = dG.barrier() + dS.barrier()
            with contextlib.ExitStack() as fi:
                fs = fi.enter_context(nc.sbuf_tensor("fs", [128, 2 * T], F32))
                ff = fi.enter_context(nc.sbuf_tensor("ff", [128, 2 * T], F32))
                fx = fi.enter_context(nc.sbuf_tensor("fx", [128, 2 * T], F32))
                fg = fi.enter_context(nc.sbuf_tensor("fg", [128, 2 * T], BF16))
                fo = fi.enter_context(nc.sbuf_tensor("fo", [128, 2 * T], F32))

                fin_ld = None
                fin_g = 0
                fin_o = 0
                fin_dma = [None, None]
                for dc in range(NCH):
                    par = dc % 2

                    @blk.sync
                    def _(q, dc=dc, par=par):
                        q.wait_ge(sC.h, cc_sg_done)
                        if dc == 0:
                            W(q, fin_entry_bar)
                        if dc >= 2:
                            q.wait_ge(sV.h, fin_o)
                        dS.rotate()
                        r0 = dc * 128
                        dS.issue(q.dma_start(out=fs[:, par * T:(par + 1) * T],
                                             in_=cc_sg_o.ap()[r0:r0 + 128, :]))
                        dS.issue(q.dma_start(out=ff[:, par * T:(par + 1) * T],
                                             in_=cc_f_o.ap()[r0:r0 + 128, :]))
                        nonlocal fin_ld
                        fin_ld = dS.issue(q.dma_start(
                            out=fx[:, par * T:(par + 1) * T],
                            in_=x_fm[r0:r0 + 128, :]))

                    @blk.scalar
                    def _(s, dc=dc, par=par):
                        W(s, fin_ld)
                        nonlocal fin_g
                        fin_g = sS.inc(s.activation(
                            fg[:, par * T:(par + 1) * T],
                            ff[:, par * T:(par + 1) * T], AF.Silu))

                    @blk.vector
                    def _(v, dc=dc, par=par):
                        v.wait_ge(sS.h, fin_g)
                        W(v, fin_dma[par])
                        v.tensor_tensor(fo[:, par * T:(par + 1) * T],
                                        fs[:, par * T:(par + 1) * T],
                                        fg[:, par * T:(par + 1) * T], ALU.mult)
                        nonlocal fin_o
                        fin_o = sV.inc(v.tensor_tensor(
                            fo[:, par * T:(par + 1) * T],
                            fo[:, par * T:(par + 1) * T],
                            fx[:, par * T:(par + 1) * T], ALU.add))

                    @blk.gpsimd
                    def _(g, dc=dc, par=par):
                        g.wait_ge(sV.h, fin_o)
                        dG.rotate()
                        r0 = dc * 128
                        fin_dma[par] = dG.issue(g.dma_start(
                            out=out_fm[r0:r0 + 128, :],
                            in_=fo[:, par * T:(par + 1) * T]))

        fin_bar_g = dG.barrier()
        fin_bar_s = dS.barrier()

        @blk.sync
        def _(q):
            W(q, fin_bar_g)
            W(q, fin_bar_s)

        @blk.gpsimd
        def _(g):
            W(g, fin_bar_g)

    return nc


def host_inputs(inp, core):
    b, h = core // 2, core % 2
    dsl = slice(h * 1024, (h + 1) * 1024)
    isl = slice(h * 512, (h + 1) * 512)
    x = np.asarray(inp["L0_attn_x"], np.float32)[b]
    in_proj = np.asarray(inp["L0_attn_in_proj_w"], np.float32)
    conv_w = np.asarray(inp["L0_attn_conv_w"], np.float32)[:, 0, :]
    conv_b = np.asarray(inp["L0_attn_conv_b"], np.float32)
    x_proj = np.asarray(inp["L0_attn_x_proj_w"], np.float32)
    dtwf = np.asarray(inp["L0_attn_dt_proj_w"], np.float32)[:, 0]
    dtbf = np.asarray(inp["L0_attn_dt_proj_b"], np.float32)
    logA = np.asarray(inp["L0_attn_log_A"], np.float32)
    Dpf = np.asarray(inp["L0_attn_D_param"], np.float32)
    nwf = np.asarray(inp["L0_attn_norm_w"], np.float32)
    nbf = np.asarray(inp["L0_attn_norm_b"], np.float32)
    out_proj = np.asarray(inp["L0_attn_out_proj_w"], np.float32)
    gate_w = np.asarray(inp["L0_ffn_gate_w"], np.float32)
    w1 = np.asarray(inp["L0_ffn_w1"], np.float32)
    b1 = np.asarray(inp["L0_ffn_b1"], np.float32)
    w2 = np.asarray(inp["L0_ffn_w2"], np.float32)
    b2 = np.asarray(inp["L0_ffn_b2"], np.float32)
    w3 = np.asarray(inp["L0_ffn_w3"], np.float32)
    b3 = np.asarray(inp["L0_ffn_b3"], np.float32)
    sw1 = np.asarray(inp["L0_ffn_sw1"], np.float32)
    sb1 = np.asarray(inp["L0_ffn_sb1"], np.float32)
    sw2 = np.asarray(inp["L0_ffn_sw2"], np.float32)
    sb2 = np.asarray(inp["L0_ffn_sb2"], np.float32)
    sw3 = np.asarray(inp["L0_ffn_sw3"], np.float32)
    sb3 = np.asarray(inp["L0_ffn_sb3"], np.float32)

    C = np.ascontiguousarray

    def tile8(vec):
        return C(vec.reshape(8, 128).T)

    ipr = np.concatenate([in_proj[dsl], in_proj[2048 + h * 1024: 2048 + (h + 1) * 1024]], 0)
    w1h = np.concatenate([w1[:, :, isl].reshape(8 * 1024, 512), sw1[:, isl]], 0)
    w3h = np.concatenate([w3[:, :, isl].reshape(8 * 1024, 512), sw3[:, isl]], 0)
    w2h = np.concatenate([w2[:, isl, :].reshape(8 * 512, 1024), sw2[isl, :]], 0)
    b1h = np.concatenate([b1[:, isl], sb1[isl][None, :]], 0)
    b3h = np.concatenate([b3[:, isl], sb3[isl][None, :]], 0)
    b2s = np.concatenate([b2, sb2[None, :]], 0)

    sel32 = np.zeros((32, 32 * 128), np.float32)
    for n in range(32):
        sel32[n, n * 128:(n + 1) * 128] = 1.0
    sel8 = np.zeros((8, 8 * 128), np.float32)
    for e in range(8):
        sel8[e, e * 128:(e + 1) * 128] = 1.0

    bf = lambda a: C(a).astype(ml_dtypes.bfloat16)

    return {
        "x_tok": C(x),
        "x_fm": C(x.T),
        "ipT": C(ipr.T),
        "cw": C(conv_w[dsl].reshape(8, 128, 4).transpose(1, 0, 2).reshape(128, 32)),
        "cb": tile8(conv_b[dsl]),
        "xpT": C(x_proj[:, dsl].T.reshape(8, 128, 33).transpose(1, 0, 2).reshape(128, 8 * 33)),
        "dtw": C(dtwf[dsl][None, :]),
        "dtb": tile8(dtbf[dsl]),
        "logA": C(logA[dsl].reshape(8, 128, 16).transpose(1, 0, 2).reshape(128, 128)),
        "Dp": tile8(Dpf[dsl]),
        "nw": tile8(nwf[dsl]),
        "nb": tile8(nbf[dsl]),
        "opT": C(out_proj[:, dsl].T),
        "gwT": C(gate_w.T.reshape(8, 128, 8).transpose(1, 0, 2).reshape(128, 64)),
        "w1": C(w1h), "w3": C(w3h), "w2": C(w2h),
        "b1t": C(b1h.reshape(9, 4, 128).transpose(2, 0, 1).reshape(128, 36)),
        "b3t": C(b3h.reshape(9, 4, 128).transpose(2, 0, 1).reshape(128, 36)),
        "b2s": C(b2s),
        "ident_bf": bf(np.eye(128, dtype=np.float32)),
        "sel32": bf(sel32),
        "sel8": bf(sel8),
        "ones_bf": bf(np.ones((128, 1), np.float32)),
        "ones_f1": np.ones((1, 128), np.float32),
        "ones_r1b": bf(np.ones((1, 128), np.float32)),
        "epsc": np.full((128, 1), 1e-5, np.float32),
    }


def kernel(**inputs):
    nc = build_program()
    in_maps = [host_inputs(inputs, c) for c in range(8)]
    res = run_bass_kernel_spmd(nc, in_maps, core_ids=list(range(8)))
    out = np.stack([np.asarray(res.results[2 * b]["out_fm"]).T for b in range(NB)])
    return np.ascontiguousarray(out.astype(np.float32))
